# version 03
# baseline (speedup 1.0000x reference)
import math
from contextlib import ExitStack

import numpy as np

import concourse.bass as bass
import concourse.mybir as mybir
from concourse.bass_utils import run_bass_kernel_spmd

F32 = mybir.dt.float32
BF16 = mybir.dt.bfloat16
AF = mybir.ActivationFunctionType
ALU = mybir.AluOpType

NCORES = 8
T = 2048
C = 1024
CT = C // 128
NTT = T // 128
NTC = T // 512


class Prog:
    ENGINES = ("tensor", "vector", "scalar", "gpsimd", "sync")
    def __init__(self, name="prog", qx_lanes=None, qw_lanes=None):
        import os as _os
        if qx_lanes is None:
            qx_lanes = int(_os.environ.get("QXL", "4"))
        if qw_lanes is None:
            qw_lanes = int(_os.environ.get("QWL", "4"))
        self.QLANES = {"qx": qx_lanes, "qw": qw_lanes}
        self.nc = bass.Bass("TRN2", target_bir_lowering=False,
                            detect_race_conditions=False)
        self.stack = ExitStack()
        self.ops = {e: [] for e in self.ENGINES}
        self.sems = {}
        self.cnt = {}
        self.qhist = {q: {0: (0,) * n} for q, n in self.QLANES.items()}
        self.seen = {e: {} for e in self.ENGINES}
        self.uid = 0

    def _n(self, prefix):
        self.uid += 1
        return f"{prefix}_{self.uid}"

    def sem(self, name):
        if name not in self.sems:
            self.sems[name] = self.stack.enter_context(self.nc.semaphore(name))
            self.cnt[name] = 0
        return self.sems[name]

    def sb(self, name, shape, dtype=F32):
        return self.stack.enter_context(self.nc.sbuf_tensor(name, shape, dtype))

    def dram(self, name, shape, dtype=F32, kind="Internal"):
        return self.nc.dram_tensor(name, shape, dtype, kind=kind).ap()

    def _xlate(self, eng, waits):
        out = {}
        for n, v in waits:
            if v <= 0:
                continue
            if n in self.QLANES:
                lanes = self.qhist[n][v]
                for li, lv in enumerate(lanes):
                    if lv > 0:
                        key = f"{n}{li}"
                        out[key] = max(out.get(key, 0), lv)
            else:
                out[n] = max(out.get(n, 0), v)
        seen = self.seen[eng]
        wl = []
        for n, v in out.items():
            if v > seen.get(n, 0):
                wl.append((self.sem(n), v))
                seen[n] = v
        return wl

    def emit(self, eng, f, waits=(), inc=None, inc_by=1):
        wl_src = list(waits)
        ih = None
        if inc in self.QLANES:
            nl = self.QLANES[inc]
            idx = self.cnt.get(inc, 0)
            lane = (idx // 16) % nl
            lname = f"{inc}{lane}"
            if self.cnt.get(lname, 0) > 0:
                wl_src.append((lname, self.cnt[lname]))
            ih = self.sem(lname)
            self.cnt[lname] = self.cnt.get(lname, 0) + inc_by
            self.cnt[inc] = idx + inc_by
            ret = self.cnt[inc]
            self.qhist[inc][ret] = tuple(
                self.cnt.get(f"{inc}{j}", 0) for j in range(nl))
        elif inc is not None:
            ih = self.sem(inc)
            self.cnt[inc] = self.cnt.get(inc, 0) + inc_by
            ret = self.cnt[inc]
        else:
            ret = None
        wl = self._xlate(eng, wl_src)

        def play(e, f=f, wl=wl, ih=ih, inc_by=inc_by):
            for h, v in wl:
                e.wait_ge(h, v)
            ins = f(e)
            if ih is not None:
                ins.then_inc(ih, inc_by)
        self.ops[eng].append(play)
        return ret

    def wait(self, eng, waits):
        wl = self._xlate(eng, waits)

        def play(e, wl=wl):
            for h, v in wl:
                e.wait_ge(h, v)
        self.ops[eng].append(play)

    def val(self, sem):
        return self.cnt.get(sem, 0)

    def finalize(self):
        def make(ops):
            def body(eng):
                for f in ops:
                    f(eng)
            return body
        with self.stack:
            with self.nc.Block() as block:
                for e in self.ENGINES:
                    if self.ops[e]:
                        getattr(block, e)(make(self.ops[e]))
        return self.nc


class Rot:

    def __init__(self, tiles):
        self.tiles = tiles
        self.free = [None] * len(tiles)
        self.i = 0

    def take(self):
        i = self.i % len(self.tiles)
        self.i += 1
        war = self.free[i]
        return i, self.tiles[i], ([war] if war else [])

    def release(self, i, sem, val):
        self.free[i] = (sem, val)


def lhsT_blocked(wT, mtile=128):
    K, M = wT.shape
    assert K % 128 == 0 and M % mtile == 0
    b = wT.reshape(K // 128, 128, M // mtile, mtile).transpose(2, 1, 0, 3)
    return np.ascontiguousarray(b.reshape(M // mtile, 128, (K // 128) * mtile))


def lhsT_flat(wT, mtile=128):
    b = lhsT_blocked(wT, mtile)
    return np.ascontiguousarray(b.transpose(1, 0, 2).reshape(128, -1))


def rhs_blocked(w):
    K, N = w.shape
    assert K % 128 == 0
    return np.ascontiguousarray(w.reshape(K // 128, 128, N))


def const_arrays():
    import ml_dtypes
    return {
        "d_maskA": np.triu(np.ones((128, 128), np.float32)).astype(ml_dtypes.bfloat16),
        "d_ident": np.eye(128, dtype=np.float32).astype(ml_dtypes.bfloat16),
        "d_maskLT": np.triu(np.ones((64, 64), np.float32)).astype(ml_dtypes.bfloat16),
        "d_onesb": np.ones((128, 1), ml_dtypes.bfloat16),
        "d_onesF": np.ones((128, 512), np.float32),
        "d_identF": np.eye(128, dtype=np.float32),
        "d_maskC4": _mask_c4().astype(ml_dtypes.bfloat16),
    }


def _mask_c4():
    m = np.zeros((128, 4 * 512), np.float32)
    for r in range(4):
        for j in range(128):
            lo = 128 * r + j
            m[j, r * 512 + lo: (r + 1) * 512] = 1.0
    return m


class Ctx:
    def __init__(self, p):
        self.p = p
        self.maskA = p.sb("c_maskA", [128, 128], BF16)
        self.ident = p.sb("c_ident", [128, 128], BF16)
        self.maskLT = p.sb("c_maskLT", [64, 64], BF16)
        self.ones_bf = p.sb("c_ones_bf", [128, 1], BF16)
        self.onesF = p.sb("c_onesF", [128, 512], F32)
        self.identF = p.sb("c_identF", [128, 128], F32)
        self.maskC4 = p.sb("c_maskC4", [128, 4 * 512], BF16)
        for nm, dst, shape, dt in (
                ("d_maskA", self.maskA, [128, 128], BF16),
                ("d_ident", self.ident, [128, 128], BF16),
                ("d_maskLT", self.maskLT, [64, 64], BF16),
                ("d_onesb", self.ones_bf, [128, 1], BF16),
                ("d_onesF", self.onesF, [128, 512], F32),
                ("d_identF", self.identF, [128, 128], F32),
                ("d_maskC4", self.maskC4, [128, 4 * 512], BF16)):
            src = p.dram(nm, shape, dt, kind="ExternalInput")
            p.emit("sync", lambda e, dst=dst, src=src: e.dma_start(
                out=dst[:], in_=src), inc="qx", inc_by=16)
        self.m_const = p.val("qx")

        self.xbf = p.sb("xbf", [128, CT * T], BF16)
        self.res = p.sb("resbuf", [128, CT * 512], F32)
        self.xstage = p.sb("xstage", [128, CT * 512], F32)
        self.xsp = [p.dram("xspill0", [C, T], F32), p.dram("xspill1", [C, T], F32)]
        self.xsp_t = [a.rearrange("(a p) t -> a p t", p=128) for a in self.xsp]
        self.spill_sel = 0

        self.xsrc_t = None
        self.m_xsrc = 0
        self.xbf_wait = ()
        self.m_xbf_users_t = 0
        self.m_xstage_users_v = [0, 0]

        self.ln_resb = p.sb("ln_resb", [128, CT * 512], BF16)
        self.ln_sq = p.sb("ln_sq", [128, CT * 512], BF16)
        self.ln_rows = p.sb("ln_rows", [1, 6 * 512], F32)
        self.ln_arep = p.sb("ln_arep", [128, 512], F32)
        self.ln_brep = p.sb("ln_brep", [128, 512], F32)
        self.ln_xs = p.sb("ln_xs", [128, 2 * 512], F32)
        self.m_lnxs_war = [0, 0]

    def xbf_tile(self, ct):
        return self.xbf[:, ct * T:(ct + 1) * T]


def emit_load_x(p, ctx, x_in_ap):
    x_t = x_in_ap.rearrange("(a p) t -> a p t", p=128)
    for ct in range(CT):
        p.emit("gpsimd", lambda e, ct=ct: e.dma_start(
            out=ctx.xbf_tile(ct), in_=x_t[ct]), inc="qw", inc_by=16)
    ctx.xbf_wait = (("qw", p.val("qw")),)
    ctx.xsrc_t = x_t
    ctx.m_xsrc = 0


class OutLN:

    def __init__(self, p, ctx, tag):
        self.p, self.ctx, self.tag = p, ctx, tag
        self.sel = ctx.spill_sel ^ 1
        self.pstack = ExitStack()
        PS = lambda nm, shape: self.pstack.enter_context(
            p.nc.psum_tensor(p._n(nm), shape, F32))
        self.r_out = Rot([PS("oln", [128, 512]), PS("oln", [128, 512])])
        self.ps_stat = PS("lnst", [128, 512])
        self.r_rep = self.r_out
        self.m_guard_v = p.val("v")
        self.m_guard_s = p.val("s")
        self.first_mm = True
        self.m_sums_prev = 0
        self.m_rowmath_prev = 0
        self.m_res_reads_prev = 0
        self.m_stage_war_v = 0
        self.sp_sh = {}
        self.done = 0
        self.stage_extra = []

    def chunk(self, tc, emitter, scale_cols=None, stage_mult=1.0):
        p, ctx = self.p, self.ctx
        resb, sq, R = ctx.ln_resb, ctx.ln_sq, ctx.ln_rows
        arep, brep, xs = ctx.ln_arep, ctx.ln_brep, ctx.ln_xs
        S5 = 512

        stg = {}
        for ot in range(CT):
            w = list(self.stage_extra)
            if self.m_stage_war_v:
                w.append(("v", self.m_stage_war_v))
            if ctx.m_xsrc:
                w.append(("qx", ctx.m_xsrc))
            src_ap = ctx.xsrc_t[ot][:, tc * S5:(tc + 1) * S5]
            stg[ot] = p.emit("sync", lambda e, ot=ot, src_ap=src_ap: e.dma_start(
                out=ctx.xstage[:, ot * S5:(ot + 1) * S5], in_=src_ap),
                waits=tuple(w), inc="qx", inc_by=16)

        for ot in range(CT):
            bi, bank, war = self.r_out.take()
            w = list(war)
            if self.first_mm:
                w += [("v", self.m_guard_v), ("s", self.m_guard_s)]
                self.first_mm = False
            if self.done > 0 and ot == 0:
                w.append(("s", self.m_res_reads_prev))
            mt = emitter(tc, ot, bank, tuple(w))
            if stage_mult != 1.0:
                assert scale_cols is None
                mv = p.emit("vector", lambda e, ot=ot, bank=bank, sm=stage_mult:
                            e.scalar_tensor_tensor(
                    ctx.res[:, ot * S5:(ot + 1) * S5],
                    ctx.xstage[:, ot * S5:(ot + 1) * S5], sm, bank[:],
                    ALU.mult, ALU.add),
                    waits=(("t", mt), ("qx", stg[ot])), inc="v")
            elif scale_cols is not None:
                sc = scale_cols(ot)
                mv = p.emit("vector", lambda e, ot=ot, bank=bank, sc=sc:
                            e.scalar_tensor_tensor(
                    ctx.res[:, ot * S5:(ot + 1) * S5], bank[:], sc,
                    ctx.xstage[:, ot * S5:(ot + 1) * S5],
                    ALU.mult, ALU.add),
                    waits=(("t", mt), ("qx", stg[ot])), inc="v")
            else:
                mv = p.emit("vector", lambda e, ot=ot, bank=bank: e.tensor_add(
                    ctx.res[:, ot * S5:(ot + 1) * S5], bank[:],
                    ctx.xstage[:, ot * S5:(ot + 1) * S5]),
                    waits=(("t", mt), ("qx", stg[ot])), inc="v")
            self.r_out.release(bi, "v", mv)
        m_res_v = p.val("v")
        self.m_stage_war_v = m_res_v

        for ct in range(CT):
            w = [("v", m_res_v)] if ct == 0 else []
            if self.done > 0 and ct == 0:
                w.append(("t", self.m_sums_prev))
            p.emit("scalar", lambda e, ct=ct: e.copy(
                resb[:, ct * S5:(ct + 1) * S5],
                ctx.res[:, ct * S5:(ct + 1) * S5]), waits=tuple(w), inc="s")
            p.emit("scalar", lambda e, ct=ct: e.activation(
                sq[:, ct * S5:(ct + 1) * S5], resb[:, ct * S5:(ct + 1) * S5],
                AF.Square), inc="s")
        m_sq = p.val("s")
        self.m_res_reads_prev = m_sq

        for which, (srcb, rowb) in enumerate(((resb, 0), (sq, 64))):
            for ct in range(CT):
                w = []
                if which == 0 and ct == 0:
                    w.append(("s", m_sq))
                    if self.m_rowmath_prev:
                        w.append(("v", self.m_rowmath_prev))
                p.emit("tensor", lambda e, srcb=srcb, rowb=rowb, ct=ct:
                       e.matmul(self.ps_stat[rowb:rowb + 1, :], ctx.ones_bf[:],
                                srcb[:, ct * S5:(ct + 1) * S5],
                                start=(ct == 0), stop=(ct == CT - 1)),
                       waits=tuple(w), inc="t" if ct == CT - 1 else None)
        self.m_sums_prev = p.val("t")

        p.emit("vector", lambda e: e.tensor_scalar_mul(
            R[:, 2 * S5:3 * S5], self.ps_stat[0:1, :], 1.0 / C),
            waits=(("t", self.m_sums_prev),), inc="v")
        p.emit("vector", lambda e: e.tensor_scalar_mul(
            R[:, 3 * S5:4 * S5], self.ps_stat[64:65, :], 1.0 / C), inc="v")
        self.m_rowmath_prev = p.val("v")
        p.emit("vector", lambda e: e.tensor_mul(
            R[:, 0:S5], R[:, 2 * S5:3 * S5], R[:, 2 * S5:3 * S5]), inc="v")
        p.emit("vector", lambda e: e.tensor_sub(
            R[:, S5:2 * S5], R[:, 3 * S5:4 * S5], R[:, 0:S5]), inc="v")
        p.emit("vector", lambda e: e.tensor_scalar_add(
            R[:, S5:2 * S5], R[:, S5:2 * S5], 1e-5), inc="v")
        m_var = p.val("v")
        p.emit("scalar", lambda e: e.activation(
            R[:, 3 * S5:4 * S5], R[:, S5:2 * S5], AF.Sqrt),
            waits=(("v", m_var),), inc="s")
        m_sqrt = p.val("s")
        p.emit("vector", lambda e: e.reciprocal(
            R[:, 4 * S5:5 * S5], R[:, 3 * S5:4 * S5]),
            waits=(("s", m_sqrt),), inc="v")
        p.emit("vector", lambda e: e.tensor_mul(
            R[:, 5 * S5:6 * S5], R[:, 2 * S5:3 * S5], R[:, 4 * S5:5 * S5]),
            inc="v")
        p.emit("vector", lambda e: e.tensor_scalar_mul(
            R[:, 5 * S5:6 * S5], R[:, 5 * S5:6 * S5], -1.0), inc="v")
        m_ab = p.val("v")

        for which, dst in ((4, arep), (5, brep)):
            bi, bank, war = self.r_rep.take()
            w = list(war)
            if which == 4:
                w.append(("v", m_ab))
            mt = p.emit("tensor", lambda e, which=which, bank=bank: e.matmul(
                bank[:], ctx.onesF[0:1, 0:128], R[:, which * S5:(which + 1) * S5],
                start=True, stop=True), waits=tuple(w), inc="t")
            mv = p.emit("vector", lambda e, dst=dst, bank=bank: e.tensor_copy(
                dst[:], bank[:]), waits=(("t", mt),), inc="v")
            self.r_rep.release(bi, "v", mv)
        m_rep = p.val("v")

        for ct in range(CT):
            h = ct % 2
            off = h * S5
            w = [("v", m_rep)] if ct == 0 else []
            if ct >= 2:
                pm = self.sp_sh[(ct - 2, self.done)]
                w += [("qx", pm[0]), ("s", pm[1])]
            elif self.done > 0:
                pm = self.sp_sh[(ct + 6, self.done - 1)]
                w += [("qx", pm[0]), ("s", pm[1])]
            p.emit("vector", lambda e, ct=ct, off=off: e.tensor_mul(
                xs[:, off:off + S5], ctx.res[:, ct * S5:(ct + 1) * S5],
                arep[:]), waits=tuple(w), inc="v")
            p.emit("vector", lambda e, off=off: e.tensor_add(
                xs[:, off:off + S5], xs[:, off:off + S5], brep[:]), inc="v")
            mv = p.val("v")
            msh = p.emit("scalar", lambda e, ct=ct, tc=tc, off=off: e.copy(
                ctx.xbf[:, ct * T + tc * S5: ct * T + (tc + 1) * S5],
                xs[:, off:off + S5]),
                waits=(("v", mv), ("t", ctx.m_xbf_users_t))
                if (ct == 0 and self.done == 0) else (("v", mv),), inc="s")
            msp = p.emit("sync", lambda e, ct=ct, tc=tc, off=off,
                         sel=self.sel: e.dma_start(
                out=ctx.xsp_t[sel][ct][:, tc * S5:(tc + 1) * S5],
                in_=xs[:, off:off + S5]),
                waits=(("v", mv),), inc="qx", inc_by=16)
            self.sp_sh[(ct, self.done)] = (msp, msh)
        self.done += 1

    def end(self):
        p, ctx = self.p, self.ctx
        assert self.done == NTC
        self.pstack.close()
        ctx.spill_sel = self.sel
        ctx.xsrc_t = ctx.xsp_t[self.sel]
        ctx.m_xsrc = p.val("qx")
        ctx.xbf_wait = (("s", p.val("s")),)
        ctx.m_xbf_users_t = 0


def emit_out_ln(p, ctx, emitter, tag, scale_cols=None):
    ol = OutLN(p, ctx, tag)
    for tc in range(NTC):
        ol.chunk(tc, emitter, scale_cols=scale_cols)
    ol.end()


def emit_store_out(p, ctx, out_ap):
    src = ctx.xsrc_t
    m = ctx.m_xsrc
    out_t = out_ap.rearrange("(a p) t -> a p t", p=128)
    for ct in range(CT):
        p.emit("sync", lambda e, ct=ct: e.dma_start(
            out=out_t[ct], in_=src[ct]), waits=(("qx", m),), inc="qx", inc_by=16)
    return p.val("qx")


def prep_l0_attn(inputs):
    d = {}
    d["l0_wqT"] = lhsT_blocked(np.ascontiguousarray(inputs["L0_attn_Wq"].T))
    d["l0_wkT"] = lhsT_blocked(np.ascontiguousarray(inputs["L0_attn_Wk"].T))
    d["l0_wv"] = rhs_blocked(np.asarray(inputs["L0_attn_Wv"]))
    d["l0_wcT"] = lhsT_flat(np.ascontiguousarray(inputs["L0_attn_Wc"].T))
    lt = np.asarray(inputs["L0_attn_Ltri"]).transpose(0, 2, 1)
    d["l0_ltriT"] = np.ascontiguousarray(
        lt.transpose(1, 0, 2).reshape(64, 16 * 64))
    return d


def emit_l0_attn(p, ctx, debug_yt=None, npairs=8, do_chunks=True):
    import os as _os
    SKIP = set(_os.environ.get("L0SKIP", "").split(","))
    NH = 16
    NCH = NTT

    d_wq = p.dram("l0_wqT", [8, 128, 8 * 128], F32, kind="ExternalInput")
    d_wk = p.dram("l0_wkT", [8, 128, 8 * 128], F32, kind="ExternalInput")
    d_wv = p.dram("l0_wv", [8, 128, C], F32, kind="ExternalInput")
    d_wc = p.dram("l0_wcT", [128, 8 * 1024], F32, kind="ExternalInput")
    d_lt = p.dram("l0_ltriT", [64, NH * 64], F32, kind="ExternalInput")

    wq = p.sb("l0_wq", [128, 2 * 1024], BF16)
    wk = p.sb("l0_wk", [128, 2 * 1024], BF16)
    wv = p.sb("l0_wvn", [128, 2 * 1024], BF16)
    wvt = p.sb("l0_wvt", [128, 2 * 1024], BF16)
    wc = p.sb("l0_wc", [128, 8 * 1024], BF16)
    ltT = p.sb("l0_ltT", [64, NH * 64], BF16)
    lmT = p.sb("l0_lmT", [64, 2 * 64], BF16)
    bm = p.sb("l0_bm", [128, 64], BF16)
    q2 = p.sb("l0_q2", [128, T], BF16)
    k2 = p.sb("l0_k2", [128, T], BF16)
    vt2 = p.sb("l0_vt2", [128, NTT * 128], BF16)
    asb = p.sb("l0_asb", [128, 2 * 128], BF16)
    ktk = p.sb("l0_ktk", [128, 2 * 64], BF16)
    s_f = p.sb("l0_sf", [128, 64], F32)
    s_b = p.sb("l0_sb", [128, 64], BF16)
    yt = p.sb("l0_yt", [128, CT * T], BF16)

    with ExitStack() as pstack:
        PS = lambda nm, shape, dt=F32: pstack.enter_context(
            p.nc.psum_tensor(p._n(nm), shape, dt))
        r_proj = Rot([PS("l0p", [128, 512]), PS("l0p", [128, 512])])
        r_sc = Rot([PS("l0sc", [128, 128]), PS("l0sc", [128, 128])])
        r_y = Rot([PS("l0y", [64, 128]), PS("l0y", [64, 128])])
        ps_su = PS("l0su", [128, 64])
        ps_kt = PS("l0kt", [128, 64], BF16)
        m_su_read_v = 0
        m_kt_read_s = 0

        xw = list(ctx.xbf_wait)

        m_lt = p.emit("gpsimd", lambda e: e.dma_start(
            out=ltT[:], in_=d_lt), inc="qw", inc_by=16)

        wq_free_t = [0, 0]
        wk_free_t = [0, 0]
        wv_free_t = [0, 0]
        wq_m, wk_m, wv_m = {}, {}, {}

        def prefetch_pair(hp):
            sl = hp % 2
            wq_m[hp] = p.emit("gpsimd", lambda e, hp=hp, sl=sl: e.dma_start(
                out=wq[:, sl * 1024:(sl + 1) * 1024], in_=d_wq[hp]),
                waits=(("t", wq_free_t[sl]),), inc="qw", inc_by=16)
            wk_m[hp] = p.emit("gpsimd", lambda e, hp=hp, sl=sl: e.dma_start(
                out=wk[:, sl * 1024:(sl + 1) * 1024], in_=d_wk[hp]),
                waits=(("t", wk_free_t[sl]),), inc="qw", inc_by=16)
            wv_m[hp] = p.emit("gpsimd", lambda e, hp=hp, sl=sl: e.dma_start(
                out=wv[:, sl * 1024:(sl + 1) * 1024], in_=d_wv[hp]),
                waits=(("t", wv_free_t[sl]),), inc="qw", inc_by=16)

        prefetch_pair(0)
        if npairs > 1:
            prefetch_pair(1)

        for hp in range(npairs):
            sl = hp % 2
            h0, h1 = 2 * hp, 2 * hp + 1

            for i, h in enumerate((h0, h1) if "bm" not in SKIP else ()):
                w = []
                if hp == 0 and i == 0:
                    w.append(("qw", m_lt))
                    w.append(("qx", ctx.m_const))
                if i == 0:
                    w.append(("t", p.val("t")))
                p.emit("vector", lambda e, i=i, h=h: e.tensor_mul(
                    lmT[:, i * 64:(i + 1) * 64], ltT[:, h * 64:(h + 1) * 64],
                    ctx.maskLT[:]), waits=tuple(w), inc="v")
            m_lm = p.val("v")
            bm_mt = {}
            for i in range(2 if "bm" not in SKIP else 0):
                w = [("v", m_lm)]
                if i == 0 and m_su_read_v:
                    w.append(("v", m_su_read_v))
                if i == 1:
                    w.append(("v", p.val("v")))
                bm_mt[i] = p.emit("tensor", lambda e, i=i: e.matmul(
                    ps_su[i * 64:(i + 1) * 64, :], lmT[:, i * 64:(i + 1) * 64],
                    lmT[:, i * 64:(i + 1) * 64], start=True, stop=True),
                    waits=tuple(w), inc="t")
                p.emit("vector", lambda e, i=i: e.tensor_copy(
                    bm[i * 64:(i + 1) * 64, :], ps_su[i * 64:(i + 1) * 64, :]),
                    waits=(("t", bm_mt[i]),), inc="v")
            m_bm = p.val("v")
            m_su_read_v = m_bm

            m_wv = wv_m[hp]
            for kt in range(CT if "wvt" not in SKIP else 0):
                for i in range(2):
                    bi, bank, war = r_sc.take()
                    w = list(war)
                    if kt == 0 and i == 0:
                        w += [("qw", m_wv), ("v", m_bm)]
                    mt = p.emit("tensor", lambda e, kt=kt, i=i, sl=sl, bank=bank:
                                e.matmul(
                        bank[:, 0:64],
                        wv[(i * 64):(i + 1) * 64,
                           sl * 1024 + kt * 128: sl * 1024 + (kt + 1) * 128],
                        bm[(i * 64):(i + 1) * 64, :],
                        start=True, stop=True),
                        waits=tuple(w), inc="t")
                    ms = p.emit("scalar", lambda e, kt=kt, i=i, sl=sl, bank=bank:
                                e.copy(
                        wvt[:, sl * 1024 + kt * 128 + i * 64:
                            sl * 1024 + kt * 128 + (i + 1) * 64],
                        bank[:, 0:64]), waits=(("t", mt),), inc="s")
                    r_sc.release(bi, "s", ms)
            m_wvt = p.val("s")
            wv_free_t[sl] = p.val("t")

            for which, (slab, dst, wm) in enumerate(
                    (((wq, q2, wq_m[hp]), (wk, k2, wk_m[hp]))) if "proj" not in SKIP else ()):
                for tc in range(NTC):
                    bi, bank, war = r_proj.take()
                    for kt in range(CT):
                        w = list(war) if kt == 0 else []
                        if tc == 0 and kt == 0:
                            w += [("qw", wm)] + xw
                            if which == 0 and hp == 0:
                                pass
                        mt = p.emit("tensor", lambda e, tc=tc, kt=kt, sl=sl,
                                    slab=slab, bank=bank: e.matmul(
                            bank[:], slab[:, sl * 1024 + kt * 128:
                                          sl * 1024 + (kt + 1) * 128],
                            ctx.xbf[:, kt * T + tc * 512: kt * T + (tc + 1) * 512],
                            start=(kt == 0), stop=(kt == CT - 1)),
                            waits=tuple(w), inc="t" if kt == CT - 1 else None)
                    ms = p.emit("scalar", lambda e, tc=tc, dst=dst, bank=bank:
                                e.copy(dst[:, tc * 512:(tc + 1) * 512], bank[:]),
                                waits=(("t", mt),), inc="s")
                    r_proj.release(bi, "s", ms)
            m_qk = p.val("s")
            wq_free_t[sl] = p.val("t")
            wk_free_t[sl] = p.val("t")
            if hp + 2 < npairs:
                prefetch_pair(hp + 2)

            for tt in range(NTT if "vt2" not in SKIP else 0):
                bi, bank, war = r_proj.take()
                for kt in range(CT):
                    w = list(war) if kt == 0 else []
                    if tt == 0 and kt == 0:
                        w += [("s", m_wvt)] + xw
                    mt = p.emit("tensor", lambda e, tt=tt, kt=kt, sl=sl, bank=bank:
                                e.matmul(
                        bank[:, 0:128],
                        ctx.xbf[:, kt * T + tt * 128: kt * T + (tt + 1) * 128],
                        wvt[:, sl * 1024 + kt * 128: sl * 1024 + (kt + 1) * 128],
                        start=(kt == 0), stop=(kt == CT - 1)),
                        waits=tuple(w), inc="t" if kt == CT - 1 else None)
                ms = p.emit("scalar", lambda e, tt=tt, bank=bank: e.copy(
                    vt2[:, tt * 128:(tt + 1) * 128], bank[:, 0:128]),
                    waits=(("t", mt),), inc="s")
                r_proj.release(bi, "s", ms)
            m_vt = p.val("s")
            ctx.m_xbf_users_t = max(ctx.m_xbf_users_t, p.val("t"))

            w = [("t", p.val("t"))]
            if m_su_read_v:
                w.append(("v", m_su_read_v))
            p.emit("vector", lambda e: e.memset(s_f[:], 0.0), waits=tuple(w), inc="v")
            p.emit("vector", lambda e: e.memset(s_b[:], 0.0), inc="v")
            m_sbf = p.val("v")

            m_y_ev = {}
            for cc in range(NCH if do_chunks else 0):
                for i, h in enumerate((h0, h1)):
                    hrow = slice(i * 64, (i + 1) * 64)
                    tsl = slice(cc * 128, (cc + 1) * 128)
                    bi_sc, bank_sc, war_sc = r_sc.take()
                    w = list(war_sc)
                    if cc == 0 and i == 0:
                        w.append(("s", m_vt))
                    mt_sc = p.emit("tensor", lambda e, hrow=hrow, tsl=tsl,
                                   bank=bank_sc: e.matmul(
                        bank[:, 0:128], k2[hrow, tsl], q2[hrow, tsl],
                        start=True, stop=True), waits=tuple(w), inc="t")
                    w = [("s", m_kt_read_s)] if m_kt_read_s else []
                    mt_kt = p.emit("tensor", lambda e, hrow=hrow, tsl=tsl, i=i:
                                   e.transpose(ps_kt[:], k2[hrow, tsl],
                                               ctx.ident[i * 64:(i + 1) * 64,
                                                         i * 64:(i + 1) * 64]),
                                   waits=tuple(w), inc="t")
                    m_asb = p.emit("vector", lambda e, i=i, bank=bank_sc:
                                   e.tensor_mul(asb[:, i * 128:(i + 1) * 128],
                                                bank[:, 0:128], ctx.maskA[:]),
                                   waits=(("t", mt_sc),), inc="v")
                    r_sc.release(bi_sc, "v", m_asb)
                    m_kt_read_s = p.emit("scalar", lambda e, i=i: e.copy(
                        ktk[:, i * 64:(i + 1) * 64], ps_kt[:]),
                        waits=(("t", mt_kt),), inc="s")
                    bi_y, bank_y, war_y = r_y.take()
                    w = list(war_y) + [("v", m_asb), ("v", m_sbf)]
                    p.emit("tensor", lambda e, cc=cc, i=i, bank=bank_y: e.matmul(
                        bank[:], vt2[:, cc * 128 + i * 64: cc * 128 + (i + 1) * 64],
                        asb[:, i * 128:(i + 1) * 128],
                        start=True, stop=False), waits=tuple(w))
                    mt_y = p.emit("tensor", lambda e, hrow=hrow, tsl=tsl, i=i,
                                  bank=bank_y: e.matmul(
                        bank[:], s_b[i * 64:(i + 1) * 64, :], q2[hrow, tsl],
                        start=False, stop=True), inc="t")
                    w = [("s", m_kt_read_s), ("v", m_sbf)]
                    if m_su_read_v:
                        w.append(("v", m_su_read_v))
                    mt_su = p.emit("tensor", lambda e, cc=cc, i=i: e.matmul(
                        ps_su[i * 64:(i + 1) * 64, :],
                        ktk[:, i * 64:(i + 1) * 64],
                        vt2[:, cc * 128 + i * 64: cc * 128 + (i + 1) * 64],
                        start=True, stop=True), waits=tuple(w), inc="t")
                    ct_y, prow = h // 2, (h % 2) * 64
                    m_y = p.emit("scalar", lambda e, ct_y=ct_y, prow=prow, cc=cc,
                                 bank=bank_y: e.copy(
                        yt[prow:prow + 64,
                           ct_y * T + cc * 128: ct_y * T + (cc + 1) * 128],
                        bank[:]), waits=(("t", mt_y),), inc="s")
                    r_y.release(bi_y, "s", m_y)
                    p.emit("vector", lambda e, i=i: e.tensor_add(
                        s_f[i * 64:(i + 1) * 64, :], s_f[i * 64:(i + 1) * 64, :],
                        ps_su[i * 64:(i + 1) * 64, :]),
                        waits=(("t", mt_su),), inc="v")
                    m_su_read_v = p.val("v")
                    m_sbf = p.emit("vector", lambda e, i=i: e.tensor_copy(
                        s_b[i * 64:(i + 1) * 64, :],
                        s_f[i * 64:(i + 1) * 64, :]), inc="v")
        m_yt = p.val("s")
        if debug_yt is not None:
            yt_t = debug_yt.rearrange("(a p) t -> a p t", p=128)
            for ct in range(CT):
                p.emit("sync", lambda e, ct=ct: e.dma_start(
                    out=yt_t[ct], in_=yt[:, ct * T:(ct + 1) * T]),
                    waits=(("s", m_yt),), inc="qx", inc_by=16)
            p.wait("sync", (("qx", p.val("qx")),))
            return
        m_wc = p.emit("gpsimd", lambda e: e.dma_start(
            out=wc[:], in_=d_wc), inc="qw", inc_by=16)

    def wc_chunk(tc, ot, bank, waits):
        mt = None
        for kt in range(CT):
            w = list(waits) if kt == 0 else []
            if tc == 0 and ot == 0 and kt == 0:
                w += [("qw", m_wc), ("s", m_yt)]
            mt = p.emit("tensor", lambda e, tc=tc, ot=ot, kt=kt, bank=bank:
                        e.matmul(
                bank[:], wc[:, ot * 1024 + kt * 128: ot * 1024 + (kt + 1) * 128],
                yt[:, kt * T + tc * 512: kt * T + (tc + 1) * 512],
                start=(kt == 0), stop=(kt == CT - 1)),
                waits=tuple(w), inc="t" if kt == CT - 1 else None)
        return mt

    emit_out_ln(p, ctx, wc_chunk, "l0a")

def prep_l0_ffn(inputs):
    d = {}
    d["l0_w1T"] = lhsT_blocked(np.ascontiguousarray(np.asarray(inputs["L0_ffn_w1_w"]).T))
    d["l0_w2T"] = lhsT_blocked(np.ascontiguousarray(np.asarray(inputs["L0_ffn_w2_w"]).T))
    d["l0_h1T"] = lhsT_blocked(np.ascontiguousarray(np.asarray(inputs["L0_ffn_h1_w"]).T))
    d["l0_h2T"] = lhsT_blocked(np.ascontiguousarray(np.asarray(inputs["L0_ffn_h2_w"]).T))
    d["l0_w3T"] = lhsT_blocked(np.ascontiguousarray(np.asarray(inputs["L0_ffn_w3_w"]).T))
    sc = np.zeros((128, 4 * 32 + 8), np.float32)
    for i, nm in enumerate(("w1_s", "w2_s", "h1_s", "h2_s")):
        sc[:, i * 32:(i + 1) * 32] = np.asarray(inputs[f"L0_ffn_{nm}"]).reshape(32, 128).T
    sc[:, 128:136] = np.asarray(inputs["L0_ffn_w3_s"]).reshape(8, 128).T
    d["l0_fsc"] = np.ascontiguousarray(sc)
    return d


def emit_l0_ffn(p, ctx):
    HT = 32
    S5 = 512
    d_w1 = p.dram("l0_w1T", [HT, 128, 8 * 128], F32, kind="ExternalInput")
    d_w2 = p.dram("l0_w2T", [HT, 128, 8 * 128], F32, kind="ExternalInput")
    d_h1 = p.dram("l0_h1T", [HT, 128, 32 * 128], F32, kind="ExternalInput")
    d_h2 = p.dram("l0_h2T", [HT, 128, 32 * 128], F32, kind="ExternalInput")
    d_w3 = p.dram("l0_w3T", [CT, 128, 32 * 128], F32, kind="ExternalInput")
    d_sc = p.dram("l0_fsc", [128, 136], F32, kind="ExternalInput")

    hbuf = p.sb("l0f_h", [128, HT * S5], BF16)
    gbuf = p.sb("l0f_g", [128, HT * S5], BF16)
    t1s = p.sb("l0f_t1", [128, 2 * S5], BF16)
    w12 = p.sb("l0f_w12", [128, 4 * 1024], BF16)
    hsl = p.sb("l0f_hsl", [128, 2 * 4096], BF16)
    w3s = p.sb("l0f_w3s", [128, 4096], BF16)
    scs = p.sb("l0f_sc", [128, 136], F32)

    m_sc = p.emit("sync", lambda e: e.dma_start(out=scs[:], in_=d_sc),
                  inc="qx", inc_by=16)
    xw = list(ctx.xbf_wait)

    ol = OutLN(p, ctx, "l0f")
    with ExitStack() as pstack:
        PS = lambda nm, shape: pstack.enter_context(
            p.nc.psum_tensor(p._n(nm), shape, F32))
        rf = Rot([PS("l0f", [128, S5]) for _ in range(4)])

        w12_free = [0, 0]
        hsl_free = [0, 0]
        w3s_free = [0]
        w12_m = {}
        hsl_m = {}
        w3s_m = {}

        def pf_w12(q, ot):
            sl = ot % 2
            w12_m[(q, ot)] = (
                p.emit("gpsimd", lambda e, ot=ot, sl=sl: e.dma_start(
                    out=w12[:, sl * 2048:sl * 2048 + 1024], in_=d_w1[ot]),
                    waits=(("t", w12_free[sl]),), inc="qw", inc_by=16),
                p.emit("gpsimd", lambda e, ot=ot, sl=sl: e.dma_start(
                    out=w12[:, sl * 2048 + 1024:sl * 2048 + 2048], in_=d_w2[ot]),
                    inc="qw", inc_by=16))

        def pf_h(q, ot, dsrc, store):
            sl = ot % 2
            store[(q, ot)] = p.emit("gpsimd", lambda e, ot=ot, sl=sl, dsrc=dsrc:
                                    e.dma_start(
                out=hsl[:, sl * 4096:(sl + 1) * 4096], in_=dsrc[ot]),
                waits=(("t", hsl_free[sl]),), inc="qw", inc_by=16)

        def pf_w3(q, ot):
            w3s_m[(q, ot)] = p.emit("gpsimd", lambda e, ot=ot: e.dma_start(
                out=w3s[:], in_=d_w3[ot]),
                waits=(("t", w3s_free[0]),), inc="qw", inc_by=16)

        m_h_done = 0
        m_g_done = 0
        t1_war = [0, 0]
        for q in range(NTC):
            pf_w12(q, 0)
            pf_w12(q, 1)
            for ot in range(HT):
                sl = ot % 2
                bi1, b1, war1 = rf.take()
                bi2, b2, war2 = rf.take()
                for kt in range(CT):
                    w = list(war1) if kt == 0 else []
                    if kt == 0:
                        w.append(("qw", w12_m[(q, ot)][0]))
                        if q == 0 and ot == 0:
                            w += xw
                    mt1 = p.emit("tensor", lambda e, q=q, ot=ot, kt=kt, sl=sl,
                                 b1=b1: e.matmul(
                        b1[:], w12[:, sl * 2048 + kt * 128:sl * 2048 + (kt + 1) * 128],
                        ctx.xbf[:, kt * T + q * S5: kt * T + (q + 1) * S5],
                        start=(kt == 0), stop=(kt == CT - 1)),
                        waits=tuple(w), inc="t" if kt == CT - 1 else None)
                for kt in range(CT):
                    w = list(war2) if kt == 0 else []
                    if kt == 0:
                        w.append(("qw", w12_m[(q, ot)][1]))
                    mt2 = p.emit("tensor", lambda e, q=q, ot=ot, kt=kt, sl=sl,
                                 b2=b2: e.matmul(
                        b2[:], w12[:, sl * 2048 + 1024 + kt * 128:
                                   sl * 2048 + 1024 + (kt + 1) * 128],
                        ctx.xbf[:, kt * T + q * S5: kt * T + (q + 1) * S5],
                        start=(kt == 0), stop=(kt == CT - 1)),
                        waits=tuple(w), inc="t" if kt == CT - 1 else None)
                w12_free[sl] = p.val("t")
                if ot + 2 < HT:
                    pf_w12(q, ot + 2)
                tsl = ot % 2
                w = [("t", mt1), ("v", t1_war[tsl])]
                if q == 0 and ot == 0:
                    w.append(("qx", m_sc))
                ms1 = p.emit("scalar", lambda e, ot=ot, b1=b1, tsl=tsl: e.activation(
                    t1s[:, tsl * S5:(tsl + 1) * S5], b1[:], AF.Silu,
                    scale=scs[:, ot:ot + 1]),
                    waits=tuple(w), inc="s")
                rf.release(bi1, "s", ms1)
                mh = p.emit("vector", lambda e, ot=ot, b2=b2, tsl=tsl: e.scalar_tensor_tensor(
                    hbuf[:, ot * S5:(ot + 1) * S5], b2[:], scs[:, 32 + ot:33 + ot],
                    t1s[:, tsl * S5:(tsl + 1) * S5], ALU.mult, ALU.mult),
                    waits=(("t", mt2), ("s", ms1)), inc="v")
                t1_war[tsl] = mh
                rf.release(bi2, "v", mh)
            m_h_done = p.val("v")

            pf_h(q, 0, d_h1, hsl_m)
            pf_h(q, 1, d_h1, hsl_m)
            for ot in range(HT):
                sl = ot % 2
                bi, bank, war = rf.take()
                for kt in range(HT):
                    w = list(war) if kt == 0 else []
                    if kt == 0:
                        w.append(("qw", hsl_m[(q, ot)]))
                        if ot == 0:
                            w.append(("v", m_h_done))
                    mt = p.emit("tensor", lambda e, ot=ot, kt=kt, sl=sl,
                                bank=bank: e.matmul(
                        bank[:], hsl[:, sl * 4096 + kt * 128:
                                     sl * 4096 + (kt + 1) * 128],
                        hbuf[:, kt * S5:(kt + 1) * S5],
                        start=(kt == 0), stop=(kt == HT - 1)),
                        waits=tuple(w), inc="t" if kt == HT - 1 else None)
                hsl_free[sl] = p.val("t")
                if ot + 2 < HT:
                    pf_h(q, ot + 2, d_h1, hsl_m)
                ms = p.emit("scalar", lambda e, ot=ot, bank=bank: e.activation(
                    gbuf[:, ot * S5:(ot + 1) * S5], bank[:], AF.Gelu,
                    scale=scs[:, 64 + ot:65 + ot]),
                    waits=(("t", mt),), inc="s")
                rf.release(bi, "s", ms)
            m_g_done = p.val("s")

            hsl_m2 = {}
            pf_h(q, 0, d_h2, hsl_m2)
            pf_h(q, 1, d_h2, hsl_m2)
            for ot in range(HT):
                sl = ot % 2
                bi, bank, war = rf.take()
                for kt in range(HT):
                    w = list(war) if kt == 0 else []
                    if kt == 0:
                        w.append(("qw", hsl_m2[(q, ot)]))
                        if ot == 0:
                            w.append(("s", m_g_done))
                    mt = p.emit("tensor", lambda e, ot=ot, kt=kt, sl=sl,
                                bank=bank: e.matmul(
                        bank[:], hsl[:, sl * 4096 + kt * 128:
                                     sl * 4096 + (kt + 1) * 128],
                        gbuf[:, kt * S5:(kt + 1) * S5],
                        start=(kt == 0), stop=(kt == HT - 1)),
                        waits=tuple(w), inc="t" if kt == HT - 1 else None)
                hsl_free[sl] = p.val("t")
                if ot + 2 < HT:
                    pf_h(q, ot + 2, d_h2, hsl_m2)
                ms = p.emit("scalar", lambda e, ot=ot, bank=bank: e.activation(
                    hbuf[:, ot * S5:(ot + 1) * S5], bank[:], AF.Gelu,
                    scale=scs[:, 96 + ot:97 + ot]),
                    waits=(("t", mt),), inc="s")
                rf.release(bi, "s", ms)
            m_g2_done = p.val("s")

            pf_w3(q, 0)

            def w3_emitter(tc, ot, bank, waits, q=q, m_g2_done=m_g2_done):
                mt = None
                for kt in range(HT):
                    w = list(waits) if kt == 0 else []
                    if kt == 0:
                        w.append(("qw", w3s_m[(q, ot)]))
                        if ot == 0:
                            w.append(("s", m_g2_done))
                    mt = p.emit("tensor", lambda e, ot=ot, kt=kt,
                                bank=bank: e.matmul(
                        bank[:], w3s[:, kt * 128:(kt + 1) * 128],
                        hbuf[:, kt * S5:(kt + 1) * S5],
                        start=(kt == 0), stop=(kt == HT - 1)),
                        waits=tuple(w), inc="t" if kt == HT - 1 else None)
                w3s_free[0] = p.val("t")
                if ot + 1 < CT:
                    pf_w3(q, ot + 1)
                return mt

            ol.chunk(q, w3_emitter, scale_cols=lambda ot: scs[:, 128 + ot:129 + ot])
    ol.end()


def prep_l1_attn(inputs):
    d = {}
    d["l1_wq"] = lhsT_blocked(np.ascontiguousarray(np.asarray(inputs["L1_attn_Wq"])))
    d["l1_wk"] = lhsT_blocked(np.ascontiguousarray(np.asarray(inputs["L1_attn_Wk"])), mtile=64)
    d["l1_wv"] = np.ascontiguousarray(
        rhs_blocked(np.asarray(inputs["L1_attn_Wv"])).transpose(1, 0, 2).reshape(128, CT * 64))
    d["l1_wo"] = lhsT_blocked(np.ascontiguousarray(np.asarray(inputs["L1_attn_Wo"])))
    d["l1_bo"] = np.ascontiguousarray(np.asarray(inputs["L1_attn_bo"]).reshape(1, C))
    return d


def emit_l1_attn(p, ctx):
    NH = 16
    S5 = 512
    d_wq = p.dram("l1_wq", [8, 128, 8 * 128], F32, kind="ExternalInput")
    d_wk = p.dram("l1_wk", [1, 128, 8 * 64], F32, kind="ExternalInput")
    d_wv = p.dram("l1_wv", [128, CT * 64], F32, kind="ExternalInput")
    d_wo = p.dram("l1_wo", [8, 128, 8 * 128], F32, kind="ExternalInput")
    d_bo = p.dram("l1_bo", [1, C], F32, kind="ExternalInput")

    qT = p.sb("l1_qT", [128, CT * T], BF16)
    k2 = p.sb("l1_k2", [128, T], BF16)
    vext = p.sb("l1_vx", [128, NTT * 65], BF16)
    att = p.sb("l1_att", [128, CT * T], BF16)
    pT = p.sb("l1_pT", [128, 3 * S5], BF16)
    praw = p.sb("l1_praw", [128, S5], BF16)
    zrep = p.sb("l1_zrep", [128, S5], F32)
    zr = p.sb("l1_zr", [1, S5], F32)
    wsl = p.sb("l1_wsl", [128, 2 * 1024], BF16)
    wksl = p.sb("l1_wksl", [128, 512], BF16)
    wvsl = p.sb("l1_wvsl", [128, CT * 64], BF16)
    bo_sb = p.sb("l1_bo_sb", [1, C], F32)

    xw = list(ctx.xbf_wait)

    with ExitStack() as pstack:
        PS = lambda nm, shape: pstack.enter_context(
            p.nc.psum_tensor(p._n(nm), shape, F32))
        r_st = Rot([PS("l1s", [128, S5]) for _ in range(3)])
        r_o = Rot([PS("l1o", [128, S5]) for _ in range(2)])
        r_z = Rot([PS("l1z", [128, S5]) for _ in range(2)])

        m_wk = p.emit("gpsimd", lambda e: e.dma_start(
            out=wksl[:], in_=d_wk[0]), inc="qw", inc_by=16)
        m_wv = p.emit("gpsimd", lambda e: e.dma_start(
            out=wvsl[:], in_=d_wv), inc="qw", inc_by=16)
        m_bo = p.emit("sync", lambda e: e.dma_start(
            out=bo_sb[:], in_=d_bo), inc="qx", inc_by=16)
        wq_m = {}
        wq_free = [0, 0]

        def pf_wq(ot):
            sl = ot % 2
            wq_m[ot] = p.emit("gpsimd", lambda e, ot=ot, sl=sl: e.dma_start(
                out=wsl[:, sl * 1024:(sl + 1) * 1024], in_=d_wq[ot]),
                waits=(("t", wq_free[sl]),), inc="qw", inc_by=16)

        pf_wq(0)
        pf_wq(1)

        for ot in range(CT):
            sl = ot % 2
            for tc in range(NTC):
                bi, bank, war = r_o.take()
                for kt in range(CT):
                    w = list(war) if kt == 0 else []
                    if kt == 0:
                        w.append(("qw", wq_m[ot]))
                        if ot == 0 and tc == 0:
                            w += xw
                    mt = p.emit("tensor", lambda e, ot=ot, tc=tc, kt=kt, sl=sl,
                                bank=bank: e.matmul(
                        bank[:], wsl[:, sl * 1024 + kt * 128:
                                     sl * 1024 + (kt + 1) * 128],
                        ctx.xbf[:, kt * T + tc * S5: kt * T + (tc + 1) * S5],
                        start=(kt == 0), stop=(kt == CT - 1)),
                        waits=tuple(w), inc="t" if kt == CT - 1 else None)
                ms = p.emit("scalar", lambda e, ot=ot, tc=tc, bank=bank: e.copy(
                    qT[:, ot * T + tc * S5: ot * T + (tc + 1) * S5], bank[:]),
                    waits=(("t", mt),), inc="s")
                r_o.release(bi, "s", ms)
            wq_free[sl] = p.val("t")
            if ot + 2 < CT:
                pf_wq(ot + 2)

        for half in range(2):
            for tc in range(NTC):
                bi, bank, war = r_st.take()
                for kt in range(CT):
                    w = list(war) if kt == 0 else []
                    if half == 0 and tc == 0 and kt == 0:
                        w += [("qw", m_wk)] + xw
                    mt = p.emit("tensor", lambda e, half=half, tc=tc, kt=kt,
                                bank=bank: e.matmul(
                        bank[half * 64:(half + 1) * 64, :],
                        wksl[:, kt * 64:(kt + 1) * 64],
                        ctx.xbf[:, kt * T + tc * S5: kt * T + (tc + 1) * S5],
                        start=(kt == 0), stop=(kt == CT - 1)),
                        waits=tuple(w), inc="t" if kt == CT - 1 else None)
                ms = p.emit("scalar", lambda e, half=half, tc=tc, bank=bank:
                            e.copy(k2[half * 64:(half + 1) * 64,
                                      tc * S5:(tc + 1) * S5],
                                   bank[half * 64:(half + 1) * 64, :]),
                            waits=(("t", mt),), inc="s")
                r_st.release(bi, "s", ms)

        for tt in range(NTT):
            p.emit("vector", lambda e, tt=tt: e.memset(
                vext[:, tt * 65 + 64:tt * 65 + 65], 1.0), inc="v")
        m_vext = p.val("v")
        for tt in range(NTT):
            bi, bank, war = r_st.take()
            for kt in range(CT):
                w = list(war) if kt == 0 else []
                if tt == 0 and kt == 0:
                    w += [("qw", m_wv)] + xw
                mt = p.emit("tensor", lambda e, tt=tt, kt=kt, bank=bank:
                            e.matmul(
                    bank[:, 0:64],
                    ctx.xbf[:, kt * T + tt * 128: kt * T + (tt + 1) * 128],
                    wvsl[:, kt * 64:(kt + 1) * 64],
                    start=(kt == 0), stop=(kt == CT - 1)),
                    waits=tuple(w), inc="t" if kt == CT - 1 else None)
            ms = p.emit("scalar", lambda e, tt=tt, bank=bank: e.copy(
                vext[:, tt * 65:tt * 65 + 64], bank[:, 0:64]),
                waits=(("t", mt),), inc="s")
            r_st.release(bi, "s", ms)
        m_qkv = p.val("s")

        pT_war = [0, 0, 0]
        praw_war = 0
        for h in range(NH):
            hb = (h % 2) * 64
            qtile = h // 2
            for ic in range(NTC):
                bi_o, bank_o, war_o = r_o.take()
                bi_z, bank_z, war_z = r_z.take()
                w_first_pv = list(war_o)
                w_first_z = list(war_z)
                jts = list(range(4 * ic + 4))
                exp_ms = {}
                stq = {}

                def emit_pv(pidx, last):
                    pjt, pslot = stq[pidx]
                    w = [exp_ms[pidx]]
                    if pidx == 0:
                        w += w_first_pv
                    p.emit("tensor", lambda e, pjt=pjt, pslot=pslot, hb=hb,
                           bank=bank_o: e.matmul(
                        bank[hb:hb + 64, :],
                        vext[:, pjt * 65:pjt * 65 + 64],
                        pT[:, pslot * S5:(pslot + 1) * S5],
                        start=(pidx == 0), stop=last), waits=tuple(w))
                    w = []
                    if pidx == 0:
                        w += w_first_z
                    mz = p.emit("tensor", lambda e, pjt=pjt, pslot=pslot,
                                bank=bank_z: e.matmul(
                        bank[0:1, :],
                        vext[:, pjt * 65 + 64:pjt * 65 + 65],
                        pT[:, pslot * S5:(pslot + 1) * S5],
                        start=(pidx == 0), stop=last), waits=tuple(w), inc="t")
                    pT_war[pslot] = mz
                    return mz

                for idx, jt in enumerate(jts):
                    bi_s, bank_s, war_s = r_st.take()
                    w = list(war_s)
                    if h == 0 and ic == 0 and idx == 0:
                        w += [("s", m_qkv), ("v", m_vext)]
                    mt_s = p.emit("tensor", lambda e, jt=jt, hb=hb, qtile=qtile,
                                  ic=ic, bank=bank_s: e.matmul(
                        bank[:], k2[hb:hb + 64, jt * 128:(jt + 1) * 128],
                        qT[hb:hb + 64,
                           qtile * T + ic * S5: qtile * T + (ic + 1) * S5],
                        start=True, stop=True), waits=tuple(w), inc="t")
                    slot = idx % 3
                    diag = (jt // 4 == ic)
                    if diag:
                        nonlocal_praw = praw_war
                        ms_e = p.emit("scalar", lambda e, bank=bank_s: e.activation(
                            praw[:], bank[:], AF.Exp, scale=0.125),
                            waits=(("t", mt_s), ("v", nonlocal_praw),
                                   ("t", pT_war[slot]))[0:3], inc="s")
                        mv_e = p.emit("vector", lambda e, jt=jt, slot=slot:
                                      e.tensor_mul(
                            pT[:, slot * S5:(slot + 1) * S5], praw[:],
                            ctx.maskC4[:, (jt % 4) * S5:(jt % 4 + 1) * S5]),
                            waits=(("s", ms_e), ("t", pT_war[slot])), inc="v")
                        praw_war = mv_e
                        exp_ms[idx] = ("v", mv_e)
                    else:
                        ms_e = p.emit("scalar", lambda e, slot=slot, bank=bank_s:
                                      e.activation(
                            pT[:, slot * S5:(slot + 1) * S5], bank[:],
                            AF.Exp, scale=0.125),
                            waits=(("t", mt_s), ("t", pT_war[slot])), inc="s")
                        exp_ms[idx] = ("s", ms_e)
                    r_st.release(bi_s, exp_ms[idx][0], exp_ms[idx][1])
                    stq[idx] = (jt, slot)
                    if idx >= 2:
                        emit_pv(idx - 2, last=False)
                n = len(jts)
                for pidx in range(max(0, n - 2), n):
                    emit_pv(pidx, last=(pidx == n - 1))
                mt_pv = p.val("t")
                mzr = p.emit("vector", lambda e, bank=bank_z: e.reciprocal(
                    zr[:], bank[0:1, :]), waits=(("t", mt_pv),), inc="v")
                mt_rep = p.emit("tensor", lambda e, hb=hb, bank=bank_z: e.matmul(
                    bank[hb:hb + 64, :], ctx.onesF[0:1, 0:64], zr[:],
                    start=True, stop=True), waits=(("v", mzr),), inc="t")
                mv_rep = p.emit("vector", lambda e, hb=hb, bank=bank_z:
                                e.tensor_copy(zrep[hb:hb + 64, :],
                                              bank[hb:hb + 64, :]),
                                waits=(("t", mt_rep),), inc="v")
                mnorm = p.emit("vector", lambda e, hb=hb, qtile=qtile, ic=ic,
                               bank=bank_o: e.tensor_mul(
                    att[hb:hb + 64,
                        qtile * T + ic * S5: qtile * T + (ic + 1) * S5],
                    bank[hb:hb + 64, :], zrep[hb:hb + 64, :]),
                    waits=(("v", mv_rep),), inc="v")
                r_o.release(bi_o, "v", mnorm)
                r_z.release(bi_z, "v", mnorm)
        m_att_v = p.val("v")

    seq = [(tc, ot) for tc in range(NTC) for ot in range(CT)]
    wo_m = {}
    wo_free = [wq_free[0], wq_free[1]]

    def pf_wo(i):
        if i >= len(seq):
            return
        tc, ot = seq[i]
        sl = i % 2
        wo_m[i] = p.emit("gpsimd", lambda e, ot=ot, sl=sl: e.dma_start(
            out=wsl[:, sl * 1024:(sl + 1) * 1024], in_=d_wo[ot]),
            waits=(("t", wo_free[sl]),), inc="qw", inc_by=16)

    pf_wo(0)
    pf_wo(1)
    wo_i = [0]

    def wo_emitter(tc, ot, bank, waits):
        i = wo_i[0]
        assert seq[i] == (tc, ot)
        sl = i % 2
        for kt in range(CT):
            w = list(waits) if kt == 0 else []
            if kt == 0:
                w.append(("qw", wo_m[i]))
                if ot == 0 and tc == 0:
                    w += [("v", m_att_v), ("qx", m_bo)]
            p.emit("tensor", lambda e, ot=ot, tc=tc, kt=kt, sl=sl, bank=bank:
                   e.matmul(
                bank[:], wsl[:, sl * 1024 + kt * 128: sl * 1024 + (kt + 1) * 128],
                att[:, kt * T + tc * S5: kt * T + (tc + 1) * S5],
                start=(kt == 0), stop=False),
                waits=tuple(w))
        mt = p.emit("tensor", lambda e, ot=ot, bank=bank: e.matmul(
            bank[:], bo_sb[0:1, ot * 128:(ot + 1) * 128],
            ctx.onesF[0:1, 0:S5], start=False, stop=True), inc="t")
        wo_free[sl] = p.val("t")
        pf_wo(i + 2)
        wo_i[0] += 1
        return mt

    emit_out_ln(p, ctx, wo_emitter, "l1a")


def prep_l1_ffn(inputs):
    d = {}
    d["l1f_w1T"] = lhsT_blocked(np.ascontiguousarray(np.asarray(inputs["L1_ffn_W1"]).T))
    d["l1f_w2T"] = lhsT_blocked(np.ascontiguousarray(np.asarray(inputs["L1_ffn_W2"]).T))
    d["l1f_b1"] = np.ascontiguousarray(np.asarray(inputs["L1_ffn_b1"]).reshape(1, 4096))
    d["l1f_b2"] = np.ascontiguousarray(np.asarray(inputs["L1_ffn_b2"]).reshape(1, C))
    d["l1f_scale"] = np.ascontiguousarray(np.asarray(inputs["L1_ffn_scale"]).reshape(1, 1))
    import ml_dtypes
    rm = np.zeros((4, 4 * 128), np.float32)
    for j in range(4):
        rm[j, j * 128:(j + 1) * 128] = 1.0
    d["l1f_rowmask"] = rm.astype(ml_dtypes.bfloat16)
    return d


def emit_l1_ffn(p, ctx):
    HT = 32
    S5 = 512
    d_w1 = p.dram("l1f_w1T", [HT, 128, 8 * 128], F32, kind="ExternalInput")
    d_w2 = p.dram("l1f_w2T", [CT, 128, 32 * 128], F32, kind="ExternalInput")
    d_b1 = p.dram("l1f_b1", [1, 4096], F32, kind="ExternalInput")
    d_b2 = p.dram("l1f_b2", [1, C], F32, kind="ExternalInput")
    d_sc = p.dram("l1f_scale", [1, 1], F32, kind="ExternalInput")
    d_rm = p.dram("l1f_rowmask", [4, 4 * 128], BF16, kind="ExternalInput")
    ev4_dram = p.dram(p._n("l1f_ev4d"), [4, 1024], BF16)

    hbuf = p.sb("l1f_h", [128, HT * S5], BF16)
    w1sl = p.sb("l1f_w1s", [128, 2 * 1024], BF16)
    w2sl = p.sb("l1f_w2s", [128, 2 * 4096], BF16)
    e1 = p.sb("l1f_e1", [4, HT * 128], BF16)
    o2 = p.sb("l1f_o2", [128, HT * 16], BF16)
    we = p.sb("l1f_we", [16, 1024], BF16)
    wet = p.sb("l1f_wet", [128, 16], BF16)
    ev4 = p.sb("l1f_ev4", [4, 1024], BF16)
    c_sb = p.sb("l1f_c", [4, T], BF16)
    g_sb = p.sb("l1f_g", [16, S5], BF16)
    b1r = p.sb("l1f_b1r", [1, 4096], F32)
    b2r = p.sb("l1f_b2r", [1, C], F32)
    sc1 = p.sb("l1f_sc1", [1, 1], F32)
    sc4 = p.sb("l1f_sc4", [4, 1], F32)
    sc16 = p.sb("l1f_sc16", [16, 1], F32)
    rmask = p.sb("l1f_rm", [4, 4 * 128], BF16)

    xw = list(ctx.xbf_wait)

    ol = OutLN(p, ctx, "l1f")
    with ExitStack() as pstack:
        PS = lambda nm, shape, dt=F32: pstack.enter_context(
            p.nc.psum_tensor(p._n(nm), shape, dt))
        rf = Rot([PS("l1f", [128, S5]) for _ in range(2)])
        ps_t = PS("l1ft", [4, 128], BF16)
        ps_t2 = PS("l1ft2", [16, 128], BF16)

        for dst, srcd in ((b1r, d_b1), (b2r, d_b2), (sc1, d_sc), (rmask, d_rm)):
            p.emit("sync", lambda e, dst=dst, srcd=srcd: e.dma_start(
                out=dst[:], in_=srcd), inc="qx", inc_by=16)
        m_lds = p.val("qx")

        bi, bank, war = rf.take()
        mt = p.emit("tensor", lambda e, bank=bank: e.matmul(
            bank[0:16, 0:1], ctx.onesF[0:1, 0:16], sc1[:],
            start=True, stop=True),
            waits=tuple(list(war) + [("qx", m_lds)] + xw), inc="t")
        mv = p.emit("vector", lambda e, bank=bank: e.tensor_copy(
            sc16[:], bank[0:16, 0:1]), waits=(("t", mt),), inc="v")
        p.emit("vector", lambda e, bank=bank: e.tensor_copy(
            sc4[:], bank[0:4, 0:1]), inc="v")
        mv_sc = p.val("v")
        rf.release(bi, "v", mv_sc)

        for tc in range(NTC):
            bi, bank, war = rf.take()
            for kt in range(CT):
                w = list(war) if kt == 0 else []
                mt = p.emit("tensor", lambda e, tc=tc, kt=kt, bank=bank:
                            e.matmul(
                    bank[0:4, :], ctx.xbf[:, kt * T + 2: kt * T + 10: 2],
                    ctx.xbf[:, kt * T + tc * S5: kt * T + (tc + 1) * S5],
                    start=(kt == 0), stop=(kt == CT - 1)),
                    waits=tuple(w), inc="t" if kt == CT - 1 else None)
            mv = p.emit("vector", lambda e, tc=tc, bank=bank: e.tensor_scalar_mul(
                c_sb[:, tc * S5:(tc + 1) * S5], bank[0:4, :], sc4[:]),
                waits=(("t", mt), ("v", mv_sc)), inc="v")
            rf.release(bi, "v", mv)

        m_pt_read = 0
        for ht in range(HT):
            j, cti = ht // 8, ht % 8
            w = [("v", m_pt_read)] if m_pt_read else []
            mt = p.emit("tensor", lambda e, cti=cti: e.transpose(
                ps_t[:], ctx.xbf[:, cti * T + 1: cti * T + 9: 2],
                ctx.ident[:]), waits=tuple(w), inc="t")
            m_pt_read = p.emit("vector", lambda e, ht=ht, j=j: e.tensor_mul(
                e1[:, ht * 128:(ht + 1) * 128], ps_t[:],
                rmask[:, j * 128:(j + 1) * 128]), waits=(("t", mt),), inc="v")

        p.emit("vector", lambda e: e.memset(o2[:], 0.0), inc="v")
        for kt in range(HT):
            pm, cti = kt // 8, kt % 8
            for j in range(4):
                p.emit("vector", lambda e, kt=kt, j=j, pm=pm, cti=cti:
                       e.tensor_copy(
                    o2[:, kt * 16 + j * 4 + pm: kt * 16 + j * 4 + pm + 1],
                    ctx.xbf[:, cti * T + 10 + 2 * j: cti * T + 11 + 2 * j]),
                    inc="v")
        m_o2 = p.val("v")

        for cti in range(CT):
            w = [("v", m_pt_read)]
            mt = p.emit("tensor", lambda e, cti=cti: e.transpose(
                ps_t[:], ctx.xbf[:, cti * T + 9: cti * T + 17: 2],
                ctx.ident[:]), waits=tuple(w), inc="t")
            m_pt_read = p.emit("vector", lambda e, cti=cti: e.tensor_copy(
                ev4[:, cti * 128:(cti + 1) * 128], ps_t[:]),
                waits=(("t", mt),), inc="v")
        m_ev4_dma = p.emit("sync", lambda e: e.dma_start(
            out=ev4_dram, in_=ev4[:]), waits=(("v", m_pt_read),),
            inc="qx", inc_by=16)
        p.emit("vector", lambda e: e.memset(wet[:], 0.0), inc="v")
        m_wet0 = p.val("v")
        wet_dmas = 0
        for otE in range(CT):
            j, half = otE // 2, otE % 2
            src_ap = ev4_dram[j, half * 512:(half + 1) * 512].rearrange(
                "(pp f) -> pp f", f=4)
            p.emit("sync", lambda e, otE=otE, j=j, src_ap=src_ap: e.dma_start(
                out=wet[:, j * 4:(j + 1) * 4], in_=src_ap),
                waits=(("qx", m_ev4_dma), ("v", m_wet0),
                       ("t", p.val("t"))), inc="qx", inc_by=16)
            m_wet = p.val("qx")
            mt = p.emit("tensor", lambda e: e.transpose(
                ps_t2[:], wet[:], ctx.ident[:]),
                waits=(("qx", m_wet),), inc="t")
            mv = p.emit("vector", lambda e, otE=otE: e.tensor_copy(
                we[:, otE * 128:(otE + 1) * 128], ps_t2[:]),
                waits=(("t", mt),), inc="v")
            if otE < CT - 1 and (otE + 1) // 2 != j:
                p.emit("vector", lambda e, j=j: e.memset(
                    wet[:, j * 4:(j + 1) * 4], 0.0), inc="v")
            m_wet0 = p.val("v")
        m_we = p.val("v")

        w1_m = {}
        w1_free = [0, 0]
        w2_m = {}
        w2_free = [0, 0]

        def pf_w1(q, ht):
            sl = ht % 2
            w1_m[(q, ht)] = p.emit("gpsimd", lambda e, ht=ht, sl=sl: e.dma_start(
                out=w1sl[:, sl * 1024:(sl + 1) * 1024], in_=d_w1[ht]),
                waits=(("t", w1_free[sl]),), inc="qw", inc_by=16)

        def pf_w2(q, ot):
            sl = ot % 2
            w2_m[(q, ot)] = p.emit("gpsimd", lambda e, ot=ot, sl=sl: e.dma_start(
                out=w2sl[:, sl * 4096:(sl + 1) * 4096], in_=d_w2[ot]),
                waits=(("t", w2_free[sl]),), inc="qw", inc_by=16)

        for q in range(NTC):
            pf_w1(q, 0)
            pf_w1(q, 1)
            for ht in range(HT):
                sl = ht % 2
                bi, bank, war = rf.take()
                for kt in range(CT):
                    w = list(war) if kt == 0 else []
                    if kt == 0:
                        w.append(("qw", w1_m[(q, ht)]))
                        if q == 0 and ht == 0:
                            w.append(("v", m_we))
                    p.emit("tensor", lambda e, q=q, ht=ht, kt=kt, sl=sl,
                           bank=bank: e.matmul(
                        bank[:], w1sl[:, sl * 1024 + kt * 128:
                                      sl * 1024 + (kt + 1) * 128],
                        ctx.xbf[:, kt * T + q * S5: kt * T + (q + 1) * S5],
                        start=(kt == 0), stop=False), waits=tuple(w))
                p.emit("tensor", lambda e, q=q, ht=ht, bank=bank: e.matmul(
                    bank[:], e1[:, ht * 128:(ht + 1) * 128],
                    c_sb[:, q * S5:(q + 1) * S5], start=False, stop=False))
                mt = p.emit("tensor", lambda e, q=q, ht=ht, bank=bank: e.matmul(
                    bank[:], b1r[0:1, ht * 128:(ht + 1) * 128],
                    ctx.onesF[0:1, 0:S5], start=False, stop=True), inc="t")
                w1_free[sl] = p.val("t")
                if ht + 2 < HT:
                    pf_w1(q, ht + 2)
                ms = p.emit("scalar", lambda e, ht=ht, bank=bank: e.activation(
                    hbuf[:, ht * S5:(ht + 1) * S5], bank[:], AF.Gelu),
                    waits=(("t", mt),), inc="s")
                rf.release(bi, "s", ms)
            m_h = p.val("s")

            bi, bank, war = rf.take()
            for kt in range(HT):
                w = list(war) if kt == 0 else []
                if kt == 0:
                    w += [("v", m_o2), ("s", m_h)]
                mt = p.emit("tensor", lambda e, kt=kt, bank=bank: e.matmul(
                    bank[0:16, :], o2[:, kt * 16:(kt + 1) * 16],
                    hbuf[:, kt * S5:(kt + 1) * S5],
                    start=(kt == 0), stop=(kt == HT - 1)),
                    waits=tuple(w), inc="t" if kt == HT - 1 else None)
            mv_g = p.emit("vector", lambda e, bank=bank: e.tensor_scalar_mul(
                g_sb[:], bank[0:16, :], sc16[:]), waits=(("t", mt),), inc="v")
            rf.release(bi, "v", mv_g)

            pf_w2(q, 0)
            pf_w2(q, 1)

            def w2_emitter(tc, ot, bank, waits, q=q, mv_g=mv_g):
                sl = ot % 2
                for kt in range(HT):
                    w = list(waits) if kt == 0 else []
                    if kt == 0:
                        w.append(("qw", w2_m[(q, ot)]))
                        if ot == 0:
                            w.append(("v", mv_g))
                    p.emit("tensor", lambda e, ot=ot, kt=kt, sl=sl, bank=bank:
                           e.matmul(
                        bank[:], w2sl[:, sl * 4096 + kt * 128:
                                      sl * 4096 + (kt + 1) * 128],
                        hbuf[:, kt * S5:(kt + 1) * S5],
                        start=(kt == 0), stop=False), waits=tuple(w))
                p.emit("tensor", lambda e, ot=ot, bank=bank: e.matmul(
                    bank[:], we[:, ot * 128:(ot + 1) * 128], g_sb[:],
                    start=False, stop=False))
                mt = p.emit("tensor", lambda e, ot=ot, bank=bank: e.matmul(
                    bank[:], b2r[0:1, ot * 128:(ot + 1) * 128],
                    ctx.onesF[0:1, 0:S5], start=False, stop=True), inc="t")
                w2_free[sl] = p.val("t")
                if ot + 2 < CT:
                    pf_w2(q, ot + 2)
                return mt

            ol.chunk(q, w2_emitter)
    ol.end()


def prep_l2_attn(inputs):
    import ml_dtypes
    d = {}
    W_in = np.asarray(inputs["L2_attn_W_in"])
    W_pad = np.zeros((1024, 4224), np.float32)
    W_pad[:, :4192] = W_in
    d["l2_winT"] = lhsT_blocked(W_pad)
    d["l2_woutT"] = lhsT_blocked(np.ascontiguousarray(np.asarray(inputs["L2_attn_W_out"])))
    d["l2_lnw"] = np.ascontiguousarray(np.asarray(inputs["L2_attn_ln_w"]).reshape(8, 128).T)
    d["l2_lnb"] = np.ascontiguousarray(np.asarray(inputs["L2_attn_ln_b"]).reshape(8, 128).T)
    d["l2_dtb"] = np.ascontiguousarray(np.asarray(inputs["L2_attn_dt_bias"]).reshape(32, 1))
    d["l2_alog"] = np.ascontiguousarray(np.asarray(inputs["L2_attn_A_log"]).reshape(32, 1))
    d["l2_dexp"] = np.ascontiguousarray(
        np.repeat(np.asarray(inputs["L2_attn_D"]), 64).reshape(16, 128).T)
    d["l2_rmsw"] = np.ascontiguousarray(np.asarray(inputs["L2_attn_rms_w"]).reshape(16, 128).T)
    t = np.arange(T, dtype=np.float64)
    inv = 1.0 / 10000.0 ** (np.arange(0, 16, 2, dtype=np.float64) / 16.0)
    ang = t[:, None] * inv[None, :]
    d["l2_cos"] = np.ascontiguousarray(
        np.cos(ang).astype(np.float32).reshape(16, 128, 8).transpose(1, 0, 2)
        .reshape(128, 16 * 8)).astype(ml_dtypes.bfloat16)
    d["l2_sin"] = np.ascontiguousarray(
        np.sin(ang).astype(np.float32).reshape(16, 128, 8).transpose(1, 0, 2)
        .reshape(128, 16 * 8)).astype(ml_dtypes.bfloat16)
    return d


def emit_l2_attn(p, ctx):
    import os as _os
    PH = int(_os.environ.get("L2PHASE", "9"))
    S5 = 512
    NH2, DH, DS, NCH = 32, 64, 32, 16
    d_win = p.dram("l2_winT", [33, 128, 1024], F32, kind="ExternalInput")
    d_wout = p.dram("l2_woutT", [8, 128, 2048], F32, kind="ExternalInput")
    d_lnw = p.dram("l2_lnw", [128, 8], F32, kind="ExternalInput")
    d_lnb = p.dram("l2_lnb", [128, 8], F32, kind="ExternalInput")
    d_dtb = p.dram("l2_dtb", [32, 1], F32, kind="ExternalInput")
    d_alog = p.dram("l2_alog", [32, 1], F32, kind="ExternalInput")
    d_dexp = p.dram("l2_dexp", [128, 16], F32, kind="ExternalInput")
    d_rmsw = p.dram("l2_rmsw", [128, 16], F32, kind="ExternalInput")
    d_cos = p.dram("l2_cos", [128, 16 * 8], BF16, kind="ExternalInput")
    d_sin = p.dram("l2_sin", [128, 16 * 8], BF16, kind="ExternalInput")
    z_dram = p.dram(p._n("l2_zd"), [16, 128, T], BF16)
    xs_dram = p.dram(p._n("l2_xsd"), [16, 128, T], BF16)
    cs_dram = p.dram(p._n("l2_csd"), [32, T], F32)

    winsl = p.sb("l2_winsl", [128, 2 * 1024], BF16)
    zxst = p.sb("l2_zxst", [128, 2 * S5], BF16)
    hsc = p.sb("l2_hsc", [128, 2 * S5], BF16)
    bc_sb = p.sb("l2_bc", [64, T], BF16)
    dt_sb = p.sb("l2_dt", [96, T], F32)
    cs_sb = p.sb("l2_cs", [96, T], F32)
    a_sc = p.sb("l2_a", [96, 128], F32)
    sp1 = p.sb("l2_sp1", [96, S5], F32)
    brop = p.sb("l2_brop", [32, T], BF16)
    crop = p.sb("l2_crop", [32, T], BF16)
    bctk = p.sb("l2_bctk", [128, 64], BF16)
    btr = p.sb("l2_btr", [128, NCH * 32], BF16)
    ctr = p.sb("l2_ctr", [128, 32], BF16)
    cstok = p.sb("l2_cstok", [128, NCH * 32], F32)
    dttok = p.sb("l2_dttok", [128, NCH * 32], F32)
    csrep = p.sb("l2_csrep", [128, 16 * 128], F32)
    xsc = p.sb("l2_xsc", [128, NCH * 128], BF16)
    zc = p.sb("l2_zc", [128, NCH * 128], BF16)
    xtc = p.sb("l2_xtc", [128, 2048 + 64], BF16)
    ltf = p.sb("l2_ltf", [128, 128], F32)
    ltb = p.sb("l2_ltb", [128, 128], BF16)
    msc = p.sb("l2_msc", [128, 128], BF16)
    cp_sb = p.sb("l2_cp", [32, 128], BF16)
    bpp = p.sb("l2_bpp", [128, 128], BF16)
    zsil = p.sb("l2_zsil", [128, 128], BF16)
    ptb = p.sb("l2_ptb", [128, 128], BF16)
    ecs_sb = p.sb("l2_ecs", [32, 128], BF16)
    ytmp = p.sb("l2_ytmp", [128, 128], F32)
    s_f = p.sb("l2_sf", [32, NH2 * 64], F32)
    s_b = p.sb("l2_sb", [32, NH2 * 64], BF16)
    dac = p.sb("l2_dac", [32, 1], F32)
    dfac = p.sb("l2_dfac", [128, 1], F32)
    ywin = p.sb("l2_ywin", [128, 16 * S5], BF16)
    woutsl = p.sb("l2_wouts", [128, 2048], BF16)
    lnw = p.sb("l2_lnw_s", [128, 8], F32)
    lnb = p.sb("l2_lnb_s", [128, 8], F32)
    dtb = p.sb("l2_dtb_s", [96, 1], F32)
    alog = p.sb("l2_alog_s", [96, 1], F32)
    acol = p.sb("l2_acol", [96, 1], F32)
    dexp = p.sb("l2_dexp_s", [128, 16], F32)
    rmsw = p.sb("l2_rmsw_s", [128, 16], F32)
    cosb = p.sb("l2_cos_s", [128, 16 * 8], BF16)
    sinb = p.sb("l2_sin_s", [128, 16 * 8], BF16)


    xw = list(ctx.xbf_wait)
    R = ctx.ln_rows
    hlnb = ctx.xstage[:].bitcast(BF16)

    ol = OutLN(p, ctx, "l2a")
    with ExitStack() as pstack:
        PS = lambda nm, shape, dt=F32: pstack.enter_context(
            p.nc.psum_tensor(p._n(nm), shape, dt))
        r_a = Rot([PS("l2a", [128, S5])])
        r_sc2 = Rot([PS("l2sc", [128, 128]), PS("l2sc", [128, 128])])
        ps_tr = PS("l2tr", [128, 128], BF16)
        ps_y = PS("l2y", [128, 128])
        m_tr_read = 0
        tr_sem = "v"

        for dst, srcd in ((lnw, d_lnw), (lnb, d_lnb), (dexp, d_dexp),
                          (rmsw, d_rmsw)):
            p.emit("sync", lambda e, dst=dst, srcd=srcd: e.dma_start(
                out=dst[:], in_=srcd), inc="qx", inc_by=16)
        for dst, srcd in ((dtb, d_dtb), (alog, d_alog)):
            p.emit("sync", lambda e, dst=dst, srcd=srcd: e.dma_start(
                out=dst[64:96, :], in_=srcd), inc="qx", inc_by=16)
        for dst, srcd in ((cosb, d_cos), (sinb, d_sin)):
            p.emit("sync", lambda e, dst=dst, srcd=srcd: e.dma_start(
                out=dst[:], in_=srcd), inc="qx", inc_by=16)
        m_consts = p.val("qx")
        p.emit("scalar", lambda e: e.activation(
            acol[64:96, :], alog[64:96, :], AF.Exp),
            waits=(("qx", m_consts),), inc="s")
        p.emit("vector", lambda e: e.tensor_scalar_mul(
            acol[64:96, :], acol[64:96, :], -1.0),
            waits=(("s", p.val("s")),), inc="v")
        m_acol = p.val("v")
        p.emit("vector", lambda e: e.memset(bpp[:, 32:128], 0.0), inc="v")
        p.emit("vector", lambda e: e.memset(xtc[:], 0.0), inc="v")
        m_pads = p.val("v")

        seq = [(tc, ot) for tc in range(NTC) for ot in range(33)]
        win_m = {}
        win_free = [0, 0]

        def pf_win(i):
            if i >= len(seq):
                return
            tc, ot = seq[i]
            sl = i % 2
            win_m[i] = p.emit("gpsimd", lambda e, ot=ot, sl=sl: e.dma_start(
                out=winsl[:, sl * 1024:(sl + 1) * 1024], in_=d_win[ot]),
                waits=(("t", win_free[sl]),), inc="qw", inc_by=16)

        pf_win(0)
        pf_win(1)

        m_hln = {}
        m_sq_prev = 0
        m_hlnsums_prev = 0
        m_row_prev = 0
        zx_war = [0, 0]
        hsc_war = [0, 0]
        spl_m = {}
        i_seq = 0
        for tc in range(NTC):
            for ct in range(CT):
                w = [] if not (tc == 0 and ct == 0) else list(xw)
                if tc > 0 and ct == 0:
                    w.append(("t", m_hlnsums_prev))
                p.emit("scalar", lambda e, ct=ct, tc=tc: e.activation(
                    ctx.ln_sq[:, ct * S5:(ct + 1) * S5],
                    ctx.xbf[:, ct * T + tc * S5: ct * T + (tc + 1) * S5],
                    AF.Square), waits=tuple(w), inc="s")
            m_sq = p.val("s")
            bi_st, b_st, war_st = r_a.take()
            for which in range(2):
                for ct in range(CT):
                    w = []
                    if which == 0 and ct == 0:
                        w = list(war_st) + [("s", m_sq)] + (xw if tc == 0 else [])
                        if m_row_prev:
                            w.append(("v", m_row_prev))
                    srcap = (ctx.xbf[:, ct * T + tc * S5: ct * T + (tc + 1) * S5]
                             if which == 0 else
                             ctx.ln_sq[:, ct * S5:(ct + 1) * S5])
                    rowbase = 0 if which == 0 else 64
                    p.emit("tensor", lambda e, srcap=srcap, rowbase=rowbase,
                           ct=ct, b=b_st: e.matmul(
                        b[rowbase:rowbase + 1, :], ctx.ones_bf[:], srcap,
                        start=(ct == 0), stop=(ct == CT - 1)),
                        waits=tuple(w), inc="t" if ct == CT - 1 else None)
            m_sums = p.val("t")
            m_hlnsums_prev = m_sums
            p.emit("vector", lambda e, b=b_st: e.tensor_scalar_mul(
                R[:, 2 * S5:3 * S5], b[0:1, :], 1.0 / C),
                waits=(("t", m_sums),), inc="v")
            p.emit("vector", lambda e, b=b_st: e.tensor_scalar_mul(
                R[:, 3 * S5:4 * S5], b[64:65, :], 1.0 / C), inc="v")
            m_row_prev = p.val("v")
            r_a.release(bi_st, "v", m_row_prev)
            p.emit("vector", lambda e: e.tensor_mul(
                R[:, 0:S5], R[:, 2 * S5:3 * S5], R[:, 2 * S5:3 * S5]), inc="v")
            p.emit("vector", lambda e: e.tensor_sub(
                R[:, S5:2 * S5], R[:, 3 * S5:4 * S5], R[:, 0:S5]), inc="v")
            p.emit("vector", lambda e: e.tensor_scalar_add(
                R[:, S5:2 * S5], R[:, S5:2 * S5], 1e-6), inc="v")
            m_var = p.val("v")
            p.emit("scalar", lambda e: e.activation(
                R[:, 3 * S5:4 * S5], R[:, S5:2 * S5], AF.Sqrt),
                waits=(("v", m_var),), inc="s")
            m_sqr = p.val("s")
            p.emit("vector", lambda e: e.reciprocal(
                R[:, 4 * S5:5 * S5], R[:, 3 * S5:4 * S5]),
                waits=(("s", m_sqr),), inc="v")
            p.emit("vector", lambda e: e.tensor_mul(
                R[:, 5 * S5:6 * S5], R[:, 2 * S5:3 * S5], R[:, 4 * S5:5 * S5]),
                inc="v")
            p.emit("vector", lambda e: e.tensor_scalar_mul(
                R[:, 5 * S5:6 * S5], R[:, 5 * S5:6 * S5], -1.0), inc="v")
            m_ab = p.val("v")
            for which, dst in ((4, ctx.ln_arep), (5, ctx.ln_brep)):
                bi, bank, war = r_a.take()
                w = list(war)
                if which == 4:
                    w.append(("v", m_ab))
                mt = p.emit("tensor", lambda e, which=which, bank=bank: e.matmul(
                    bank[:], ctx.onesF[0:1, 0:128],
                    R[:, which * S5:(which + 1) * S5],
                    start=True, stop=True), waits=tuple(w), inc="t")
                mv = p.emit("vector", lambda e, dst=dst, bank=bank: e.tensor_copy(
                    dst[:], bank[:]), waits=(("t", mt),), inc="v")
                r_a.release(bi, "v", mv)
            m_rep = p.val("v")
            for ct in range(CT):
                hs = ct % 2
                w = [("v", m_rep)] if ct == 0 else []
                if hsc_war[hs]:
                    w.append(("v", hsc_war[hs]))
                if tc > 0 and ct == 0:
                    w.append(("t", m_hln.get(tc - 1, 0)))
                p.emit("vector", lambda e, ct=ct, tc=tc, hs=hs: e.tensor_mul(
                    hsc[:, hs * S5:(hs + 1) * S5],
                    ctx.xbf[:, ct * T + tc * S5: ct * T + (tc + 1) * S5],
                    ctx.ln_arep[:]), waits=tuple(w), inc="v")
                p.emit("vector", lambda e, hs=hs: e.tensor_add(
                    hsc[:, hs * S5:(hs + 1) * S5], hsc[:, hs * S5:(hs + 1) * S5],
                    ctx.ln_brep[:]), inc="v")
                mv = p.emit("vector", lambda e, ct=ct, hs=hs: e.tensor_scalar(
                    hlnb[:, ct * S5:(ct + 1) * S5],
                    hsc[:, hs * S5:(hs + 1) * S5],
                    lnw[:, ct:ct + 1], lnb[:, ct:ct + 1], ALU.mult, ALU.add),
                    inc="v")
                hsc_war[hs] = mv
            m_hln_v = p.val("v")

            SUB = _os.environ.get("L2SUB", "full")
            for ot in range(0 if SUB == "hln" else 33):
                sl = i_seq % 2
                mwin = win_m[i_seq]
                if ot < 32:
                    bi, bank, war = r_a.take()
                    for kt in range(CT):
                        w = list(war) if kt == 0 else []
                        if kt == 0:
                            w.append(("qw", mwin))
                            w.append(("v", m_hln_v))
                        mt = p.emit("tensor", lambda e, kt=kt, sl=sl, bank=bank:
                                    e.matmul(
                            bank[:], winsl[:, sl * 1024 + kt * 128:
                                           sl * 1024 + (kt + 1) * 128],
                            hlnb[:, kt * S5:(kt + 1) * S5],
                            start=(kt == 0), stop=(kt == CT - 1)),
                            waits=tuple(w), inc="t" if kt == CT - 1 else None)
                    zx = ot % 2
                    w = [("t", mt)]
                    if zx_war[zx]:
                        w.append(("qx", zx_war[zx]))
                    if ot < 16:
                        ms = p.emit("scalar", lambda e, zx=zx, bank=bank: e.copy(
                            zxst[:, zx * S5:(zx + 1) * S5], bank[:]),
                            waits=tuple(w), inc="s")
                        dtgt = z_dram
                        kt_ = ot
                    else:
                        ms = p.emit("scalar", lambda e, zx=zx, bank=bank:
                                    e.activation(
                            zxst[:, zx * S5:(zx + 1) * S5], bank[:], AF.Silu),
                            waits=tuple(w), inc="s")
                        dtgt = xs_dram
                        kt_ = ot - 16
                    r_a.release(bi, "s", ms)
                    zx_war[zx] = p.emit("sync", lambda e, kt_=kt_, tc=tc, zx=zx,
                                        dtgt=dtgt: e.dma_start(
                        out=dtgt[kt_][:, tc * S5:(tc + 1) * S5],
                        in_=zxst[:, zx * S5:(zx + 1) * S5]),
                        waits=(("s", ms),), inc="qx", inc_by=16)
                    spl_m[(tc, ot)] = zx_war[zx]
                elif SUB != "zx":
                    bi, bank, war = r_a.take()
                    for kt in range(CT):
                        w = list(war) if kt == 0 else []
                        if kt == 0:
                            w += [("qw", mwin), ("v", m_hln_v)]
                        mt1 = p.emit("tensor", lambda e, kt=kt, sl=sl, bank=bank:
                                     e.matmul(
                            bank[:], winsl[:, sl * 1024 + kt * 128:
                                           sl * 1024 + (kt + 1) * 128],
                            hlnb[:, kt * S5:(kt + 1) * S5],
                            start=(kt == 0), stop=(kt == CT - 1)),
                            waits=tuple(w), inc="t" if kt == CT - 1 else None)
                    ms1 = p.emit("scalar", lambda e, tc=tc, bank=bank:
                                 e.activation(
                        bc_sb[:, tc * S5:(tc + 1) * S5], bank[0:64, :],
                        AF.Silu), waits=(("t", mt1),), inc="s")
                    w = [("t", mt1), ("qx", m_consts)]
                    p.emit("vector", lambda e, bank=bank: e.tensor_scalar(
                        sp1[64:96, :], bank[64:96, :],
                        dtb[64:96, :], 0.0, ALU.add, ALU.max),
                        waits=tuple(w), inc="v")
                    m_relu = p.val("v")
                    p.emit("scalar", lambda e, tc=tc, bank=bank: e.activation(
                        dt_sb[64:96, tc * S5:(tc + 1) * S5], bank[64:96, :],
                        AF.Abs, bias=dtb[64:96, :]),
                        waits=(("t", mt1), ("qx", m_consts)), inc="s")
                    p.emit("scalar", lambda e, tc=tc: e.activation(
                        dt_sb[64:96, tc * S5:(tc + 1) * S5],
                        dt_sb[64:96, tc * S5:(tc + 1) * S5], AF.Exp,
                        scale=-1.0), inc="s")
                    ms2 = p.emit("scalar", lambda e, tc=tc: e.activation(
                        dt_sb[64:96, tc * S5:(tc + 1) * S5],
                        dt_sb[64:96, tc * S5:(tc + 1) * S5], AF.Ln, bias=1.0),
                        inc="s")
                    r_a.release(bi, "s", ms2)
                    mv = p.emit("vector", lambda e, tc=tc: e.tensor_add(
                        dt_sb[64:96, tc * S5:(tc + 1) * S5],
                        dt_sb[64:96, tc * S5:(tc + 1) * S5],
                        sp1[64:96, :]),
                        waits=(("s", ms2), ("v", m_relu)), inc="v")
                win_free[sl] = p.val("t")
                pf_win(i_seq + 2)
                i_seq += 1
            m_hln[tc] = p.val("t")
        m_dt_v = p.val("v")
        m_bc_s = p.val("s")

        if PH < 2:
            return
        m_brop = {}
        for tt in range(NCH):
            bk = 0
            w = [("s", m_bc_s), ("qx", m_consts)]
            if m_tr_read:
                w.append((tr_sem, m_tr_read))
            mt = p.emit("tensor", lambda e, tt=tt: e.transpose(
                ps_tr[:, 0:64], bc_sb[:, tt * 128:(tt + 1) * 128],
                ctx.ident[0:64, 0:64]), waits=tuple(w), inc="t")
            mv = p.emit("vector", lambda e, bk=bk: e.tensor_copy(
                bctk[:, bk * 64:(bk + 1) * 64], ps_tr[:, 0:64]),
                waits=(("t", mt),), inc="v")
            m_tr_read, tr_sem = mv, "v"
            for half, (dst8, dst_sb) in enumerate(((btr, brop), (ctr, crop))):
                base = bk * 64 + half * 32
                dcol = (tt * 32) if half == 0 else 0
                cs_ = slice(tt * 8, (tt + 1) * 8)
                p.emit("vector", lambda e, base=base, dcol=dcol, cs_=cs_,
                       dst8=dst8: e.tensor_mul(
                    dst8[:, dcol:dcol + 16:2], bctk[:, base:base + 16:2],
                    cosb[:, cs_]), inc="v")
                p.emit("vector", lambda e, base=base, cs_=cs_: e.tensor_mul(
                    ytmp[:, 0:8], bctk[:, base + 1:base + 16:2], sinb[:, cs_]),
                    waits=(("v", p.val("v")),), inc="v")
                p.emit("vector", lambda e, dcol=dcol, dst8=dst8: e.tensor_sub(
                    dst8[:, dcol:dcol + 16:2], dst8[:, dcol:dcol + 16:2],
                    ytmp[:, 0:8]), waits=(("v", p.val("v")),), inc="v")
                p.emit("vector", lambda e, base=base, dcol=dcol, cs_=cs_,
                       dst8=dst8: e.tensor_mul(
                    dst8[:, dcol + 1:dcol + 16:2], bctk[:, base:base + 16:2],
                    sinb[:, cs_]), waits=(("v", p.val("v")),), inc="v")
                p.emit("vector", lambda e, base=base, cs_=cs_: e.tensor_mul(
                    ytmp[:, 0:8], bctk[:, base + 1:base + 16:2], cosb[:, cs_]),
                    waits=(("v", p.val("v")),), inc="v")
                p.emit("vector", lambda e, dcol=dcol, dst8=dst8: e.tensor_add(
                    dst8[:, dcol + 1:dcol + 16:2], dst8[:, dcol + 1:dcol + 16:2],
                    ytmp[:, 0:8]), waits=(("v", p.val("v")),), inc="v")
                p.emit("vector", lambda e, base=base, dcol=dcol, dst8=dst8:
                       e.tensor_copy(
                    dst8[:, dcol + 16:dcol + 32], bctk[:, base + 16:base + 32]),
                    waits=(("v", p.val("v")),), inc="v")
                m_rot = p.val("v")
                w = [("v", m_rot)]
                if m_tr_read:
                    w.append((tr_sem, m_tr_read))
                srcap_col = (tt * 32) if half == 0 else 0
                src_t = dst8
                mt2 = p.emit("tensor", lambda e, src_t=src_t, c0=srcap_col:
                             e.transpose(
                    ps_tr[0:32, 0:128], src_t[:, c0:c0 + 32], ctx.ident[:]),
                    waits=tuple(w), inc="t")
                mv2 = p.emit("vector", lambda e, tt=tt, dst_sb=dst_sb:
                             e.tensor_copy(
                    dst_sb[:, tt * 128:(tt + 1) * 128], ps_tr[0:32, 0:128]),
                    waits=(("t", mt2),), inc="v")
                m_tr_read, tr_sem = mv2, "v"
                m_brop[(tt, half)] = mv2
        m_rope_v = p.val("v")

        if PH < 3:
            return
        for c in range(NCH):
            w = [("v", m_dt_v), ("v", m_acol)] if c == 0 else []
            p.emit("vector", lambda e, c=c: e.tensor_scalar_mul(
                a_sc[64:96, :], dt_sb[64:96, c * 128:(c + 1) * 128],
                acol[64:96, :]), waits=tuple(w), inc="v")
            p.emit("vector", lambda e, c=c: e.tensor_tensor_scan(
                cs_sb[64:96, c * 128:(c + 1) * 128], ctx.onesF[64:96, 0:128],
                a_sc[64:96, :], 0.0, ALU.mult, ALU.add), inc="v")
        m_cs_v = p.val("v")
        m_cs_dma = p.emit("sync", lambda e: e.dma_start(
            out=cs_dram, in_=cs_sb[64:96, :]), waits=(("v", m_cs_v),),
            inc="qx", inc_by=16)
        for tt in range(NCH):
            for srcb, dstb in ((cs_sb, cstok), (dt_sb, dttok)):
                bi, bank, war = r_sc2.take()
                w = [("v", m_cs_v)] + list(war)
                mt = p.emit("tensor", lambda e, tt=tt, srcb=srcb, bank=bank:
                            e.transpose(
                    bank[:, 0:32], srcb[64:96, tt * 128:(tt + 1) * 128],
                    ctx.identF[64:96, 64:96]), waits=tuple(w), inc="t")
                mv = p.emit("vector", lambda e, tt=tt, dstb=dstb, bank=bank:
                            e.tensor_copy(
                    dstb[:, tt * 32:(tt + 1) * 32], bank[:, 0:32]),
                    waits=(("t", mt),), inc="v")
                r_sc2.release(bi, "v", mv)
        m_tok_v = p.val("v")
        m_ncs_v = m_tok_v

        if PH < 4:
            return
        p.emit("vector", lambda e: e.memset(s_f[:], 0.0), inc="v")
        p.emit("vector", lambda e: e.memset(s_b[:], 0.0), inc="v")
        m_sbf_v = p.val("v")

        wout_m = {}
        wout_free = [0, 0]

        def pf_wout(i):
            tc, ot = i // 8, i % 8
            if i >= 32:
                return
            wout_m[i] = p.emit("gpsimd", lambda e, ot=ot: e.dma_start(
                out=woutsl[:], in_=d_wout[ot]),
                waits=(("t", wout_free[0]),), inc="qw", inc_by=16)

        xsc_m = [0, 0]
        zc_m = [0, 0]
        m_spills_done = spl_m[(NTC - 1, 31)]

        def load_chunk(c):
            for dram_t, dst, mlist in ((xs_dram, xsc, xsc_m),
                                       (z_dram, zc, zc_m)):
                srcap = dram_t.rearrange("k p t -> p k t")[
                    :, :, c * 128:(c + 1) * 128]
                w = [("qx", m_spills_done), ("t", p.val("t")),
                     ("v", p.val("v")), ("s", p.val("s"))]
                mlist[0] = p.emit("sync", lambda e, dst=dst,
                                  srcap=srcap: e.dma_start(
                    out=dst[:].rearrange("p (k t) -> p k t", t=128),
                    in_=srcap),
                    waits=tuple(w), inc="qx", inc_by=16)

        m_msc_read = 0
        m_ltf_read = 0
        m_ltb_read = 0
        m_pt_read = 0
        m_csrep_users_s = 0
        m_csrep_users_v = 0
        m_ecs_read = 0
        m_cp_read = 0
        m_dac_read = 0
        m_dfac_read = 0
        m_bpp_read = 0
        m_psy_stt = 0
        m_ytmp_read = 0
        m_zsil_read = 0
        m_yread_t = 0
        m_csrep = 0

        for tc in range(NTC):
            for cc4 in range(4):
                c = tc * 4 + cc4
                sl = 0
                load_chunk(c)
                bi_sc, b_sc, war_sc = r_sc2.take()
                w = list(war_sc) + [("v", m_rope_v)]
                if c == 0:
                    w.append(("v", m_sbf_v))
                    w.append(("v", m_pads))
                mt_sc = p.emit("tensor", lambda e, c=c, b=b_sc: e.matmul(
                    b[:, 0:128], brop[:, c * 128:(c + 1) * 128],
                    crop[:, c * 128:(c + 1) * 128], start=True, stop=True),
                    waits=tuple(w), inc="t")
                w = [("t", mt_sc)]
                if m_msc_read:
                    w.append(("t", m_msc_read))
                mv_msc = p.emit("vector", lambda e, b=b_sc: e.tensor_mul(
                    msc[:], b[:, 0:128], ctx.maskA[:]), waits=tuple(w), inc="v")
                r_sc2.release(bi_sc, "v", mv_msc)

                for half in range(2):
                    w = [("qx", m_cs_dma)]
                    if m_csrep_users_s:
                        w += [("s", m_csrep_users_s), ("v", m_csrep_users_v)]
                    base_ap = cs_dram[half * 16:(half + 1) * 16,
                                      c * 128:(c + 1) * 128]
                    rep_ap = base_ap.unsqueeze(0).partition_broadcast(128)
                    m_csrep = p.emit("sync", lambda e, rep_ap=rep_ap:
                                     e.dma_start(
                        out=csrep[:].rearrange("p (h t) -> p h t", t=128),
                        in_=rep_ap), waits=tuple(w), inc="qx", inc_by=16)
                    for hh in range(16):
                        h = half * 16 + hh
                        hb = (h % 2) * 64
                        kt_x = h // 2
                        xslc = slice(kt_x * 128, (kt_x + 1) * 128)
                        w = [("qx", xsc_m[0])]
                        if m_tr_read:
                            w.append((tr_sem, m_tr_read))
                        mt_x = p.emit("tensor", lambda e, hb=hb, xslc=xslc:
                                      e.transpose(
                            ps_tr[:, 0:64], xsc[hb:hb + 64, xslc],
                            ctx.ident[hb:hb + 64, hb:hb + 64]),
                            waits=tuple(w), inc="t")
                        mv_x = p.emit("vector", lambda e, h=h, c=c:
                                      e.tensor_scalar_mul(
                            xtc[:, h * 64:(h + 1) * 64],
                            ps_tr[:, 0:64],
                            dttok[:, c * 32 + h:c * 32 + h + 1]),
                            waits=(("t", mt_x), ("v", m_tok_v)), inc="v")
                        m_tr_read, tr_sem = mv_x, "v"
                        w = [("qx", m_csrep), ("v", m_ncs_v)]
                        if m_ltf_read:
                            w.append(("s", m_ltf_read))
                        mv_ltf = p.emit("vector", lambda e, hh=hh, c=c, h=h:
                                        e.tensor_scalar(
                            ltf[:], csrep[:, hh * 128:(hh + 1) * 128],
                            cstok[:, c * 32 + h:c * 32 + h + 1], 0.0,
                            ALU.subtract, ALU.min), waits=tuple(w), inc="v")
                        w = [("v", mv_ltf)]
                        if m_ltb_read:
                            w.append(("v", m_ltb_read))
                        m_ltf_read = p.emit("scalar", lambda e: e.activation(
                            ltb[:], ltf[:], AF.Exp), waits=tuple(w), inc="s")
                        w = [("s", m_ltf_read), ("v", mv_msc)]
                        if m_pt_read:
                            w.append(("t", m_pt_read))
                        m_ltb_read = p.emit("vector", lambda e: e.tensor_mul(
                            ptb[:], msc[:], ltb[:]), waits=tuple(w), inc="v")
                        mv_pt = m_ltb_read
                        w = [("qx", m_csrep)]
                        if m_ecs_read:
                            w.append(("v", m_ecs_read))
                        ms_ecs = p.emit("scalar", lambda e, hh=hh: e.activation(
                            ecs_sb[:], csrep[0:32, hh * 128:(hh + 1) * 128],
                            AF.Exp), waits=tuple(w), inc="s")
                        w = [("s", ms_ecs), ("v", m_rope_v)]
                        if m_cp_read:
                            w.append(("t", m_cp_read))
                        m_ecs_read = p.emit("vector", lambda e, c=c: e.tensor_mul(
                            cp_sb[:], crop[:, c * 128:(c + 1) * 128], ecs_sb[:]),
                            waits=tuple(w), inc="v")
                        mv_cp = m_ecs_read
                        w = [("qx", m_csrep)]
                        if m_dac_read:
                            w.append(("v", m_dac_read))
                        ms_dac = p.emit("scalar", lambda e, hh=hh: e.activation(
                            dac[:], csrep[0:32, hh * 128 + 127:hh * 128 + 128],
                            AF.Exp), waits=tuple(w), inc="s")
                        w = [("qx", m_csrep), ("v", m_ncs_v)]
                        if m_dfac_read:
                            w.append(("v", m_dfac_read))
                        ms_dfac = p.emit("scalar", lambda e, hh=hh, c=c, h=h:
                                         e.activation(
                            dfac[:], cstok[:, c * 32 + h:c * 32 + h + 1],
                            AF.Exp, scale=-1.0,
                            bias=csrep[:, hh * 128 + 127:hh * 128 + 128]),
                            waits=tuple(w), inc="s")
                        w = [("s", ms_dfac), ("v", m_rope_v)]
                        if m_bpp_read:
                            w.append(("t", m_bpp_read))
                        mv_bpp = p.emit("vector", lambda e, c=c: e.tensor_scalar_mul(
                            bpp[:, 0:32], btr[:, c * 32:(c + 1) * 32], dfac[:]),
                            waits=tuple(w), inc="v")
                        m_dfac_read = mv_bpp
                        w = [("v", mv_pt), ("v", mv_x)]
                        if m_psy_stt:
                            w.append(("v", m_psy_stt))
                        p.emit("tensor", lambda e, h=h, hb=hb: e.matmul(
                            ps_y[hb:hb + 64, :],
                            xtc[:, h * 64:(h + 1) * 64],
                            ptb[:], start=True, stop=False), waits=tuple(w))
                        m_cp_read = p.emit("tensor", lambda e, h=h, hb=hb:
                                           e.matmul(
                            ps_y[hb:hb + 64, :],
                            s_b[:, h * 64:(h + 1) * 64], cp_sb[:],
                            start=False, stop=True),
                            waits=(("v", mv_cp), ("v", m_sbf_v)), inc="t")
                        mt_y = m_cp_read
                        bi_s, b_si, war_si = r_sc2.take()
                        w = list(war_si) + [("v", mv_bpp), ("v", mv_x)]
                        if c == 0 and h == 0:
                            w.append(("v", m_pads))
                        m_bpp_read = p.emit("tensor", lambda e, h=h,
                                            b=b_si: e.matmul(
                            b[:, 0:128], bpp[:],
                            xtc[:, h * 64: h * 64 + 128],
                            start=True, stop=True), waits=tuple(w), inc="t")
                        mt_si = m_bpp_read
                        m_pt_read = mt_si
                        w = [("t", mt_si), ("s", ms_dac)]
                        mv_su = p.emit("vector", lambda e, h=h, b=b_si:
                                       e.scalar_tensor_tensor(
                            s_f[:, h * 64:(h + 1) * 64],
                            s_f[:, h * 64:(h + 1) * 64], dac[:],
                            b[0:32, 0:64], ALU.mult, ALU.add),
                            waits=tuple(w), inc="v")
                        r_sc2.release(bi_s, "v", mv_su)
                        m_dac_read = mv_su
                        m_sbf_v = p.emit("vector", lambda e, h=h: e.tensor_copy(
                            s_b[:, h * 64:(h + 1) * 64],
                            s_f[:, h * 64:(h + 1) * 64]), inc="v")
                        w = [("qx", zc_m[0])]
                        if m_zsil_read:
                            w.append(("v", m_zsil_read))
                        ms_z = p.emit("scalar", lambda e, hb=hb, xslc=xslc:
                                      e.activation(
                            zsil[hb:hb + 64, :], zc[hb:hb + 64, xslc], AF.Silu),
                            waits=tuple(w), inc="s")
                        w = [("t", mt_y), ("qx", xsc_m[0]), ("qx", m_consts)]
                        if m_ytmp_read:
                            w.append(("v", m_ytmp_read))
                        m_psy_stt = p.emit("vector", lambda e, hb=hb, xslc=xslc,
                                           kt_x=kt_x: e.scalar_tensor_tensor(
                            ytmp[hb:hb + 64, 0:128], xsc[hb:hb + 64, xslc],
                            dexp[hb:hb + 64, kt_x:kt_x + 1],
                            ps_y[hb:hb + 64, :], ALU.mult, ALU.add),
                            waits=tuple(w), inc="v")
                        if c == 0 and h == 0:
                            import os as _os2
                            if _os2.environ.get("L2DBG"):
                                dbg = p.dram("l2dbg", [128, 5 * 128], F32,
                                             kind="ExternalOutput")
                                for col, (buf, mm) in enumerate((
                                        (ltb, ("s", m_ltf_read)),
                                        (ptb, ("v", mv_pt)),
                                        (msc, ("v", mv_msc)),
                                        (xtc, ("v", mv_x)),
                                        (ltf, ("v", mv_ltf)))):
                                    p.emit("gpsimd", lambda e, col=col, buf=buf:
                                           e.dma_start(
                                        out=dbg[:, col * 128:(col + 1) * 128],
                                        in_=buf[:, 0:128]),
                                        waits=(mm, ("v", m_psy_stt)),
                                        inc="qw", inc_by=16)
                        w = [("v", m_psy_stt), ("s", ms_z)]
                        if tc > 0 and c % 4 == 0 and half == 0 and hh == 0:
                            w.append(("t", m_yread_t))
                        m_ytmp_read = p.emit("vector", lambda e, hb=hb, kt_x=kt_x,
                                             cc4=cc4: e.tensor_mul(
                            ywin[hb:hb + 64,
                                 kt_x * S5 + cc4 * 128: kt_x * S5 + (cc4 + 1) * 128],
                            ytmp[hb:hb + 64, 0:128], zsil[hb:hb + 64, :]),
                            waits=tuple(w), inc="v")
                        m_zsil_read = m_ytmp_read
                    m_csrep_users_s = p.val("s")
                    m_csrep_users_v = p.val("v")
            m_ywin_v = p.val("v")
            if PH < 5:
                continue

            for kt in range(NCH):
                w = [("v", m_ywin_v)] if kt == 0 else []
                if tc > 0 and kt == 0:
                    w.append(("t", m_sq_prev))
                p.emit("scalar", lambda e, kt=kt: e.activation(
                    ctx.ln_sq[:, (kt % 8) * S5:(kt % 8 + 1) * S5]
                    if False else
                    ctx.ln_sq[:, (kt % CT) * S5:(kt % CT + 1) * S5],
                    ywin[:, kt * S5:(kt + 1) * S5], AF.Square),
                    waits=tuple(w), inc="s") if False else None
            bi_r, b_r, war_r = r_a.take()
            first = True
            for half8 in range(2):
                for k8 in range(8):
                    kt = half8 * 8 + k8
                    w = [("v", m_ywin_v)] if first else []
                    if tc > 0 and first:
                        w.append(("t", m_sq_prev))
                    if half8 == 1 and k8 == 0:
                        w.append(("t", p.val("t")))
                    first = False
                    p.emit("scalar", lambda e, kt=kt, k8=k8: e.activation(
                        ctx.ln_sq[:, k8 * S5:(k8 + 1) * S5],
                        ywin[:, kt * S5:(kt + 1) * S5], AF.Square),
                        inc="s", waits=tuple(w))
                m_sqy = p.val("s")
                rowb = half8 * 64
                for k8 in range(8):
                    w = []
                    if half8 == 0 and k8 == 0:
                        w = list(war_r) + [("s", m_sqy)]
                    elif k8 == 0:
                        w = [("s", m_sqy)]
                    p.emit("tensor", lambda e, k8=k8, rowb=rowb, b=b_r: e.matmul(
                        b[rowb:rowb + 1, :], ctx.ones_bf[:],
                        ctx.ln_sq[:, k8 * S5:(k8 + 1) * S5],
                        start=(k8 == 0), stop=(k8 == 7)),
                        waits=tuple(w), inc="t" if k8 == 7 else None)
            m_sq_prev = p.val("t")
            p.emit("vector", lambda e, b=b_r: e.tensor_scalar_mul(
                R[:, 0:S5], b[0:1, :], 1.0 / 2048.0),
                waits=(("t", m_sq_prev),), inc="v")
            p.emit("vector", lambda e, b=b_r: e.scalar_tensor_tensor(
                R[:, 0:S5], b[64:65, :], 1.0 / 2048.0, R[:, 0:S5],
                ALU.mult, ALU.add), inc="v")
            mvx = p.val("v")
            r_a.release(bi_r, "v", mvx)
            p.emit("vector", lambda e: e.tensor_scalar_add(
                R[:, 0:S5], R[:, 0:S5], 1e-6), inc="v")
            m_r1 = p.val("v")
            p.emit("scalar", lambda e: e.activation(
                R[:, S5:2 * S5], R[:, 0:S5], AF.Sqrt),
                waits=(("v", m_r1),), inc="s")
            m_r2 = p.val("s")
            p.emit("vector", lambda e: e.reciprocal(
                R[:, 2 * S5:3 * S5], R[:, S5:2 * S5]),
                waits=(("s", m_r2),), inc="v")
            m_r3 = p.val("v")
            bi_r, b_r, war_r = r_a.take()
            mt_rep = p.emit("tensor", lambda e, b=b_r: e.matmul(
                b[:], ctx.onesF[0:1, 0:128], R[:, 2 * S5:3 * S5],
                start=True, stop=True),
                waits=tuple(list(war_r) + [("v", m_r3)]), inc="t")
            mv_rep = p.emit("vector", lambda e, b=b_r: e.tensor_copy(
                ctx.ln_arep[:], b[:]), waits=(("t", mt_rep),), inc="v")
            r_a.release(bi_r, "v", mv_rep)
            hs_war = [0, 0]
            for kt in range(NCH):
                hs = kt % 2
                w = [("v", mv_rep)] if kt == 0 else []
                if hs_war[hs]:
                    w.append(("v", hs_war[hs]))
                p.emit("vector", lambda e, kt=kt, hs=hs: e.tensor_mul(
                    hsc[:, hs * S5:(hs + 1) * S5], ywin[:, kt * S5:(kt + 1) * S5],
                    ctx.ln_arep[:]), waits=tuple(w), inc="v")
                mvk = p.emit("vector", lambda e, kt=kt, hs=hs: e.tensor_scalar_mul(
                    ywin[:, kt * S5:(kt + 1) * S5], hsc[:, hs * S5:(hs + 1) * S5],
                    rmsw[:, kt:kt + 1]), inc="v")
                hs_war[hs] = mvk
            m_yfin_v = p.val("v")

            def wout_emitter(tci, ot, bank, waits, tc=tc, m_yfin_v=m_yfin_v):
                i = tc * 8 + ot
                pf_wout(i)
                mt = None
                for kt in range(NCH):
                    w = list(waits) if kt == 0 else []
                    if kt == 0:
                        w.append(("qw", wout_m[i]))
                        if ot == 0:
                            w.append(("v", m_yfin_v))
                    mt = p.emit("tensor", lambda e, ot=ot, kt=kt,
                                bank=bank: e.matmul(
                        bank[:], woutsl[:, kt * 128:(kt + 1) * 128],
                        ywin[:, kt * S5:(kt + 1) * S5],
                        start=(kt == 0), stop=(kt == NCH - 1)),
                        waits=tuple(w), inc="t" if kt == NCH - 1 else None)
                wout_free[0] = p.val("t")
                return mt

            if tc == 0:
                ol.stage_extra = [("t", m_hln[NTC - 1])]
                if _os.environ.get("L2DBG2"):
                    dbg2 = p.dram("l2dbg2", [128, 16384], F32,
                                  kind="ExternalOutput")
                    for col, (buf, r0, r1, nw, mm) in enumerate((
                            (dt_sb, 64, 96, T, ("v", m_dt_v)),
                            (cs_sb, 64, 96, T, ("v", m_cs_v)),
                            (brop, 0, 32, T, ("v", m_rope_v)),
                            (crop, 0, 32, T, ("v", m_rope_v)),
                            (ywin, 0, 128, 16 * 512, ("v", m_yfin_v)))):
                        p.emit("gpsimd", lambda e, col=col, buf=buf, r0=r0,
                               r1=r1, nw=nw: e.dma_start(
                            out=dbg2[r0:r1, col * 2048:col * 2048 + nw]
                            if nw == T else dbg2[r0:r1, 8192:16384],
                            in_=buf[r0:r1, 0:nw]), waits=(mm,),
                            inc="qw", inc_by=16)
            ol.chunk(tc, wout_emitter, stage_mult=2.0)
            m_yread_t = p.val("t")
    if PH >= 5:
        ol.end()


def prep_l2_ffn(inputs):
    d = {}
    d["l2f_wq"] = lhsT_blocked(np.ascontiguousarray(np.asarray(inputs["L2_ffn_Wq"])))
    d["l2f_wk"] = lhsT_blocked(np.ascontiguousarray(np.asarray(inputs["L2_ffn_Wk"])))
    g = np.zeros((1024, 512), np.float32)
    g[:, 0:40] = np.asarray(inputs["L2_ffn_Ws"])
    g[:, 40:80] = np.asarray(inputs["L2_ffn_Wd"])
    d["l2f_wg"] = rhs_blocked(g)
    wv = np.asarray(inputs["L2_ffn_Wv"]).transpose(1, 0, 2).reshape(1024, 320)
    wvp = np.zeros((1024, 512), np.float32)
    wvp[:, :320] = wv
    d["l2f_wv"] = rhs_blocked(wvp)
    wo = np.asarray(inputs["L2_ffn_Wo"]).reshape(320, 1024)
    wop = np.zeros((384, 1024), np.float32)
    wop[:320] = wo
    d["l2f_wo"] = lhsT_blocked(wop)
    return d


def emit_l2_ffn(p, ctx):
    S5 = 512
    NH, DH, NE = 8, 64, 5
    d_wq = p.dram("l2f_wq", [4, 128, 8 * 128], F32, kind="ExternalInput")
    d_wk = p.dram("l2f_wk", [4, 128, 8 * 128], F32, kind="ExternalInput")
    d_wg = p.dram("l2f_wg", [8, 128, 512], F32, kind="ExternalInput")
    d_wv = p.dram("l2f_wv", [8, 128, 512], F32, kind="ExternalInput")
    d_wo = p.dram("l2f_wo", [8, 128, 3 * 128], F32, kind="ExternalInput")

    qT = p.sb("l2f_qT", [128, 4 * T], BF16)
    kT = p.sb("l2f_kT", [128, 4 * T], BF16)
    att_sc = p.sb("l2f_attsc", [128, S5], BF16)
    atok = p.sb("l2f_atok", [128, 4 * 512], BF16)
    v8 = p.sb("l2f_v8", [128, NTT * (NH * 65)], BF16)
    gl = p.sb("l2f_gl", [128, 2 * 80], F32)
    sig = p.sb("l2f_sig", [128, 40], F32)
    glf = p.sb("l2f_glf", [128, 40], F32)
    msk1 = p.sb("l2f_msk1", [128, 40], F32)
    msk2 = p.sb("l2f_msk2", [128, 40], F32)
    m1 = p.sb("l2f_m1", [128, 16], F32)
    maskv = p.sb("l2f_maskv", [128, NTT * 80], F32)
    xv_sb = p.sb("l2f_xv", [128, 2 * 320], BF16)
    se_sb = p.sb("l2f_se", [128, 320], BF16)
    seT = p.sb("l2f_seT", [128, 3 * T], BF16)
    pT = p.sb("l2f_pT", [128, 2 * S5], BF16)
    zrep = p.sb("l2f_zrep", [128, S5], F32)
    zr = p.sb("l2f_zr", [1, S5], F32)
    wgsl = p.sb("l2f_wgsl", [128, 8 * 512], BF16)
    wsl = wgsl
    wosl = p.sb("l2f_wosl", [128, 2 * 384], BF16)

    xw = list(ctx.xbf_wait)

    with ExitStack() as pstack:
        PS = lambda nm, shape, dt=F32: pstack.enter_context(
            p.nc.psum_tensor(p._n(nm), shape, dt))
        r_st = Rot([PS("l2fs", [128, S5]) for _ in range(2)])
        r_o = Rot([PS("l2fo", [128, S5]) for _ in range(2)])
        r_z = Rot([PS("l2fz", [128, S5])])
        ps_tr = PS("l2ftr", [128, 128], BF16)
        r_a = Rot([PS("l2fa", [128, S5]) for _ in range(2)])

        m_wg = p.emit("gpsimd", lambda e: e.dma_start(
            out=wgsl[:].rearrange("p (k f) -> p k f", k=8),
            in_=d_wg.rearrange("k p f -> p k f")), inc="qw", inc_by=16)
        wq_m = {}
        wq_free = [0, 0]

        def pf_w(i, dsrc):
            sl = i % 2
            ot = i % 4
            wq_m[i] = p.emit("gpsimd", lambda e, ot=ot, sl=sl, dsrc=dsrc:
                             e.dma_start(
                out=wsl[:, sl * 1024:(sl + 1) * 1024], in_=dsrc[ot]),
                waits=(("t", wq_free[sl]),), inc="qw", inc_by=16)

        gl_war = [0, 0]
        for tt in range(NTT):
            bi, bank, war = r_a.take()
            for kt in range(CT):
                w = list(war) if kt == 0 else []
                if tt == 0 and kt == 0:
                    w.append(("qw", m_wg))
                    w += xw
                mt = p.emit("tensor", lambda e, tt=tt, kt=kt, bank=bank:
                            e.matmul(
                    bank[:],
                    ctx.xbf[:, kt * T + tt * 128: kt * T + (tt + 1) * 128],
                    wgsl[:, kt * 512:(kt + 1) * 512],
                    start=(kt == 0), stop=(kt == CT - 1)),
                    waits=tuple(w), inc="t" if kt == CT - 1 else None)
            sl = tt % 2
            w = [("t", mt)]
            if gl_war[sl]:
                w.append(("v", gl_war[sl]))
            mv_gl = p.emit("vector", lambda e, sl=sl, bank=bank: e.tensor_copy(
                gl[:, sl * 80:(sl + 1) * 80], bank[:, 0:80]),
                waits=tuple(w), inc="v")
            r_a.release(bi, "v", mv_gl)
            for half in range(2):
                c0g = sl * 80 + half * 40
                c0 = tt * 80 + half * 40
                p.emit("vector", lambda e, c0g=c0g: e.tensor_copy(
                    glf[:], gl[:, c0g:c0g + 40]),
                    waits=(("v", p.val("v")),), inc="v")
                if half == 0:
                    p.emit("scalar", lambda e, c0g=c0g: e.activation(
                        sig[:], gl[:, c0g:c0g + 40], AF.Sigmoid),
                        waits=(("v", mv_gl), ("v", p.val("v"))), inc="s")
                    m_sig = p.val("s")
                p.emit("vector", lambda e: e.reduce_max(
                    m1[:, 0:8].rearrange("p (a b) -> p a b", b=1),
                    glf[:].rearrange("p (h e) -> p h e", e=5),
                    axis=mybir.AxisListType.X),
                    waits=(("v", p.val("v")),), inc="v")
                for h in range(NH):
                    p.emit("vector", lambda e, h=h: e.tensor_single_scalar(
                        msk1[:, h * 5:(h + 1) * 5], glf[:, h * 5:(h + 1) * 5],
                        m1[:, h:h + 1], ALU.is_ge),
                        waits=(("v", p.val("v")),), inc="v")
                p.emit("vector", lambda e: e.scalar_tensor_tensor(
                    glf[:], msk1[:], -65000.0, glf[:], ALU.mult, ALU.add),
                    waits=(("v", p.val("v")),), inc="v")
                p.emit("vector", lambda e: e.reduce_max(
                    m1[:, 8:16].rearrange("p (a b) -> p a b", b=1),
                    glf[:].rearrange("p (h e) -> p h e", e=5),
                    axis=mybir.AxisListType.X),
                    waits=(("v", p.val("v")),), inc="v")
                for h in range(NH):
                    p.emit("vector", lambda e, h=h: e.tensor_single_scalar(
                        msk2[:, h * 5:(h + 1) * 5], glf[:, h * 5:(h + 1) * 5],
                        m1[:, 8 + h:9 + h], ALU.is_ge),
                        waits=(("v", p.val("v")),), inc="v")
                p.emit("vector", lambda e, c0=c0: e.tensor_add(
                    maskv[:, c0:c0 + 40], msk1[:], msk2[:]),
                    waits=(("v", p.val("v")),), inc="v")
                if half == 0:
                    p.emit("vector", lambda e, c0=c0: e.tensor_mul(
                        maskv[:, c0:c0 + 40], maskv[:, c0:c0 + 40], sig[:]),
                        waits=(("v", p.val("v")), ("s", m_sig)), inc="v")
            gl_war[sl] = p.val("v")
        m_masks = p.val("v")

        m_wv2 = p.emit("gpsimd", lambda e: e.dma_start(
            out=wgsl[:].rearrange("p (k f) -> p k f", k=8),
            in_=d_wv.rearrange("k p f -> p k f")),
            waits=(("t", p.val("t")),), inc="qw", inc_by=16)
        xv_war = [0, 0]
        for tt in range(NTT):
            bi, bank, war = r_a.take()
            for kt in range(CT):
                w = list(war) if kt == 0 else []
                if tt == 0 and kt == 0:
                    w.append(("qw", m_wv2))
                mt = p.emit("tensor", lambda e, tt=tt, kt=kt, bank=bank:
                            e.matmul(
                    bank[:],
                    ctx.xbf[:, kt * T + tt * 128: kt * T + (tt + 1) * 128],
                    wgsl[:, kt * 512:(kt + 1) * 512],
                    start=(kt == 0), stop=(kt == CT - 1)),
                    waits=tuple(w), inc="t" if kt == CT - 1 else None)
            sl = tt % 2
            w = [("t", mt)]
            if xv_war[sl]:
                w.append(("v", xv_war[sl]))
            ms = p.emit("scalar", lambda e, sl=sl, bank=bank: e.copy(
                xv_sb[:, sl * 320:(sl + 1) * 320], bank[:, 0:320]),
                waits=tuple(w), inc="s")
            r_a.release(bi, "s", ms)
            for h in range(NH):
                base = tt * (NH * 65) + h * 65
                for e_ in range(NE):
                    scol = maskv[:, tt * 80 + h * 5 + e_:
                                 tt * 80 + h * 5 + e_ + 1]
                    xcol = xv_sb[:, sl * 320 + e_ * 64: sl * 320 + (e_ + 1) * 64]
                    w = [("s", ms), ("v", p.val("v"))]
                    if e_ == 0:
                        p.emit("vector", lambda e, base=base, scol=scol,
                               xcol=xcol: e.tensor_scalar_mul(
                            v8[:, base:base + 64], xcol, scol),
                            waits=tuple(w), inc="v")
                    else:
                        p.emit("vector", lambda e, base=base, scol=scol,
                               xcol=xcol: e.scalar_tensor_tensor(
                            v8[:, base:base + 64], xcol, scol,
                            v8[:, base:base + 64], ALU.mult, ALU.add),
                            waits=tuple(w), inc="v")
                p.emit("vector", lambda e, base=base: e.memset(
                    v8[:, base + 64:base + 65], 1.0), inc="v")
            xv_war[sl] = p.val("v")
        m_v8 = p.val("v")

        wq_free[0] = p.val("t")
        wq_free[1] = p.val("t")
        pf_w(0, d_wq)
        pf_w(1, d_wq)
        for which, dst in ((0, qT), (1, kT)):
            for ot in range(4):
                i = which * 4 + ot
                sl = i % 2
                for tc in range(NTC):
                    bi, bank, war = r_st.take()
                    for kt in range(CT):
                        w = list(war) if kt == 0 else []
                        if kt == 0:
                            w.append(("qw", wq_m[i]))
                            if i == 0 and tc == 0:
                                w += xw
                        mt = p.emit("tensor", lambda e, tc=tc, kt=kt, sl=sl,
                                    bank=bank: e.matmul(
                            bank[:], wsl[:, sl * 1024 + kt * 128:
                                         sl * 1024 + (kt + 1) * 128],
                            ctx.xbf[:, kt * T + tc * S5: kt * T + (tc + 1) * S5],
                            start=(kt == 0), stop=(kt == CT - 1)),
                            waits=tuple(w), inc="t" if kt == CT - 1 else None)
                    ms = p.emit("scalar", lambda e, ot=ot, tc=tc, dst=dst,
                                bank=bank: e.copy(
                        dst[:, ot * T + tc * S5: ot * T + (tc + 1) * S5],
                        bank[:]), waits=(("t", mt),), inc="s")
                    r_st.release(bi, "s", ms)
                wq_free[sl] = p.val("t")
                if i + 2 < 8:
                    pf_w(i + 2, d_wq if i + 2 < 4 else d_wk)

        m_qk = p.val("s")

        pT_war = [0, 0]
        m_tr = 0
        p.emit("vector", lambda e: e.memset(seT[:], 0.0), inc="v")
        m_seT = p.val("v")
        for ic in range(NTC):
            for h in range(NH):
                hb = (h % 2) * 64
                htile = h // 2
                bi_o, bank_o, war_o = r_o.take()
                bi_z, bank_z, war_z = r_z.take()
                w_first_pv = list(war_o)
                w_first_z = list(war_z)
                exp_ms = {}
                stq = {}

                def emit_pv(pidx, last, h=h, hb=hb, bank_o=bank_o,
                            bank_z=bank_z, w_first_pv=w_first_pv,
                            w_first_z=w_first_z, exp_ms=exp_ms, stq=stq):
                    pjt, pslot = stq[pidx]
                    w = [exp_ms[pidx]]
                    if pidx == 0:
                        w += w_first_pv
                    p.emit("tensor", lambda e, pjt=pjt, pslot=pslot, hb=hb:
                           e.matmul(
                        bank_o[hb:hb + 64, :],
                        v8[:, pjt * 520 + h * 65: pjt * 520 + h * 65 + 64],
                        pT[:, pslot * S5:(pslot + 1) * S5],
                        start=(pidx == 0), stop=last), waits=tuple(w))
                    w = []
                    if pidx == 0:
                        w += w_first_z
                    mz = p.emit("tensor", lambda e, pjt=pjt, pslot=pslot:
                                e.matmul(
                        bank_z[0:1, :],
                        v8[:, pjt * 520 + h * 65 + 64: pjt * 520 + h * 65 + 65],
                        pT[:, pslot * S5:(pslot + 1) * S5],
                        start=(pidx == 0), stop=last), waits=tuple(w), inc="t")
                    pT_war[pslot] = mz
                    return mz

                for jt in range(NTT):
                    bi_s, bank_s, war_s = r_st.take()
                    w = list(war_s)
                    if h == 0 and ic == 0 and jt == 0:
                        w += [("s", m_qk), ("v", m_v8)]
                    mt_s = p.emit("tensor", lambda e, jt=jt, hb=hb, htile=htile,
                                  ic=ic, bank=bank_s: e.matmul(
                        bank[:], kT[hb:hb + 64,
                                    htile * T + jt * 128: htile * T + (jt + 1) * 128],
                        qT[hb:hb + 64,
                           htile * T + ic * S5: htile * T + (ic + 1) * S5],
                        start=True, stop=True), waits=tuple(w), inc="t")
                    if jt >= 2:
                        emit_pv(jt - 2, last=False)
                    slot = jt % 2
                    ms_e = p.emit("scalar", lambda e, slot=slot, bank=bank_s:
                                  e.activation(
                        pT[:, slot * S5:(slot + 1) * S5], bank[:],
                        AF.Exp, scale=0.125),
                        waits=(("t", mt_s), ("t", pT_war[slot])), inc="s")
                    exp_ms[jt] = ("s", ms_e)
                    r_st.release(bi_s, "s", ms_e)
                    stq[jt] = (jt, slot)
                for pidx in (NTT - 2, NTT - 1):
                    emit_pv(pidx, last=(pidx == NTT - 1))
                mt_pv = p.val("t")
                mzr = p.emit("vector", lambda e, bank=bank_z: e.reciprocal(
                    zr[:], bank[0:1, :]), waits=(("t", mt_pv),), inc="v")
                mt_rep = p.emit("tensor", lambda e, hb=hb, bank=bank_z: e.matmul(
                    bank[hb:hb + 64, :], ctx.onesF[0:1, 0:64], zr[:],
                    start=True, stop=True), waits=(("v", mzr),), inc="t")
                mv_rep = p.emit("vector", lambda e, hb=hb, bank=bank_z:
                                e.tensor_copy(zrep[hb:hb + 64, :],
                                              bank[hb:hb + 64, :]),
                                waits=(("t", mt_rep),), inc="v")
                mnorm = p.emit("vector", lambda e, hb=hb, bank=bank_o:
                               e.tensor_mul(
                    att_sc[hb:hb + 64, :],
                    bank[hb:hb + 64, :], zrep[hb:hb + 64, :]),
                    waits=(("v", mv_rep),), inc="v")
                r_o.release(bi_o, "v", mnorm)
                r_z.release(bi_z, "v", mnorm)
                for j4 in range(4):
                    w = [("v", mnorm)]
                    if m_tr:
                        w.append(("v", m_tr))
                    mt = p.emit("tensor", lambda e, hb=hb, j4=j4: e.transpose(
                        ps_tr[:, 0:64],
                        att_sc[hb:hb + 64, j4 * 128:(j4 + 1) * 128],
                        ctx.ident[hb:hb + 64, hb:hb + 64]),
                        waits=tuple(w), inc="t")
                    m_tr = p.emit("vector", lambda e, j4=j4, h=h: e.tensor_copy(
                        atok[:, j4 * 512 + h * 64: j4 * 512 + (h + 1) * 64],
                        ps_tr[:, 0:64]), waits=(("t", mt),), inc="v")
            m_atok = p.val("v")
            for j4 in range(4):
                tt = ic * 4 + j4
                for e_ in range(NE):
                    for h in range(NH):
                        mcol = maskv[:, tt * 80 + 40 + h * 5 + e_:
                                     tt * 80 + 40 + h * 5 + e_ + 1]
                        acol2 = atok[:, j4 * 512 + h * 64: j4 * 512 + (h + 1) * 64]
                        w = [("v", p.val("v"))]
                        if j4 == 0 and e_ == 0 and h == 0:
                            w.append(("v", m_atok))
                            w.append(("v", m_seT))
                        if h == 0:
                            p.emit("vector", lambda e, e_=e_, mcol=mcol,
                                   acol2=acol2: e.tensor_scalar_mul(
                                se_sb[:, e_ * 64:(e_ + 1) * 64], acol2, mcol),
                                waits=tuple(w), inc="v")
                        else:
                            p.emit("vector", lambda e, e_=e_, mcol=mcol,
                                   acol2=acol2: e.scalar_tensor_tensor(
                                se_sb[:, e_ * 64:(e_ + 1) * 64], acol2, mcol,
                                se_sb[:, e_ * 64:(e_ + 1) * 64],
                                ALU.mult, ALU.add), waits=tuple(w), inc="v")
                m_se = p.val("v")
                for ch in range(3):
                    ncol = 128 if ch < 2 else 64
                    w = [("v", m_se)]
                    if m_tr:
                        w.append(("v", m_tr))
                    mt = p.emit("tensor", lambda e, ch=ch, ncol=ncol:
                                e.transpose(
                        ps_tr[0:ncol, 0:128],
                        se_sb[:, ch * 128:ch * 128 + ncol], ctx.ident[:]),
                        waits=tuple(w), inc="t")
                    m_tr = p.emit("vector", lambda e, tt=tt, ch=ch, ncol=ncol:
                                  e.tensor_copy(
                        seT[0:ncol, ch * T + tt * 128: ch * T + (tt + 1) * 128],
                        ps_tr[0:ncol, 0:128]), waits=(("t", mt),), inc="v")
        m_seT = p.val("v")

    wo_m = {}
    wo_free = [0, 0]
    seq2 = [(tc, ot) for tc in range(NTC) for ot in range(CT)]

    def pf_wo2(i):
        if i >= len(seq2):
            return
        tc, ot = seq2[i]
        sl = i % 2
        wo_m[i] = p.emit("gpsimd", lambda e, ot=ot, sl=sl: e.dma_start(
            out=wosl[:, sl * 384:(sl + 1) * 384], in_=d_wo[ot]),
            waits=(("t", wo_free[sl]),), inc="qw", inc_by=16)

    pf_wo2(0)
    pf_wo2(1)
    wo_i = [0]

    def wo_emitter(tc, ot, bank, waits):
        i = wo_i[0]
        assert seq2[i] == (tc, ot)
        sl = i % 2
        mt = None
        for kt in range(3):
            w = list(waits) if kt == 0 else []
            if kt == 0:
                w.append(("qw", wo_m[i]))
                if i == 0:
                    w.append(("v", m_seT))
            mt = p.emit("tensor", lambda e, tc=tc, ot=ot, kt=kt, sl=sl,
                        bank=bank: e.matmul(
                bank[:], wosl[:, sl * 384 + kt * 128: sl * 384 + (kt + 1) * 128],
                seT[:, kt * T + tc * S5: kt * T + (tc + 1) * S5],
                start=(kt == 0), stop=(kt == 2)),
                waits=tuple(w), inc="t" if kt == 2 else None)
        wo_free[sl] = p.val("t")
        pf_wo2(i + 2)
        wo_i[0] += 1
        return mt

    emit_out_ln(p, ctx, wo_emitter, "l2f")






def prep_l3_attn(inputs):
    d = {}
    d["l3_winT"] = lhsT_blocked(np.ascontiguousarray(
        np.asarray(inputs["L3_attn_W_in"], np.float32).T))
    wx = np.asarray(inputs["L3_attn_W_x"], np.float32)
    wxp = np.zeros((2048, 128), np.float32)
    wxp[:, 0:96] = wx.T
    d["l3_wxT"] = lhsT_blocked(wxp)
    wdtT = np.asarray(inputs["L3_attn_W_dt"], np.float32).T
    d["l3_wdtT"] = np.ascontiguousarray(
        wdtT.reshape(64, 16, 128).transpose(1, 0, 2))
    d["l3_woutT"] = lhsT_blocked(np.ascontiguousarray(
        np.asarray(inputs["L3_attn_W_out"], np.float32).T))
    d["l3_binc"] = np.ascontiguousarray(
        np.asarray(inputs["L3_attn_b_in"], np.float32).reshape(32, 128).T)
    bx = np.zeros((128, 1), np.float32)
    bx[0:96, 0] = np.asarray(inputs["L3_attn_b_x"], np.float32)
    d["l3_bxc"] = bx
    d["l3_bdtc"] = np.ascontiguousarray(
        np.asarray(inputs["L3_attn_b_dt"], np.float32).reshape(16, 128).T)
    d["l3_bout"] = np.ascontiguousarray(
        np.asarray(inputs["L3_attn_b_out"], np.float32).reshape(1, C))
    d["l3_alog"] = np.ascontiguousarray(
        np.asarray(inputs["L3_attn_A_log"], np.float32)
        .reshape(16, 128, 16).transpose(1, 0, 2).reshape(128, 256))
    d["l3_dcol"] = np.ascontiguousarray(
        np.asarray(inputs["L3_attn_D"], np.float32).reshape(16, 128).T)
    return d


def emit_l3_attn(p, ctx, debug=None):
    S5 = 512
    NDT = 16
    d_win = p.dram("l3_winT", [32, 128, 8 * 128], F32, kind="ExternalInput")
    d_wx = p.dram("l3_wxT", [1, 128, 16 * 128], F32, kind="ExternalInput")
    d_wdt = p.dram("l3_wdtT", [16, 64, 128], F32, kind="ExternalInput")
    d_wout = p.dram("l3_woutT", [8, 128, 16 * 128], F32, kind="ExternalInput")
    d_binc = p.dram("l3_binc", [128, 32], F32, kind="ExternalInput")
    d_bxc = p.dram("l3_bxc", [128, 1], F32, kind="ExternalInput")
    d_bdtc = p.dram("l3_bdtc", [128, 16], F32, kind="ExternalInput")
    d_bout = p.dram("l3_bout", [1, C], F32, kind="ExternalInput")
    d_alog = p.dram("l3_alog", [128, 256], F32, kind="ExternalInput")
    d_dcol = p.dram("l3_dcol", [128, 16], F32, kind="ExternalInput")
    xin_dram = p.dram(p._n("l3_xind"), [16, 128, T], BF16)
    z_dram = p.dram(p._n("l3_zd"), [16, 128, T], BF16)
    y_dram = p.dram(p._n("l3_yd"), [16, 128, T], BF16)
    bc_dram = p.dram(p._n("l3_bcd"), [32, T], BF16)

    xt = p.sb("l3_xt", [128, T], BF16)
    zt = p.sb("l3_zt", [128, T], BF16)
    delta = p.sb("l3_delta", [128, 2 * T], F32)
    draw = p.sb("l3_draw", [64, T], BF16)
    bcsb = p.sb("l3_bc", [96, T], BF16)
    dx = p.sb("l3_dx", [128, T], BF16)
    a_b = p.sb("l3_a", [128, T], F32)
    b_b = p.sb("l3_b", [128, T], BF16)
    h_b = p.sb("l3_h", [128, T], BF16)
    yac = p.sb("l3_yac", [128, T], BF16)
    brep = p.sb("l3_brep", [128, 2 * T], BF16)
    crep = p.sb("l3_crep", [128, 2 * T], BF16)
    rowb = p.sb("l3_rowb", [1, T], BF16)
    binc = p.sb("l3_binc_s", [128, 32], F32)
    bxc = p.sb("l3_bxc_s", [128, 1], F32)
    bdtc = p.sb("l3_bdtc_s", [128, 16], F32)
    boutr = p.sb("l3_boutr", [1, C], F32)
    acol = p.sb("l3_acol", [128, 256], F32)
    dcol = p.sb("l3_dcol_s", [128, 16], F32)
    winsl = p.sb("l3_winsl", [128, 2 * 1024], BF16)
    wxsl = p.sb("l3_wxsl", [128, 16 * 128], BF16)
    wdtsl = p.sb("l3_wdtsl", [64, 2 * 128], BF16)
    woutsl = p.sb("l3_wouts", [128, 2 * 2048], BF16)
    zxst = p.sb("l3_zxst", [128, 2 * S5], BF16)

    xw = list(ctx.xbf_wait)

    with ExitStack() as pstack:
        PS = lambda nm, shape: pstack.enter_context(
            p.nc.psum_tensor(p._n(nm), shape, F32))
        r_a = Rot([PS("l3a", [128, S5]) for _ in range(4)])
        ps_rep = PS("l3rep", [128, T])

        for dst, srcd in ((binc, d_binc), (bxc, d_bxc), (bdtc, d_bdtc),
                          (boutr, d_bout), (acol, d_alog), (dcol, d_dcol)):
            p.emit("sync", lambda e, dst=dst, srcd=srcd: e.dma_start(
                out=dst[:], in_=srcd), inc="qx", inc_by=16)
        m_consts = p.val("qx")
        ms_ae = p.emit("scalar", lambda e: e.activation(
            acol[:], acol[:], AF.Exp), waits=(("qx", m_consts),), inc="s")
        m_acol = p.emit("vector", lambda e: e.tensor_scalar_mul(
            acol[:], acol[:], -1.0), waits=(("s", ms_ae),), inc="v")

        win_m = {}
        win_free = [0, 0]

        def pf_win(i):
            if i >= 32:
                return
            sl = i % 2
            win_m[i] = p.emit("gpsimd", lambda e, i=i, sl=sl: e.dma_start(
                out=winsl[:, sl * 1024:(sl + 1) * 1024], in_=d_win[i]),
                waits=(("t", win_free[sl]),), inc="qw", inc_by=16)

        pf_win(0)
        pf_win(1)
        zx_war = [0, 0]
        for ot in range(32):
            sl = ot % 2
            for tc in range(NTC):
                bi, bank, war = r_a.take()
                for kt in range(CT):
                    w = list(war) if kt == 0 else []
                    if kt == 0:
                        w.append(("qw", win_m[ot]))
                        if ot == 0 and tc == 0:
                            w += xw
                    p.emit("tensor", lambda e, tc=tc, kt=kt, sl=sl,
                           bank=bank: e.matmul(
                        bank[:], winsl[:, sl * 1024 + kt * 128:
                                       sl * 1024 + (kt + 1) * 128],
                        ctx.xbf[:, kt * T + tc * S5: kt * T + (tc + 1) * S5],
                        start=(kt == 0), stop=(kt == CT - 1)),
                        waits=tuple(w), inc="t" if kt == CT - 1 else None)
                mt = p.val("t")
                zx = tc % 2
                w = [("t", mt)]
                if zx_war[zx]:
                    w.append(("qx", zx_war[zx]))
                if ot == 0 and tc == 0:
                    w.append(("qx", m_consts))
                mv = p.emit("vector", lambda e, ot=ot, zx=zx, bank=bank:
                            e.tensor_scalar(
                    zxst[:, zx * S5:(zx + 1) * S5], bank[:],
                    binc[:, ot:ot + 1], 0.0, ALU.add, ALU.add),
                    waits=tuple(w), inc="v")
                r_a.release(bi, "v", mv)
                dtgt = xin_dram if ot < 16 else z_dram
                kt_ = ot % 16
                zx_war[zx] = p.emit("sync", lambda e, kt_=kt_, tc=tc, zx=zx,
                                    dtgt=dtgt, mv=mv: e.dma_start(
                    out=dtgt[kt_][:, tc * S5:(tc + 1) * S5],
                    in_=zxst[:, zx * S5:(zx + 1) * S5]),
                    waits=(("v", mv),), inc="qx", inc_by=16)
            win_free[sl] = p.val("t")
            pf_win(ot + 2)
        m_xin_spill = p.val("qx")
        ctx.m_xbf_users_t = max(ctx.m_xbf_users_t, p.val("t"))

        m_wx = p.emit("gpsimd", lambda e: e.dma_start(
            out=wxsl[:], in_=d_wx[0]), inc="qw", inc_by=16)
        m_stg_reads = 0
        stgbufs = (brep, crep)
        for tc in range(NTC):
            lds = []
            for half in range(2):
                w = [("qx", m_xin_spill)]
                if m_stg_reads:
                    w.append(("t", m_stg_reads))
                lds.append(p.emit("sync", lambda e, tc=tc, half=half:
                                  e.dma_start(
                    out=stgbufs[half][:].rearrange("p (k t) -> p k t", t=S5),
                    in_=xin_dram.rearrange("k p t -> p k t")[
                        :, half * 8:(half + 1) * 8, tc * S5:(tc + 1) * S5]),
                    waits=tuple(w), inc="qx", inc_by=16))
            bi, bank, war = r_a.take()
            for kt in range(16):
                w = list(war) if kt == 0 else []
                if kt == 0:
                    w += [("qx", lds[0]), ("qw", m_wx)]
                if kt == 8:
                    w.append(("qx", lds[1]))
                p.emit("tensor", lambda e, tc=tc, kt=kt, bank=bank: e.matmul(
                    bank[:], wxsl[:, kt * 128:(kt + 1) * 128],
                    stgbufs[kt // 8][:, (kt % 8) * S5:(kt % 8 + 1) * S5],
                    start=(kt == 0), stop=(kt == 15)),
                    waits=tuple(w), inc="t" if kt == 15 else None)
            mt = p.val("t")
            m_stg_reads = mt
            mv1 = p.emit("vector", lambda e, tc=tc, bank=bank: e.tensor_scalar(
                draw[:, tc * S5:(tc + 1) * S5], bank[0:64, :],
                bxc[0:64, 0:1], 0.0, ALU.add, ALU.add),
                waits=(("t", mt),), inc="v")
            mv2 = p.emit("vector", lambda e, tc=tc, bank=bank: e.tensor_scalar(
                bcsb[64:96, tc * S5:(tc + 1) * S5], bank[64:96, :],
                bxc[64:96, 0:1], 0.0, ALU.add, ALU.add), inc="v")
            r_a.release(bi, "v", mv2)
        m_draw = p.val("v")
        m_bc_dma = p.emit("sync", lambda e: e.dma_start(
            out=bc_dram, in_=bcsb[64:96, :]), waits=(("v", m_draw),),
            inc="qx", inc_by=16)
        m_dbc_t = p.val("t")

        wdt_m = {}
        wdt_free = [0, 0]

        def pf_wdt(dt_i):
            if dt_i >= NDT:
                return
            sl = dt_i % 2
            wdt_m[dt_i] = p.emit("gpsimd", lambda e, dt_i=dt_i, sl=sl:
                                 e.dma_start(
                out=wdtsl[:, sl * 128:(sl + 1) * 128], in_=d_wdt[dt_i]),
                waits=(("t", wdt_free[sl]),), inc="qw", inc_by=16)

        pf_wdt(0)
        pf_wdt(1)

        rep_war = {0: ("t", m_dbc_t), 1: ("t", m_dbc_t)}
        repB_m = {}
        repC_m = {}
        rowb_free = [0]
        ps_rep_free = [0]

        def rep_build(dt_i, n):
            sl = n % 2
            w = [("qx", m_bc_dma)]
            if rowb_free[0]:
                w.append(("t", rowb_free[0]))
            mr1 = p.emit("sync", lambda e, n=n: e.dma_start(
                out=rowb[0:1, 0:T], in_=bc_dram[n:n + 1, :]),
                waits=tuple(w), inc="qx", inc_by=16)
            for c in range(4):
                w = []
                if c == 0:
                    w.append(("qx", mr1))
                    if ps_rep_free[0]:
                        w.append(("s", ps_rep_free[0]))
                p.emit("tensor", lambda e, c=c: e.matmul(
                    ps_rep[:, c * S5:(c + 1) * S5], ctx.maskA[0:1, 0:128],
                    rowb[0:1, c * S5:(c + 1) * S5], start=True, stop=True),
                    waits=tuple(w), inc="t" if c == 3 else None)
            mtB = p.val("t")
            mr2 = p.emit("sync", lambda e, n=n: e.dma_start(
                out=rowb[0:1, 0:T], in_=bc_dram[16 + n:17 + n, :]),
                waits=(("t", mtB),), inc="qx", inc_by=16)
            wB = [("t", mtB)]
            if rep_war[sl]:
                wB.append(rep_war[sl])
            msB = p.emit("scalar", lambda e, sl=sl: e.copy(
                brep[:, sl * T:(sl + 1) * T], ps_rep[:]),
                waits=tuple(wB), inc="s")
            repB_m[n] = msB
            for c in range(4):
                w = [("s", msB)] if c == 0 else []
                if c == 0:
                    w.append(("qx", mr2))
                p.emit("tensor", lambda e, c=c: e.matmul(
                    ps_rep[:, c * S5:(c + 1) * S5], ctx.maskA[0:1, 0:128],
                    rowb[0:1, c * S5:(c + 1) * S5],
                    start=True, stop=True), waits=tuple(w),
                    inc="t" if c == 3 else None)
            mtC = p.val("t")
            rowb_free[0] = mtC
            wC = [("t", mtC)]
            if rep_war[sl]:
                wC.append(rep_war[sl])
            msC = p.emit("scalar", lambda e, sl=sl: e.copy(
                crep[:, sl * T:(sl + 1) * T], ps_rep[:]),
                waits=tuple(wC), inc="s")
            repC_m[n] = msC
            ps_rep_free[0] = msC

        xt_war = 0
        zt_war = 0
        y_sp = {}
        m_scan_h_prev = 0
        for dt_i in range(NDT):
            sl = dt_i % 2
            dso = (dt_i % 2) * T
            bcol = bdtc[:, dt_i:dt_i + 1]
            for tc in range(NTC):
                bi, bank, war = r_a.take()
                w = list(war)
                if tc == 0:
                    w += [("qw", wdt_m[dt_i]), ("v", m_draw)]
                mt = p.emit("tensor", lambda e, tc=tc, sl=sl, bank=bank:
                            e.matmul(
                    bank[:], wdtsl[:, sl * 128:(sl + 1) * 128],
                    draw[:, tc * S5:(tc + 1) * S5], start=True, stop=True),
                    waits=tuple(w), inc="t")
                w = [("t", mt)]
                if tc == 0:
                    w.append(("qx", m_consts))
                    if m_scan_h_prev:
                        w.append(("v", m_scan_h_prev))
                ms1 = p.emit("scalar", lambda e, tc=tc, bank=bank, bcol=bcol:
                             e.activation(
                    a_b[:, tc * S5:(tc + 1) * S5], bank[:], AF.Abs,
                    bias=bcol), waits=tuple(w), inc="s")
                p.emit("scalar", lambda e, tc=tc: e.activation(
                    a_b[:, tc * S5:(tc + 1) * S5],
                    a_b[:, tc * S5:(tc + 1) * S5], AF.Exp, scale=-1.0),
                    inc="s")
                ms3 = p.emit("scalar", lambda e, tc=tc: e.activation(
                    a_b[:, tc * S5:(tc + 1) * S5],
                    a_b[:, tc * S5:(tc + 1) * S5], AF.Ln, bias=1.0), inc="s")
                w = [("t", mt)]
                if tc == 0:
                    w.append(("s", p.val("s")))
                mv4 = p.emit("vector", lambda e, tc=tc, dso=dso, bank=bank,
                             bcol=bcol: e.tensor_scalar(
                    delta[:, dso + tc * S5:dso + (tc + 1) * S5], bank[:],
                    bcol, 0.0, ALU.add, ALU.max), waits=tuple(w), inc="v")
                mv5 = p.emit("vector", lambda e, tc=tc, dso=dso: e.tensor_add(
                    delta[:, dso + tc * S5:dso + (tc + 1) * S5],
                    delta[:, dso + tc * S5:dso + (tc + 1) * S5],
                    a_b[:, tc * S5:(tc + 1) * S5]),
                    waits=(("s", ms3),), inc="v")
                r_a.release(bi, "v", mv5)
            m_delta = p.val("v")
            wdt_free[sl] = p.val("t")
            pf_wdt(dt_i + 2)
            w = [("qx", m_xin_spill)]
            if xt_war:
                w.append(("v", xt_war))
            m_xt = p.emit("sync", lambda e, dt_i=dt_i: e.dma_start(
                out=xt[:], in_=xin_dram[dt_i]), waits=tuple(w),
                inc="qx", inc_by=16)
            mv_dx = p.emit("vector", lambda e, dso=dso: e.tensor_mul(
                dx[:], delta[:, dso:dso + T], xt[:]),
                waits=(("qx", m_xt),), inc="v")
            rep_build(dt_i, 0)
            rep_build(dt_i, 1)
            for n in range(16):
                nsl = n % 2
                w = [("v", mv_dx), ("v", m_acol), ("v", p.val("v"))]
                ms_a = p.emit("scalar", lambda e, dt_i=dt_i, n=n, dso=dso:
                              e.activation(
                    a_b[:], delta[:, dso:dso + T], AF.Exp,
                    scale=acol[:, dt_i * 16 + n:dt_i * 16 + n + 1]),
                    waits=tuple(w), inc="s")
                mv_b = p.emit("vector", lambda e, nsl=nsl: e.tensor_mul(
                    b_b[:], dx[:], brep[:, nsl * T:(nsl + 1) * T]),
                    waits=(("s", repB_m[n]),), inc="v")
                mv_h = p.emit("vector", lambda e: e.tensor_tensor_scan(
                    h_b[:], a_b[:], b_b[:], 0.0, ALU.mult, ALU.add),
                    waits=(("s", ms_a),), inc="v")
                p.emit("vector", lambda e, nsl=nsl: e.tensor_mul(
                    b_b[:], h_b[:], crep[:, nsl * T:(nsl + 1) * T]),
                    waits=(("s", repC_m[n]),), inc="v")
                if n == 0:
                    w = [("qx", y_sp[dt_i - 1])] if dt_i > 0 else []
                    mv_y = p.emit("vector", lambda e: e.tensor_copy(
                        yac[:], b_b[:]), waits=tuple(w), inc="v")
                else:
                    mv_y = p.emit("vector", lambda e: e.tensor_add(
                        yac[:], yac[:], b_b[:]), inc="v")
                rep_war[nsl] = ("v", mv_y)
                if n == 15:
                    m_scan_h_prev = mv_h
                if n + 2 < 16:
                    rep_build(dt_i, n + 2)
            w = [("qx", m_xin_spill)]
            if zt_war:
                w.append(("v", zt_war))
            m_zt = p.emit("sync", lambda e, dt_i=dt_i: e.dma_start(
                out=zt[:], in_=z_dram[dt_i]), waits=tuple(w),
                inc="qx", inc_by=16)
            mv_y2 = p.emit("vector", lambda e, dt_i=dt_i:
                           e.scalar_tensor_tensor(
                yac[:], xt[:], dcol[:, dt_i:dt_i + 1], yac[:],
                ALU.mult, ALU.add), inc="v")
            xt_war = mv_y2
            ms_z = p.emit("scalar", lambda e: e.activation(
                zt[:], zt[:], AF.Silu), waits=(("qx", m_zt),), inc="s")
            mv_y3 = p.emit("vector", lambda e: e.tensor_mul(
                yac[:], yac[:], zt[:]), waits=(("s", ms_z),), inc="v")
            zt_war = mv_y3
            y_sp[dt_i] = p.emit("sync", lambda e, dt_i=dt_i: e.dma_start(
                out=y_dram[dt_i], in_=yac[:]), waits=(("v", mv_y3),),
                inc="qx", inc_by=16)
        m_y_done = p.val("qx")
        m_scan_v = p.val("v")

    wout_m = {}
    wout_free = [0, 0]
    seq = [(tc, ot) for tc in range(NTC) for ot in range(CT)]

    def pf_wout(i):
        if i >= len(seq):
            return
        _, ot = seq[i]
        sl = i % 2
        wout_m[i] = p.emit("gpsimd", lambda e, ot=ot, sl=sl: e.dma_start(
            out=woutsl[:, sl * 2048:(sl + 1) * 2048], in_=d_wout[ot]),
            waits=(("t", wout_free[sl]),), inc="qw", inc_by=16)

    pf_wout(0)
    pf_wout(1)
    wo_i = [0]
    m_ystage = [0, 0]
    m_stg_war = [0]

    def wo_emitter(tc, ot, bank, waits):
        i = wo_i[0]
        assert seq[i] == (tc, ot)
        sl = i % 2
        if ot == 0:
            for half in range(2):
                w = [("qx", m_y_done)]
                w.append(("v", m_scan_v) if tc == 0 else
                         ("t", m_stg_war[0]))
                m_ystage[half] = p.emit("sync", lambda e, tc=tc, half=half:
                                        e.dma_start(
                    out=stgbufs[half][:].rearrange("p (k t) -> p k t", t=S5),
                    in_=y_dram.rearrange("k p t -> p k t")[
                        :, half * 8:(half + 1) * 8, tc * S5:(tc + 1) * S5]),
                    waits=tuple(w), inc="qx", inc_by=16)
        for kt in range(16):
            w = list(waits) if kt == 0 else []
            if kt == 0:
                w += [("qw", wout_m[i]), ("qx", m_ystage[0])]
            if kt == 8:
                w.append(("qx", m_ystage[1]))
            p.emit("tensor", lambda e, kt=kt, sl=sl, bank=bank: e.matmul(
                bank[:], woutsl[:, sl * 2048 + kt * 128:
                                sl * 2048 + (kt + 1) * 128],
                stgbufs[kt // 8][:, (kt % 8) * S5:(kt % 8 + 1) * S5],
                start=(kt == 0), stop=False), waits=tuple(w))
        mt = p.emit("tensor", lambda e, ot=ot, bank=bank: e.matmul(
            bank[:], boutr[0:1, ot * 128:(ot + 1) * 128],
            ctx.onesF[0:1, 0:S5], start=False, stop=True), inc="t")
        wout_free[sl] = p.val("t")
        if ot == CT - 1:
            m_stg_war[0] = p.val("t")
        pf_wout(i + 2)
        wo_i[0] += 1
        return mt

    emit_out_ln(p, ctx, wo_emitter, "l3a")

    if debug:
        dbg = p.dram("l3_dbg", [16, 128, T], BF16, kind="ExternalOutput")
        srcs = {"xin": xin_dram, "z": z_dram, "y": y_dram}
        for k in range(16):
            p.emit("sync", lambda e, k=k, s=srcs[debug]: e.dma_start(
                out=dbg[k], in_=s[k]), waits=(("qx", p.val("qx")),),
                inc="qx", inc_by=16)


def prep_l3_ffn(inputs):
    d = {}
    for nm, key in (("l3f_wrT", "L3_ffn_W_r"), ("l3f_wkT", "L3_ffn_W_k"),
                    ("l3f_wvT", "L3_ffn_W_v")):
        d[nm] = lhsT_blocked(np.ascontiguousarray(
            np.asarray(inputs[key], np.float32).T))
    d["l3f_mu"] = np.ascontiguousarray(np.stack(
        [np.asarray(inputs["L3_ffn_mu_r"], np.float32).reshape(8, 128).T,
         np.asarray(inputs["L3_ffn_mu_k"], np.float32).reshape(8, 128).T],
        axis=2).reshape(128, 16))
    return d


def emit_l3_ffn(p, ctx):
    S5 = 512
    d_wr = p.dram("l3f_wrT", [8, 128, 1024], F32, kind="ExternalInput")
    d_wk = p.dram("l3f_wkT", [8, 128, 1024], F32, kind="ExternalInput")
    d_wv = p.dram("l3f_wvT", [8, 128, 1024], F32, kind="ExternalInput")
    d_mu = p.dram("l3f_mu", [128, 16], F32, kind="ExternalInput")

    xsh = p.sb("l3f_xsh", [128, T], BF16)
    tmp = p.sb("l3f_tmp", [128, T], BF16)
    xrb = p.sb("l3f_xr", [128, CT * T], BF16)
    xkb = p.sb("l3f_xk", [128, CT * T], BF16)
    rb = p.sb("l3f_rb", [128, CT * S5], BF16)
    kab = p.sb("l3f_ka", [128, CT * S5], BF16)
    obuf = p.sb("l3f_ob", [128, CT * S5], BF16)
    mus = p.sb("l3f_mus", [128, 16], F32)
    wsl = p.sb("l3f_wsl", [128, 2 * 1024], BF16)

    m_mu = p.emit("sync", lambda e: e.dma_start(out=mus[:], in_=d_mu),
                  inc="qx", inc_by=16)
    xw = list(ctx.xbf_wait)

    for ct in range(CT):
        w = list(xw) if ct == 0 else []
        if ct == 0:
            w.append(("qx", m_mu))
        p.emit("vector", lambda e, ct=ct: e.memset(xsh[:, 0:1], 0.0),
               waits=tuple(w), inc="v")
        p.emit("vector", lambda e, ct=ct: e.tensor_copy(
            xsh[:, 1:T], ctx.xbf[:, ct * T:ct * T + T - 1]), inc="v")
        p.emit("vector", lambda e, ct=ct: e.tensor_sub(
            tmp[:], ctx.xbf[:, ct * T:(ct + 1) * T], xsh[:]), inc="v")
        p.emit("vector", lambda e, ct=ct: e.scalar_tensor_tensor(
            xrb[:, ct * T:(ct + 1) * T], tmp[:], mus[:, 2 * ct:2 * ct + 1],
            xsh[:], ALU.mult, ALU.add), inc="v")
        p.emit("vector", lambda e, ct=ct: e.scalar_tensor_tensor(
            xkb[:, ct * T:(ct + 1) * T], tmp[:], mus[:, 2 * ct + 1:2 * ct + 2],
            xsh[:], ALU.mult, ALU.add), inc="v")
    m_lerp = p.val("v")

    seq = []
    for tc in range(NTC):
        seq += [(tc, "r", ot) for ot in range(CT)]
        seq += [(tc, "k", ot) for ot in range(CT)]
        seq += [(tc, "v", ot) for ot in range(CT)]
    dmap = {"r": d_wr, "k": d_wk, "v": d_wv}
    w_m = {}
    w_free = [0, 0]

    def pf_w(i):
        if i >= len(seq):
            return
        _, wh, ot = seq[i]
        sl = i % 2
        w_m[i] = p.emit("gpsimd", lambda e, wh=wh, ot=ot, sl=sl: e.dma_start(
            out=wsl[:, sl * 1024:(sl + 1) * 1024], in_=dmap[wh][ot]),
            waits=(("t", w_free[sl]),), inc="qw", inc_by=16)

    pf_w(0)
    pf_w(1)
    w_i = [0]

    def mm_group(i, rhs_buf, bank, waits):
        sl = i % 2
        for kt in range(CT):
            w = list(waits) if kt == 0 else []
            if kt == 0:
                w.append(("qw", w_m[i]))
            p.emit("tensor", lambda e, kt=kt, sl=sl, bank=bank,
                   rhs_buf=rhs_buf: e.matmul(
                bank[:], wsl[:, sl * 1024 + kt * 128:
                             sl * 1024 + (kt + 1) * 128],
                rhs_buf[kt], start=(kt == 0), stop=(kt == CT - 1)),
                waits=tuple(w), inc="t" if kt == CT - 1 else None)
        mt = p.val("t")
        w_free[sl] = mt
        pf_w(i + 2)
        w_i[0] += 1
        return mt

    ol = OutLN(p, ctx, "l3f")
    with ExitStack() as pstack:
        PS = lambda nm, shape: pstack.enter_context(
            p.nc.psum_tensor(p._n(nm), shape, F32))
        rf = Rot([PS("l3f", [128, S5]) for _ in range(4)])
        rb_war = 0
        ka_war = 0
        ob_war = 0
        for tc in range(NTC):
            for ot in range(CT):
                bi, bank, war = rf.take()
                w = list(war)
                if ot == 0:
                    w.append(("v", m_lerp))
                    if rb_war:
                        w.append(("v", rb_war))
                rhs = [xrb[:, kt * T + tc * S5: kt * T + (tc + 1) * S5]
                       for kt in range(CT)]
                mt = mm_group(w_i[0], rhs, bank, tuple(w))
                ms = p.emit("scalar", lambda e, ot=ot, bank=bank: e.activation(
                    rb[:, ot * S5:(ot + 1) * S5], bank[:], AF.Sigmoid),
                    waits=(("t", mt),), inc="s")
                rf.release(bi, "s", ms)
            m_r = p.val("s")
            for ot in range(CT):
                bi, bank, war = rf.take()
                w = list(war)
                if ot == 0 and ka_war:
                    w.append(("t", ka_war))
                rhs = [xkb[:, kt * T + tc * S5: kt * T + (tc + 1) * S5]
                       for kt in range(CT)]
                mt = mm_group(w_i[0], rhs, bank, tuple(w))
                p.emit("scalar", lambda e, ot=ot, bank=bank: e.activation(
                    kab[:, ot * S5:(ot + 1) * S5], bank[:], AF.Relu),
                    waits=(("t", mt),), inc="s")
                ms = p.emit("scalar", lambda e, ot=ot: e.activation(
                    kab[:, ot * S5:(ot + 1) * S5],
                    kab[:, ot * S5:(ot + 1) * S5], AF.Square), inc="s")
                rf.release(bi, "s", ms)
            m_ka = p.val("s")
            for ot in range(CT):
                bi, bank, war = rf.take()
                w = list(war)
                if ot == 0:
                    w.append(("s", m_ka))
                    if ob_war:
                        w.append(("t", ob_war))
                rhs = [kab[:, kt * S5:(kt + 1) * S5] for kt in range(CT)]
                mt = mm_group(w_i[0], rhs, bank, tuple(w))
                mv = p.emit("vector", lambda e, ot=ot, bank=bank: e.tensor_mul(
                    obuf[:, ot * S5:(ot + 1) * S5], bank[:],
                    rb[:, ot * S5:(ot + 1) * S5]),
                    waits=(("t", mt), ("s", m_r)), inc="v")
                rf.release(bi, "v", mv)
            rb_war = p.val("v")
            m_ob = p.val("v")

            def o_emitter(tc_, ot, bank, waits, m_ob=m_ob):
                w = list(waits)
                w.append(("v", m_ob))
                mt = p.emit("tensor", lambda e, ot=ot, bank=bank: e.matmul(
                    bank[:], ctx.ident[:],
                    obuf[:, ot * S5:(ot + 1) * S5], start=True, stop=True),
                    waits=tuple(w), inc="t")
                return mt

            ol.chunk(tc, o_emitter)
            ka_war = p.val("t")
            ob_war = p.val("t")
    ol.end()


MODS = [
    (prep_l0_attn, emit_l0_attn),
    (prep_l0_ffn, emit_l0_ffn),
    (prep_l1_attn, emit_l1_attn),
    (prep_l1_ffn, emit_l1_ffn),
    (prep_l2_attn, emit_l2_attn),
    (prep_l2_ffn, emit_l2_ffn),
    (prep_l3_attn, emit_l3_attn),
    (prep_l3_ffn, emit_l3_ffn),
]


def kernel(**inputs):
    consts = const_arrays()
    xs = [np.ascontiguousarray(np.asarray(inputs["L0_attn_X"][i],
                                          np.float32).T)
          for i in range(NCORES)]
    for prep, emit in MODS:
        wfeed = prep(inputs)
        wfeed.update(consts)
        p = Prog(qx_lanes=2 if emit is emit_l2_attn else 4)
        x_in = p.dram("x_in", [C, T], F32, kind="ExternalInput")
        x_out = p.dram("x_out", [C, T], F32, kind="ExternalOutput")
        ctx = Ctx(p)
        emit_load_x(p, ctx, x_in)
        emit(p, ctx)
        m_out = emit_store_out(p, ctx, x_out)
        p.wait("sync", (("qx", m_out),))
        nc = p.finalize()
        feeds = [dict(wfeed, x_in=xs[i]) for i in range(NCORES)]
        r = run_bass_kernel_spmd(nc, feeds, core_ids=list(range(NCORES)))
        xs = [np.asarray(r.results[i]["x_out"]) for i in range(NCORES)]
    return np.stack([x.T for x in xs]).astype(np.float32)


# version 04 (final)
# speedup vs baseline: 1.0321x; 1.0321x over previous
import math
from contextlib import ExitStack

import numpy as np

import concourse.bass as bass
import concourse.mybir as mybir
from concourse.bass_utils import run_bass_kernel_spmd

F32 = mybir.dt.float32
BF16 = mybir.dt.bfloat16
AF = mybir.ActivationFunctionType
ALU = mybir.AluOpType

NCORES = 8
T = 2048
C = 1024
CT = C // 128
NTT = T // 128
NTC = T // 512


class Prog:
    ENGINES = ("tensor", "vector", "scalar", "gpsimd", "sync")
    def __init__(self, name="prog", qx_lanes=None, qw_lanes=None):
        import os as _os
        if qx_lanes is None:
            qx_lanes = int(_os.environ.get("QXL", "4"))
        if qw_lanes is None:
            qw_lanes = int(_os.environ.get("QWL", "4"))
        self.QLANES = {"qx": qx_lanes, "qw": qw_lanes}
        self.nc = bass.Bass("TRN2", target_bir_lowering=False,
                            detect_race_conditions=False)
        self.stack = ExitStack()
        self.ops = {e: [] for e in self.ENGINES}
        self.sems = {}
        self.cnt = {}
        self.qhist = {q: {0: (0,) * n} for q, n in self.QLANES.items()}
        self.seen = {e: {} for e in self.ENGINES}
        self.cur_stack = None
        self.uid = 0

    def _n(self, prefix):
        self.uid += 1
        return f"{prefix}_{self.uid}"

    def sem(self, name):
        if name not in self.sems:
            self.sems[name] = self.stack.enter_context(self.nc.semaphore(name))
            self.cnt[name] = 0
        return self.sems[name]

    def sb(self, name, shape, dtype=F32):
        st = self.cur_stack if self.cur_stack is not None else self.stack
        return st.enter_context(self.nc.sbuf_tensor(name, shape, dtype))

    def barrier(self):
        vals = [(s, self.val(s)) for s in ("t", "v", "s")]
        for q, n in self.QLANES.items():
            vals += [(f"{q}{l}", self.cnt.get(f"{q}{l}", 0))
                     for l in range(n)]
        for eng in self.ENGINES:
            self.emit(eng, lambda e: e.nop(), waits=tuple(vals))

    def module_scope(self):
        from contextlib import contextmanager

        @contextmanager
        def scope():
            st = ExitStack()
            self.cur_stack = st
            try:
                yield
            finally:
                st.close()
                self.cur_stack = None
                self.barrier()
        return scope()

    def dram(self, name, shape, dtype=F32, kind="Internal"):
        return self.nc.dram_tensor(name, shape, dtype, kind=kind).ap()

    def _xlate(self, eng, waits):
        out = {}
        for n, v in waits:
            if v <= 0:
                continue
            if n in self.QLANES:
                lanes = self.qhist[n][v]
                for li, lv in enumerate(lanes):
                    if lv > 0:
                        key = f"{n}{li}"
                        out[key] = max(out.get(key, 0), lv)
            else:
                out[n] = max(out.get(n, 0), v)
        seen = self.seen[eng]
        wl = []
        for n, v in out.items():
            if v > seen.get(n, 0):
                wl.append((self.sem(n), v))
                seen[n] = v
        return wl

    def emit(self, eng, f, waits=(), inc=None, inc_by=1):
        wl_src = list(waits)
        ih = None
        if inc in self.QLANES:
            nl = self.QLANES[inc]
            idx = self.cnt.get(inc, 0)
            lane = (idx // 16) % nl
            lname = f"{inc}{lane}"
            if self.cnt.get(lname, 0) > 0:
                wl_src.append((lname, self.cnt[lname]))
            ih = self.sem(lname)
            self.cnt[lname] = self.cnt.get(lname, 0) + inc_by
            self.cnt[inc] = idx + inc_by
            ret = self.cnt[inc]
            self.qhist[inc][ret] = tuple(
                self.cnt.get(f"{inc}{j}", 0) for j in range(nl))
        elif inc is not None:
            ih = self.sem(inc)
            self.cnt[inc] = self.cnt.get(inc, 0) + inc_by
            ret = self.cnt[inc]
        else:
            ret = None
        wl = self._xlate(eng, wl_src)

        def play(e, f=f, wl=wl, ih=ih, inc_by=inc_by):
            for h, v in wl:
                e.wait_ge(h, v)
            ins = f(e)
            if ih is not None:
                ins.then_inc(ih, inc_by)
        self.ops[eng].append(play)
        return ret

    def wait(self, eng, waits):
        wl = self._xlate(eng, waits)

        def play(e, wl=wl):
            for h, v in wl:
                e.wait_ge(h, v)
        self.ops[eng].append(play)

    def val(self, sem):
        return self.cnt.get(sem, 0)

    def finalize(self):
        def make(ops):
            def body(eng):
                for f in ops:
                    f(eng)
            return body
        with self.stack:
            with self.nc.Block() as block:
                for e in self.ENGINES:
                    if self.ops[e]:
                        getattr(block, e)(make(self.ops[e]))
        return self.nc


class Rot:

    def __init__(self, tiles):
        self.tiles = tiles
        self.free = [None] * len(tiles)
        self.i = 0

    def take(self):
        i = self.i % len(self.tiles)
        self.i += 1
        war = self.free[i]
        return i, self.tiles[i], ([war] if war else [])

    def release(self, i, sem, val):
        self.free[i] = (sem, val)


def lhsT_blocked(wT, mtile=128):
    K, M = wT.shape
    assert K % 128 == 0 and M % mtile == 0
    b = wT.reshape(K // 128, 128, M // mtile, mtile).transpose(2, 1, 0, 3)
    return np.ascontiguousarray(b.reshape(M // mtile, 128, (K // 128) * mtile))


def lhsT_flat(wT, mtile=128):
    b = lhsT_blocked(wT, mtile)
    return np.ascontiguousarray(b.transpose(1, 0, 2).reshape(128, -1))


def rhs_blocked(w):
    K, N = w.shape
    assert K % 128 == 0
    return np.ascontiguousarray(w.reshape(K // 128, 128, N))


def const_arrays():
    import ml_dtypes
    return {
        "d_maskA": np.triu(np.ones((128, 128), np.float32)).astype(ml_dtypes.bfloat16),
        "d_ident": np.eye(128, dtype=np.float32).astype(ml_dtypes.bfloat16),
        "d_maskLT": np.triu(np.ones((64, 64), np.float32)).astype(ml_dtypes.bfloat16),
        "d_onesb": np.ones((128, 1), ml_dtypes.bfloat16),
        "d_onesF": np.ones((128, 512), np.float32),
        "d_identF": np.eye(128, dtype=np.float32),
        "d_maskC4": _mask_c4().astype(ml_dtypes.bfloat16),
    }


def _mask_c4():
    m = np.zeros((128, 4 * 512), np.float32)
    for r in range(4):
        for j in range(128):
            lo = 128 * r + j
            m[j, r * 512 + lo: (r + 1) * 512] = 1.0
    return m


class Ctx:
    def __init__(self, p):
        self.p = p
        self.maskA = p.sb("c_maskA", [128, 128], BF16)
        self.ident = p.sb("c_ident", [128, 128], BF16)
        self.maskLT = p.sb("c_maskLT", [64, 64], BF16)
        self.ones_bf = p.sb("c_ones_bf", [128, 1], BF16)
        self.onesF = p.sb("c_onesF", [128, 512], F32)
        self.identF = p.sb("c_identF", [128, 128], F32)
        self.maskC4 = p.sb("c_maskC4", [128, 4 * 512], BF16)
        for nm, dst, shape, dt in (
                ("d_maskA", self.maskA, [128, 128], BF16),
                ("d_ident", self.ident, [128, 128], BF16),
                ("d_maskLT", self.maskLT, [64, 64], BF16),
                ("d_onesb", self.ones_bf, [128, 1], BF16),
                ("d_onesF", self.onesF, [128, 512], F32),
                ("d_identF", self.identF, [128, 128], F32),
                ("d_maskC4", self.maskC4, [128, 4 * 512], BF16)):
            src = p.dram(nm, shape, dt, kind="ExternalInput")
            p.emit("sync", lambda e, dst=dst, src=src: e.dma_start(
                out=dst[:], in_=src), inc="qx", inc_by=16)
        self.m_const = p.val("qx")

        self.xbf = p.sb("xbf", [128, CT * T], BF16)
        self.res = p.sb("resbuf", [128, CT * 512], F32)
        self.xstage = p.sb("xstage", [128, CT * 512], F32)
        self.xsp = [p.dram("xspill0", [C, T], F32), p.dram("xspill1", [C, T], F32)]
        self.xsp_t = [a.rearrange("(a p) t -> a p t", p=128) for a in self.xsp]
        self.spill_sel = 0

        self.xsrc_t = None
        self.m_xsrc = 0
        self.xbf_wait = ()
        self.m_xbf_users_t = 0
        self.m_xstage_users_v = [0, 0]

        self.ln_resb = p.sb("ln_resb", [128, CT * 512], BF16)
        self.ln_sq = p.sb("ln_sq", [128, CT * 512], BF16)
        self.ln_rows = p.sb("ln_rows", [1, 6 * 512], F32)
        self.ln_arep = p.sb("ln_arep", [128, 512], F32)
        self.ln_brep = p.sb("ln_brep", [128, 512], F32)
        self.ln_xs = p.sb("ln_xs", [128, 2 * 512], F32)
        self.m_lnxs_war = [0, 0]

    def xbf_tile(self, ct):
        return self.xbf[:, ct * T:(ct + 1) * T]


def emit_load_x(p, ctx, x_in_ap):
    x_t = x_in_ap.rearrange("(a p) t -> a p t", p=128)
    for ct in range(CT):
        p.emit("gpsimd", lambda e, ct=ct: e.dma_start(
            out=ctx.xbf_tile(ct), in_=x_t[ct]), inc="qw", inc_by=16)
    ctx.xbf_wait = (("qw", p.val("qw")),)
    ctx.xsrc_t = x_t
    ctx.m_xsrc = 0


class OutLN:

    def __init__(self, p, ctx, tag):
        self.p, self.ctx, self.tag = p, ctx, tag
        self.sel = ctx.spill_sel ^ 1
        self.pstack = ExitStack()
        PS = lambda nm, shape: self.pstack.enter_context(
            p.nc.psum_tensor(p._n(nm), shape, F32))
        self.r_out = Rot([PS("oln", [128, 512]), PS("oln", [128, 512])])
        self.ps_stat = PS("lnst", [128, 512])
        self.r_rep = self.r_out
        self.m_guard_v = p.val("v")
        self.m_guard_s = p.val("s")
        self.first_mm = True
        self.m_sums_prev = 0
        self.m_rowmath_prev = 0
        self.m_res_reads_prev = 0
        self.m_stage_war_v = 0
        self.sp_sh = {}
        self.done = 0
        self.stage_extra = []

    def chunk(self, tc, emitter, scale_cols=None, stage_mult=1.0):
        p, ctx = self.p, self.ctx
        resb, sq, R = ctx.ln_resb, ctx.ln_sq, ctx.ln_rows
        arep, brep, xs = ctx.ln_arep, ctx.ln_brep, ctx.ln_xs
        S5 = 512

        stg = {}
        for ot in range(CT):
            w = list(self.stage_extra)
            if self.m_stage_war_v:
                w.append(("v", self.m_stage_war_v))
            if ctx.m_xsrc:
                w.append(("qx", ctx.m_xsrc))
            src_ap = ctx.xsrc_t[ot][:, tc * S5:(tc + 1) * S5]
            stg[ot] = p.emit("sync", lambda e, ot=ot, src_ap=src_ap: e.dma_start(
                out=ctx.xstage[:, ot * S5:(ot + 1) * S5], in_=src_ap),
                waits=tuple(w), inc="qx", inc_by=16)

        for ot in range(CT):
            bi, bank, war = self.r_out.take()
            w = list(war)
            if self.first_mm:
                w += [("v", self.m_guard_v), ("s", self.m_guard_s)]
                self.first_mm = False
            if self.done > 0 and ot == 0:
                w.append(("s", self.m_res_reads_prev))
            mt = emitter(tc, ot, bank, tuple(w))
            if stage_mult != 1.0:
                assert scale_cols is None
                mv = p.emit("vector", lambda e, ot=ot, bank=bank, sm=stage_mult:
                            e.scalar_tensor_tensor(
                    ctx.res[:, ot * S5:(ot + 1) * S5],
                    ctx.xstage[:, ot * S5:(ot + 1) * S5], sm, bank[:],
                    ALU.mult, ALU.add),
                    waits=(("t", mt), ("qx", stg[ot])), inc="v")
            elif scale_cols is not None:
                sc = scale_cols(ot)
                mv = p.emit("vector", lambda e, ot=ot, bank=bank, sc=sc:
                            e.scalar_tensor_tensor(
                    ctx.res[:, ot * S5:(ot + 1) * S5], bank[:], sc,
                    ctx.xstage[:, ot * S5:(ot + 1) * S5],
                    ALU.mult, ALU.add),
                    waits=(("t", mt), ("qx", stg[ot])), inc="v")
            else:
                mv = p.emit("vector", lambda e, ot=ot, bank=bank: e.tensor_add(
                    ctx.res[:, ot * S5:(ot + 1) * S5], bank[:],
                    ctx.xstage[:, ot * S5:(ot + 1) * S5]),
                    waits=(("t", mt), ("qx", stg[ot])), inc="v")
            self.r_out.release(bi, "v", mv)
        m_res_v = p.val("v")
        self.m_stage_war_v = m_res_v

        for ct in range(CT):
            w = [("v", m_res_v)] if ct == 0 else []
            if self.done > 0 and ct == 0:
                w.append(("t", self.m_sums_prev))
            p.emit("scalar", lambda e, ct=ct: e.copy(
                resb[:, ct * S5:(ct + 1) * S5],
                ctx.res[:, ct * S5:(ct + 1) * S5]), waits=tuple(w), inc="s")
            p.emit("scalar", lambda e, ct=ct: e.activation(
                sq[:, ct * S5:(ct + 1) * S5], resb[:, ct * S5:(ct + 1) * S5],
                AF.Square), inc="s")
        m_sq = p.val("s")
        self.m_res_reads_prev = m_sq

        for which, (srcb, rowb) in enumerate(((resb, 0), (sq, 64))):
            for ct in range(CT):
                w = []
                if which == 0 and ct == 0:
                    w.append(("s", m_sq))
                    if self.m_rowmath_prev:
                        w.append(("v", self.m_rowmath_prev))
                p.emit("tensor", lambda e, srcb=srcb, rowb=rowb, ct=ct:
                       e.matmul(self.ps_stat[rowb:rowb + 1, :], ctx.ones_bf[:],
                                srcb[:, ct * S5:(ct + 1) * S5],
                                start=(ct == 0), stop=(ct == CT - 1)),
                       waits=tuple(w), inc="t" if ct == CT - 1 else None)
        self.m_sums_prev = p.val("t")

        p.emit("vector", lambda e: e.tensor_scalar_mul(
            R[:, 2 * S5:3 * S5], self.ps_stat[0:1, :], 1.0 / C),
            waits=(("t", self.m_sums_prev),), inc="v")
        p.emit("vector", lambda e: e.tensor_scalar_mul(
            R[:, 3 * S5:4 * S5], self.ps_stat[64:65, :], 1.0 / C), inc="v")
        self.m_rowmath_prev = p.val("v")
        p.emit("vector", lambda e: e.tensor_mul(
            R[:, 0:S5], R[:, 2 * S5:3 * S5], R[:, 2 * S5:3 * S5]), inc="v")
        p.emit("vector", lambda e: e.tensor_sub(
            R[:, S5:2 * S5], R[:, 3 * S5:4 * S5], R[:, 0:S5]), inc="v")
        p.emit("vector", lambda e: e.tensor_scalar_add(
            R[:, S5:2 * S5], R[:, S5:2 * S5], 1e-5), inc="v")
        m_var = p.val("v")
        p.emit("scalar", lambda e: e.activation(
            R[:, 3 * S5:4 * S5], R[:, S5:2 * S5], AF.Sqrt),
            waits=(("v", m_var),), inc="s")
        m_sqrt = p.val("s")
        p.emit("vector", lambda e: e.reciprocal(
            R[:, 4 * S5:5 * S5], R[:, 3 * S5:4 * S5]),
            waits=(("s", m_sqrt),), inc="v")
        p.emit("vector", lambda e: e.tensor_mul(
            R[:, 5 * S5:6 * S5], R[:, 2 * S5:3 * S5], R[:, 4 * S5:5 * S5]),
            inc="v")
        p.emit("vector", lambda e: e.tensor_scalar_mul(
            R[:, 5 * S5:6 * S5], R[:, 5 * S5:6 * S5], -1.0), inc="v")
        m_ab = p.val("v")

        for which, dst in ((4, arep), (5, brep)):
            bi, bank, war = self.r_rep.take()
            w = list(war)
            if which == 4:
                w.append(("v", m_ab))
            mt = p.emit("tensor", lambda e, which=which, bank=bank: e.matmul(
                bank[:], ctx.onesF[0:1, 0:128], R[:, which * S5:(which + 1) * S5],
                start=True, stop=True), waits=tuple(w), inc="t")
            mv = p.emit("vector", lambda e, dst=dst, bank=bank: e.tensor_copy(
                dst[:], bank[:]), waits=(("t", mt),), inc="v")
            self.r_rep.release(bi, "v", mv)
        m_rep = p.val("v")

        for ct in range(CT):
            h = ct % 2
            off = h * S5
            w = [("v", m_rep)] if ct == 0 else []
            if ct >= 2:
                pm = self.sp_sh[(ct - 2, self.done)]
                w += [("qx", pm[0]), ("s", pm[1])]
            elif self.done > 0:
                pm = self.sp_sh[(ct + 6, self.done - 1)]
                w += [("qx", pm[0]), ("s", pm[1])]
            p.emit("vector", lambda e, ct=ct, off=off: e.tensor_mul(
                xs[:, off:off + S5], ctx.res[:, ct * S5:(ct + 1) * S5],
                arep[:]), waits=tuple(w), inc="v")
            p.emit("vector", lambda e, off=off: e.tensor_add(
                xs[:, off:off + S5], xs[:, off:off + S5], brep[:]), inc="v")
            mv = p.val("v")
            msh = p.emit("scalar", lambda e, ct=ct, tc=tc, off=off: e.copy(
                ctx.xbf[:, ct * T + tc * S5: ct * T + (tc + 1) * S5],
                xs[:, off:off + S5]),
                waits=(("v", mv), ("t", ctx.m_xbf_users_t))
                if (ct == 0 and self.done == 0) else (("v", mv),), inc="s")
            msp = p.emit("sync", lambda e, ct=ct, tc=tc, off=off,
                         sel=self.sel: e.dma_start(
                out=ctx.xsp_t[sel][ct][:, tc * S5:(tc + 1) * S5],
                in_=xs[:, off:off + S5]),
                waits=(("v", mv),), inc="qx", inc_by=16)
            self.sp_sh[(ct, self.done)] = (msp, msh)
        self.done += 1

    def end(self):
        p, ctx = self.p, self.ctx
        assert self.done == NTC
        self.pstack.close()
        ctx.spill_sel = self.sel
        ctx.xsrc_t = ctx.xsp_t[self.sel]
        ctx.m_xsrc = p.val("qx")
        ctx.xbf_wait = (("s", p.val("s")),)
        ctx.m_xbf_users_t = 0


def emit_out_ln(p, ctx, emitter, tag, scale_cols=None):
    ol = OutLN(p, ctx, tag)
    for tc in range(NTC):
        ol.chunk(tc, emitter, scale_cols=scale_cols)
    ol.end()


def emit_store_out(p, ctx, out_ap):
    src = ctx.xsrc_t
    m = ctx.m_xsrc
    out_t = out_ap.rearrange("(a p) t -> a p t", p=128)
    for ct in range(CT):
        p.emit("sync", lambda e, ct=ct: e.dma_start(
            out=out_t[ct], in_=src[ct]), waits=(("qx", m),), inc="qx", inc_by=16)
    return p.val("qx")


def prep_l0_attn(inputs):
    d = {}
    d["l0_wqT"] = lhsT_blocked(np.ascontiguousarray(inputs["L0_attn_Wq"].T))
    d["l0_wkT"] = lhsT_blocked(np.ascontiguousarray(inputs["L0_attn_Wk"].T))
    d["l0_wv"] = rhs_blocked(np.asarray(inputs["L0_attn_Wv"]))
    d["l0_wcT"] = lhsT_flat(np.ascontiguousarray(inputs["L0_attn_Wc"].T))
    lt = np.asarray(inputs["L0_attn_Ltri"]).transpose(0, 2, 1)
    d["l0_ltriT"] = np.ascontiguousarray(
        lt.transpose(1, 0, 2).reshape(64, 16 * 64))
    return d


def emit_l0_attn(p, ctx, debug_yt=None, npairs=8, do_chunks=True):
    import os as _os
    SKIP = set(_os.environ.get("L0SKIP", "").split(","))
    NH = 16
    NCH = NTT

    d_wq = p.dram("l0_wqT", [8, 128, 8 * 128], F32, kind="ExternalInput")
    d_wk = p.dram("l0_wkT", [8, 128, 8 * 128], F32, kind="ExternalInput")
    d_wv = p.dram("l0_wv", [8, 128, C], F32, kind="ExternalInput")
    d_wc = p.dram("l0_wcT", [128, 8 * 1024], F32, kind="ExternalInput")
    d_lt = p.dram("l0_ltriT", [64, NH * 64], F32, kind="ExternalInput")

    wq = p.sb("l0_wq", [128, 2 * 1024], BF16)
    wk = p.sb("l0_wk", [128, 2 * 1024], BF16)
    wv = p.sb("l0_wvn", [128, 2 * 1024], BF16)
    wvt = p.sb("l0_wvt", [128, 2 * 1024], BF16)
    wc = p.sb("l0_wc", [128, 8 * 1024], BF16)
    ltT = p.sb("l0_ltT", [64, NH * 64], BF16)
    lmT = p.sb("l0_lmT", [64, 2 * 64], BF16)
    bm = p.sb("l0_bm", [128, 64], BF16)
    q2 = p.sb("l0_q2", [128, T], BF16)
    k2 = p.sb("l0_k2", [128, T], BF16)
    vt2 = p.sb("l0_vt2", [128, NTT * 128], BF16)
    asb = p.sb("l0_asb", [128, 2 * 128], BF16)
    ktk = p.sb("l0_ktk", [128, 2 * 64], BF16)
    s_f = p.sb("l0_sf", [128, 64], F32)
    s_b = p.sb("l0_sb", [128, 64], BF16)
    yt = p.sb("l0_yt", [128, CT * T], BF16)

    with ExitStack() as pstack:
        PS = lambda nm, shape, dt=F32: pstack.enter_context(
            p.nc.psum_tensor(p._n(nm), shape, dt))
        r_proj = Rot([PS("l0p", [128, 512]), PS("l0p", [128, 512])])
        r_sc = Rot([PS("l0sc", [128, 128]), PS("l0sc", [128, 128])])
        r_y = Rot([PS("l0y", [64, 128]), PS("l0y", [64, 128])])
        ps_su = PS("l0su", [128, 64])
        ps_kt = PS("l0kt", [128, 64], BF16)
        m_su_read_v = 0
        m_kt_read_s = 0

        xw = list(ctx.xbf_wait)

        m_lt = p.emit("gpsimd", lambda e: e.dma_start(
            out=ltT[:], in_=d_lt), inc="qw", inc_by=16)

        wq_free_t = [0, 0]
        wk_free_t = [0, 0]
        wv_free_t = [0, 0]
        wq_m, wk_m, wv_m = {}, {}, {}

        def prefetch_pair(hp):
            sl = hp % 2
            wq_m[hp] = p.emit("gpsimd", lambda e, hp=hp, sl=sl: e.dma_start(
                out=wq[:, sl * 1024:(sl + 1) * 1024], in_=d_wq[hp]),
                waits=(("t", wq_free_t[sl]),), inc="qw", inc_by=16)
            wk_m[hp] = p.emit("gpsimd", lambda e, hp=hp, sl=sl: e.dma_start(
                out=wk[:, sl * 1024:(sl + 1) * 1024], in_=d_wk[hp]),
                waits=(("t", wk_free_t[sl]),), inc="qw", inc_by=16)
            wv_m[hp] = p.emit("gpsimd", lambda e, hp=hp, sl=sl: e.dma_start(
                out=wv[:, sl * 1024:(sl + 1) * 1024], in_=d_wv[hp]),
                waits=(("t", wv_free_t[sl]),), inc="qw", inc_by=16)

        prefetch_pair(0)
        if npairs > 1:
            prefetch_pair(1)

        for hp in range(npairs):
            sl = hp % 2
            h0, h1 = 2 * hp, 2 * hp + 1

            for i, h in enumerate((h0, h1) if "bm" not in SKIP else ()):
                w = []
                if hp == 0 and i == 0:
                    w.append(("qw", m_lt))
                    w.append(("qx", ctx.m_const))
                if i == 0:
                    w.append(("t", p.val("t")))
                p.emit("vector", lambda e, i=i, h=h: e.tensor_mul(
                    lmT[:, i * 64:(i + 1) * 64], ltT[:, h * 64:(h + 1) * 64],
                    ctx.maskLT[:]), waits=tuple(w), inc="v")
            m_lm = p.val("v")
            bm_mt = {}
            for i in range(2 if "bm" not in SKIP else 0):
                w = [("v", m_lm)]
                if i == 0 and m_su_read_v:
                    w.append(("v", m_su_read_v))
                if i == 1:
                    w.append(("v", p.val("v")))
                bm_mt[i] = p.emit("tensor", lambda e, i=i: e.matmul(
                    ps_su[i * 64:(i + 1) * 64, :], lmT[:, i * 64:(i + 1) * 64],
                    lmT[:, i * 64:(i + 1) * 64], start=True, stop=True),
                    waits=tuple(w), inc="t")
                p.emit("vector", lambda e, i=i: e.tensor_copy(
                    bm[i * 64:(i + 1) * 64, :], ps_su[i * 64:(i + 1) * 64, :]),
                    waits=(("t", bm_mt[i]),), inc="v")
            m_bm = p.val("v")
            m_su_read_v = m_bm

            m_wv = wv_m[hp]
            for kt in range(CT if "wvt" not in SKIP else 0):
                for i in range(2):
                    bi, bank, war = r_sc.take()
                    w = list(war)
                    if kt == 0 and i == 0:
                        w += [("qw", m_wv), ("v", m_bm)]
                    mt = p.emit("tensor", lambda e, kt=kt, i=i, sl=sl, bank=bank:
                                e.matmul(
                        bank[:, 0:64],
                        wv[(i * 64):(i + 1) * 64,
                           sl * 1024 + kt * 128: sl * 1024 + (kt + 1) * 128],
                        bm[(i * 64):(i + 1) * 64, :],
                        start=True, stop=True),
                        waits=tuple(w), inc="t")
                    ms = p.emit("scalar", lambda e, kt=kt, i=i, sl=sl, bank=bank:
                                e.copy(
                        wvt[:, sl * 1024 + kt * 128 + i * 64:
                            sl * 1024 + kt * 128 + (i + 1) * 64],
                        bank[:, 0:64]), waits=(("t", mt),), inc="s")
                    r_sc.release(bi, "s", ms)
            m_wvt = p.val("s")
            wv_free_t[sl] = p.val("t")

            for which, (slab, dst, wm) in enumerate(
                    (((wq, q2, wq_m[hp]), (wk, k2, wk_m[hp]))) if "proj" not in SKIP else ()):
                for tc in range(NTC):
                    bi, bank, war = r_proj.take()
                    for kt in range(CT):
                        w = list(war) if kt == 0 else []
                        if tc == 0 and kt == 0:
                            w += [("qw", wm)] + xw
                            if which == 0 and hp == 0:
                                pass
                        mt = p.emit("tensor", lambda e, tc=tc, kt=kt, sl=sl,
                                    slab=slab, bank=bank: e.matmul(
                            bank[:], slab[:, sl * 1024 + kt * 128:
                                          sl * 1024 + (kt + 1) * 128],
                            ctx.xbf[:, kt * T + tc * 512: kt * T + (tc + 1) * 512],
                            start=(kt == 0), stop=(kt == CT - 1)),
                            waits=tuple(w), inc="t" if kt == CT - 1 else None)
                    ms = p.emit("scalar", lambda e, tc=tc, dst=dst, bank=bank:
                                e.copy(dst[:, tc * 512:(tc + 1) * 512], bank[:]),
                                waits=(("t", mt),), inc="s")
                    r_proj.release(bi, "s", ms)
            m_qk = p.val("s")
            wq_free_t[sl] = p.val("t")
            wk_free_t[sl] = p.val("t")
            if hp + 2 < npairs:
                prefetch_pair(hp + 2)

            for tt in range(NTT if "vt2" not in SKIP else 0):
                bi, bank, war = r_proj.take()
                for kt in range(CT):
                    w = list(war) if kt == 0 else []
                    if tt == 0 and kt == 0:
                        w += [("s", m_wvt)] + xw
                    mt = p.emit("tensor", lambda e, tt=tt, kt=kt, sl=sl, bank=bank:
                                e.matmul(
                        bank[:, 0:128],
                        ctx.xbf[:, kt * T + tt * 128: kt * T + (tt + 1) * 128],
                        wvt[:, sl * 1024 + kt * 128: sl * 1024 + (kt + 1) * 128],
                        start=(kt == 0), stop=(kt == CT - 1)),
                        waits=tuple(w), inc="t" if kt == CT - 1 else None)
                ms = p.emit("scalar", lambda e, tt=tt, bank=bank: e.copy(
                    vt2[:, tt * 128:(tt + 1) * 128], bank[:, 0:128]),
                    waits=(("t", mt),), inc="s")
                r_proj.release(bi, "s", ms)
            m_vt = p.val("s")
            ctx.m_xbf_users_t = max(ctx.m_xbf_users_t, p.val("t"))

            w = [("t", p.val("t"))]
            if m_su_read_v:
                w.append(("v", m_su_read_v))
            p.emit("vector", lambda e: e.memset(s_f[:], 0.0), waits=tuple(w), inc="v")
            p.emit("vector", lambda e: e.memset(s_b[:], 0.0), inc="v")
            m_sbf = p.val("v")

            m_y_ev = {}
            for cc in range(NCH if do_chunks else 0):
                for i, h in enumerate((h0, h1)):
                    hrow = slice(i * 64, (i + 1) * 64)
                    tsl = slice(cc * 128, (cc + 1) * 128)
                    bi_sc, bank_sc, war_sc = r_sc.take()
                    w = list(war_sc)
                    if cc == 0 and i == 0:
                        w.append(("s", m_vt))
                    mt_sc = p.emit("tensor", lambda e, hrow=hrow, tsl=tsl,
                                   bank=bank_sc: e.matmul(
                        bank[:, 0:128], k2[hrow, tsl], q2[hrow, tsl],
                        start=True, stop=True), waits=tuple(w), inc="t")
                    w = [("s", m_kt_read_s)] if m_kt_read_s else []
                    mt_kt = p.emit("tensor", lambda e, hrow=hrow, tsl=tsl, i=i:
                                   e.transpose(ps_kt[:], k2[hrow, tsl],
                                               ctx.ident[i * 64:(i + 1) * 64,
                                                         i * 64:(i + 1) * 64]),
                                   waits=tuple(w), inc="t")
                    m_asb = p.emit("vector", lambda e, i=i, bank=bank_sc:
                                   e.tensor_mul(asb[:, i * 128:(i + 1) * 128],
                                                bank[:, 0:128], ctx.maskA[:]),
                                   waits=(("t", mt_sc),), inc="v")
                    r_sc.release(bi_sc, "v", m_asb)
                    m_kt_read_s = p.emit("scalar", lambda e, i=i: e.copy(
                        ktk[:, i * 64:(i + 1) * 64], ps_kt[:]),
                        waits=(("t", mt_kt),), inc="s")
                    bi_y, bank_y, war_y = r_y.take()
                    w = list(war_y) + [("v", m_asb), ("v", m_sbf)]
                    p.emit("tensor", lambda e, cc=cc, i=i, bank=bank_y: e.matmul(
                        bank[:], vt2[:, cc * 128 + i * 64: cc * 128 + (i + 1) * 64],
                        asb[:, i * 128:(i + 1) * 128],
                        start=True, stop=False), waits=tuple(w))
                    mt_y = p.emit("tensor", lambda e, hrow=hrow, tsl=tsl, i=i,
                                  bank=bank_y: e.matmul(
                        bank[:], s_b[i * 64:(i + 1) * 64, :], q2[hrow, tsl],
                        start=False, stop=True), inc="t")
                    w = [("s", m_kt_read_s), ("v", m_sbf)]
                    if m_su_read_v:
                        w.append(("v", m_su_read_v))
                    mt_su = p.emit("tensor", lambda e, cc=cc, i=i: e.matmul(
                        ps_su[i * 64:(i + 1) * 64, :],
                        ktk[:, i * 64:(i + 1) * 64],
                        vt2[:, cc * 128 + i * 64: cc * 128 + (i + 1) * 64],
                        start=True, stop=True), waits=tuple(w), inc="t")
                    ct_y, prow = h // 2, (h % 2) * 64
                    m_y = p.emit("scalar", lambda e, ct_y=ct_y, prow=prow, cc=cc,
                                 bank=bank_y: e.copy(
                        yt[prow:prow + 64,
                           ct_y * T + cc * 128: ct_y * T + (cc + 1) * 128],
                        bank[:]), waits=(("t", mt_y),), inc="s")
                    r_y.release(bi_y, "s", m_y)
                    p.emit("vector", lambda e, i=i: e.tensor_add(
                        s_f[i * 64:(i + 1) * 64, :], s_f[i * 64:(i + 1) * 64, :],
                        ps_su[i * 64:(i + 1) * 64, :]),
                        waits=(("t", mt_su),), inc="v")
                    m_su_read_v = p.val("v")
                    m_sbf = p.emit("vector", lambda e, i=i: e.tensor_copy(
                        s_b[i * 64:(i + 1) * 64, :],
                        s_f[i * 64:(i + 1) * 64, :]), inc="v")
        m_yt = p.val("s")
        if debug_yt is not None:
            yt_t = debug_yt.rearrange("(a p) t -> a p t", p=128)
            for ct in range(CT):
                p.emit("sync", lambda e, ct=ct: e.dma_start(
                    out=yt_t[ct], in_=yt[:, ct * T:(ct + 1) * T]),
                    waits=(("s", m_yt),), inc="qx", inc_by=16)
            p.wait("sync", (("qx", p.val("qx")),))
            return
        m_wc = p.emit("gpsimd", lambda e: e.dma_start(
            out=wc[:], in_=d_wc), inc="qw", inc_by=16)

    def wc_chunk(tc, ot, bank, waits):
        mt = None
        for kt in range(CT):
            w = list(waits) if kt == 0 else []
            if tc == 0 and ot == 0 and kt == 0:
                w += [("qw", m_wc), ("s", m_yt)]
            mt = p.emit("tensor", lambda e, tc=tc, ot=ot, kt=kt, bank=bank:
                        e.matmul(
                bank[:], wc[:, ot * 1024 + kt * 128: ot * 1024 + (kt + 1) * 128],
                yt[:, kt * T + tc * 512: kt * T + (tc + 1) * 512],
                start=(kt == 0), stop=(kt == CT - 1)),
                waits=tuple(w), inc="t" if kt == CT - 1 else None)
        return mt

    emit_out_ln(p, ctx, wc_chunk, "l0a")

def prep_l0_ffn(inputs):
    d = {}
    d["l0_w1T"] = lhsT_blocked(np.ascontiguousarray(np.asarray(inputs["L0_ffn_w1_w"]).T))
    d["l0_w2T"] = lhsT_blocked(np.ascontiguousarray(np.asarray(inputs["L0_ffn_w2_w"]).T))
    d["l0_h1T"] = lhsT_blocked(np.ascontiguousarray(np.asarray(inputs["L0_ffn_h1_w"]).T))
    d["l0_h2T"] = lhsT_blocked(np.ascontiguousarray(np.asarray(inputs["L0_ffn_h2_w"]).T))
    d["l0_w3T"] = lhsT_blocked(np.ascontiguousarray(np.asarray(inputs["L0_ffn_w3_w"]).T))
    sc = np.zeros((128, 4 * 32 + 8), np.float32)
    for i, nm in enumerate(("w1_s", "w2_s", "h1_s", "h2_s")):
        sc[:, i * 32:(i + 1) * 32] = np.asarray(inputs[f"L0_ffn_{nm}"]).reshape(32, 128).T
    sc[:, 128:136] = np.asarray(inputs["L0_ffn_w3_s"]).reshape(8, 128).T
    d["l0_fsc"] = np.ascontiguousarray(sc)
    return d


def emit_l0_ffn(p, ctx):
    HT = 32
    S5 = 512
    d_w1 = p.dram("l0_w1T", [HT, 128, 8 * 128], F32, kind="ExternalInput")
    d_w2 = p.dram("l0_w2T", [HT, 128, 8 * 128], F32, kind="ExternalInput")
    d_h1 = p.dram("l0_h1T", [HT, 128, 32 * 128], F32, kind="ExternalInput")
    d_h2 = p.dram("l0_h2T", [HT, 128, 32 * 128], F32, kind="ExternalInput")
    d_w3 = p.dram("l0_w3T", [CT, 128, 32 * 128], F32, kind="ExternalInput")
    d_sc = p.dram("l0_fsc", [128, 136], F32, kind="ExternalInput")

    hbuf = p.sb("l0f_h", [128, HT * S5], BF16)
    gbuf = p.sb("l0f_g", [128, HT * S5], BF16)
    t1s = p.sb("l0f_t1", [128, 2 * S5], BF16)
    w12 = p.sb("l0f_w12", [128, 4 * 1024], BF16)
    hsl = p.sb("l0f_hsl", [128, 2 * 4096], BF16)
    w3s = p.sb("l0f_w3s", [128, 4096], BF16)
    scs = p.sb("l0f_sc", [128, 136], F32)

    m_sc = p.emit("sync", lambda e: e.dma_start(out=scs[:], in_=d_sc),
                  inc="qx", inc_by=16)
    xw = list(ctx.xbf_wait)

    ol = OutLN(p, ctx, "l0f")
    with ExitStack() as pstack:
        PS = lambda nm, shape: pstack.enter_context(
            p.nc.psum_tensor(p._n(nm), shape, F32))
        rf = Rot([PS("l0f", [128, S5]) for _ in range(4)])

        w12_free = [0, 0]
        hsl_free = [0, 0]
        w3s_free = [0]
        w12_m = {}
        hsl_m = {}
        w3s_m = {}

        def pf_w12(q, ot):
            sl = ot % 2
            w12_m[(q, ot)] = (
                p.emit("gpsimd", lambda e, ot=ot, sl=sl: e.dma_start(
                    out=w12[:, sl * 2048:sl * 2048 + 1024], in_=d_w1[ot]),
                    waits=(("t", w12_free[sl]),), inc="qw", inc_by=16),
                p.emit("gpsimd", lambda e, ot=ot, sl=sl: e.dma_start(
                    out=w12[:, sl * 2048 + 1024:sl * 2048 + 2048], in_=d_w2[ot]),
                    inc="qw", inc_by=16))

        def pf_h(q, ot, dsrc, store):
            sl = ot % 2
            store[(q, ot)] = p.emit("gpsimd", lambda e, ot=ot, sl=sl, dsrc=dsrc:
                                    e.dma_start(
                out=hsl[:, sl * 4096:(sl + 1) * 4096], in_=dsrc[ot]),
                waits=(("t", hsl_free[sl]),), inc="qw", inc_by=16)

        def pf_w3(q, ot):
            w3s_m[(q, ot)] = p.emit("gpsimd", lambda e, ot=ot: e.dma_start(
                out=w3s[:], in_=d_w3[ot]),
                waits=(("t", w3s_free[0]),), inc="qw", inc_by=16)

        m_h_done = 0
        m_g_done = 0
        t1_war = [0, 0]
        for q in range(NTC):
            pf_w12(q, 0)
            pf_w12(q, 1)
            for ot in range(HT):
                sl = ot % 2
                bi1, b1, war1 = rf.take()
                bi2, b2, war2 = rf.take()
                for kt in range(CT):
                    w = list(war1) if kt == 0 else []
                    if kt == 0:
                        w.append(("qw", w12_m[(q, ot)][0]))
                        if q == 0 and ot == 0:
                            w += xw
                    mt1 = p.emit("tensor", lambda e, q=q, ot=ot, kt=kt, sl=sl,
                                 b1=b1: e.matmul(
                        b1[:], w12[:, sl * 2048 + kt * 128:sl * 2048 + (kt + 1) * 128],
                        ctx.xbf[:, kt * T + q * S5: kt * T + (q + 1) * S5],
                        start=(kt == 0), stop=(kt == CT - 1)),
                        waits=tuple(w), inc="t" if kt == CT - 1 else None)
                for kt in range(CT):
                    w = list(war2) if kt == 0 else []
                    if kt == 0:
                        w.append(("qw", w12_m[(q, ot)][1]))
                    mt2 = p.emit("tensor", lambda e, q=q, ot=ot, kt=kt, sl=sl,
                                 b2=b2: e.matmul(
                        b2[:], w12[:, sl * 2048 + 1024 + kt * 128:
                                   sl * 2048 + 1024 + (kt + 1) * 128],
                        ctx.xbf[:, kt * T + q * S5: kt * T + (q + 1) * S5],
                        start=(kt == 0), stop=(kt == CT - 1)),
                        waits=tuple(w), inc="t" if kt == CT - 1 else None)
                w12_free[sl] = p.val("t")
                if ot + 2 < HT:
                    pf_w12(q, ot + 2)
                tsl = ot % 2
                w = [("t", mt1), ("v", t1_war[tsl])]
                if q == 0 and ot == 0:
                    w.append(("qx", m_sc))
                ms1 = p.emit("scalar", lambda e, ot=ot, b1=b1, tsl=tsl: e.activation(
                    t1s[:, tsl * S5:(tsl + 1) * S5], b1[:], AF.Silu,
                    scale=scs[:, ot:ot + 1]),
                    waits=tuple(w), inc="s")
                rf.release(bi1, "s", ms1)
                mh = p.emit("vector", lambda e, ot=ot, b2=b2, tsl=tsl: e.scalar_tensor_tensor(
                    hbuf[:, ot * S5:(ot + 1) * S5], b2[:], scs[:, 32 + ot:33 + ot],
                    t1s[:, tsl * S5:(tsl + 1) * S5], ALU.mult, ALU.mult),
                    waits=(("t", mt2), ("s", ms1)), inc="v")
                t1_war[tsl] = mh
                rf.release(bi2, "v", mh)
            m_h_done = p.val("v")

            pf_h(q, 0, d_h1, hsl_m)
            pf_h(q, 1, d_h1, hsl_m)
            for ot in range(HT):
                sl = ot % 2
                bi, bank, war = rf.take()
                for kt in range(HT):
                    w = list(war) if kt == 0 else []
                    if kt == 0:
                        w.append(("qw", hsl_m[(q, ot)]))
                        if ot == 0:
                            w.append(("v", m_h_done))
                    mt = p.emit("tensor", lambda e, ot=ot, kt=kt, sl=sl,
                                bank=bank: e.matmul(
                        bank[:], hsl[:, sl * 4096 + kt * 128:
                                     sl * 4096 + (kt + 1) * 128],
                        hbuf[:, kt * S5:(kt + 1) * S5],
                        start=(kt == 0), stop=(kt == HT - 1)),
                        waits=tuple(w), inc="t" if kt == HT - 1 else None)
                hsl_free[sl] = p.val("t")
                if ot + 2 < HT:
                    pf_h(q, ot + 2, d_h1, hsl_m)
                ms = p.emit("scalar", lambda e, ot=ot, bank=bank: e.activation(
                    gbuf[:, ot * S5:(ot + 1) * S5], bank[:], AF.Gelu,
                    scale=scs[:, 64 + ot:65 + ot]),
                    waits=(("t", mt),), inc="s")
                rf.release(bi, "s", ms)
            m_g_done = p.val("s")

            hsl_m2 = {}
            pf_h(q, 0, d_h2, hsl_m2)
            pf_h(q, 1, d_h2, hsl_m2)
            for ot in range(HT):
                sl = ot % 2
                bi, bank, war = rf.take()
                for kt in range(HT):
                    w = list(war) if kt == 0 else []
                    if kt == 0:
                        w.append(("qw", hsl_m2[(q, ot)]))
                        if ot == 0:
                            w.append(("s", m_g_done))
                    mt = p.emit("tensor", lambda e, ot=ot, kt=kt, sl=sl,
                                bank=bank: e.matmul(
                        bank[:], hsl[:, sl * 4096 + kt * 128:
                                     sl * 4096 + (kt + 1) * 128],
                        gbuf[:, kt * S5:(kt + 1) * S5],
                        start=(kt == 0), stop=(kt == HT - 1)),
                        waits=tuple(w), inc="t" if kt == HT - 1 else None)
                hsl_free[sl] = p.val("t")
                if ot + 2 < HT:
                    pf_h(q, ot + 2, d_h2, hsl_m2)
                ms = p.emit("scalar", lambda e, ot=ot, bank=bank: e.activation(
                    hbuf[:, ot * S5:(ot + 1) * S5], bank[:], AF.Gelu,
                    scale=scs[:, 96 + ot:97 + ot]),
                    waits=(("t", mt),), inc="s")
                rf.release(bi, "s", ms)
            m_g2_done = p.val("s")

            pf_w3(q, 0)

            def w3_emitter(tc, ot, bank, waits, q=q, m_g2_done=m_g2_done):
                mt = None
                for kt in range(HT):
                    w = list(waits) if kt == 0 else []
                    if kt == 0:
                        w.append(("qw", w3s_m[(q, ot)]))
                        if ot == 0:
                            w.append(("s", m_g2_done))
                    mt = p.emit("tensor", lambda e, ot=ot, kt=kt,
                                bank=bank: e.matmul(
                        bank[:], w3s[:, kt * 128:(kt + 1) * 128],
                        hbuf[:, kt * S5:(kt + 1) * S5],
                        start=(kt == 0), stop=(kt == HT - 1)),
                        waits=tuple(w), inc="t" if kt == HT - 1 else None)
                w3s_free[0] = p.val("t")
                if ot + 1 < CT:
                    pf_w3(q, ot + 1)
                return mt

            ol.chunk(q, w3_emitter, scale_cols=lambda ot: scs[:, 128 + ot:129 + ot])
    ol.end()


def prep_l1_attn(inputs):
    d = {}
    d["l1_wq"] = lhsT_blocked(np.ascontiguousarray(np.asarray(inputs["L1_attn_Wq"])))
    d["l1_wk"] = lhsT_blocked(np.ascontiguousarray(np.asarray(inputs["L1_attn_Wk"])), mtile=64)
    d["l1_wv"] = np.ascontiguousarray(
        rhs_blocked(np.asarray(inputs["L1_attn_Wv"])).transpose(1, 0, 2).reshape(128, CT * 64))
    d["l1_wo"] = lhsT_blocked(np.ascontiguousarray(np.asarray(inputs["L1_attn_Wo"])))
    d["l1_bo"] = np.ascontiguousarray(np.asarray(inputs["L1_attn_bo"]).reshape(1, C))
    return d


def emit_l1_attn(p, ctx):
    NH = 16
    S5 = 512
    d_wq = p.dram("l1_wq", [8, 128, 8 * 128], F32, kind="ExternalInput")
    d_wk = p.dram("l1_wk", [1, 128, 8 * 64], F32, kind="ExternalInput")
    d_wv = p.dram("l1_wv", [128, CT * 64], F32, kind="ExternalInput")
    d_wo = p.dram("l1_wo", [8, 128, 8 * 128], F32, kind="ExternalInput")
    d_bo = p.dram("l1_bo", [1, C], F32, kind="ExternalInput")

    qT = p.sb("l1_qT", [128, CT * T], BF16)
    k2 = p.sb("l1_k2", [128, T], BF16)
    vext = p.sb("l1_vx", [128, NTT * 65], BF16)
    att = p.sb("l1_att", [128, CT * T], BF16)
    pT = p.sb("l1_pT", [128, 3 * S5], BF16)
    praw = p.sb("l1_praw", [128, S5], BF16)
    zrep = p.sb("l1_zrep", [128, S5], F32)
    zr = p.sb("l1_zr", [1, S5], F32)
    wsl = p.sb("l1_wsl", [128, 2 * 1024], BF16)
    wksl = p.sb("l1_wksl", [128, 512], BF16)
    wvsl = p.sb("l1_wvsl", [128, CT * 64], BF16)
    bo_sb = p.sb("l1_bo_sb", [1, C], F32)

    xw = list(ctx.xbf_wait)

    with ExitStack() as pstack:
        PS = lambda nm, shape: pstack.enter_context(
            p.nc.psum_tensor(p._n(nm), shape, F32))
        r_st = Rot([PS("l1s", [128, S5]) for _ in range(3)])
        r_o = Rot([PS("l1o", [128, S5]) for _ in range(2)])
        r_z = Rot([PS("l1z", [128, S5]) for _ in range(2)])

        m_wk = p.emit("gpsimd", lambda e: e.dma_start(
            out=wksl[:], in_=d_wk[0]), inc="qw", inc_by=16)
        m_wv = p.emit("gpsimd", lambda e: e.dma_start(
            out=wvsl[:], in_=d_wv), inc="qw", inc_by=16)
        m_bo = p.emit("sync", lambda e: e.dma_start(
            out=bo_sb[:], in_=d_bo), inc="qx", inc_by=16)
        wq_m = {}
        wq_free = [0, 0]

        def pf_wq(ot):
            sl = ot % 2
            wq_m[ot] = p.emit("gpsimd", lambda e, ot=ot, sl=sl: e.dma_start(
                out=wsl[:, sl * 1024:(sl + 1) * 1024], in_=d_wq[ot]),
                waits=(("t", wq_free[sl]),), inc="qw", inc_by=16)

        pf_wq(0)
        pf_wq(1)

        for ot in range(CT):
            sl = ot % 2
            for tc in range(NTC):
                bi, bank, war = r_o.take()
                for kt in range(CT):
                    w = list(war) if kt == 0 else []
                    if kt == 0:
                        w.append(("qw", wq_m[ot]))
                        if ot == 0 and tc == 0:
                            w += xw
                    mt = p.emit("tensor", lambda e, ot=ot, tc=tc, kt=kt, sl=sl,
                                bank=bank: e.matmul(
                        bank[:], wsl[:, sl * 1024 + kt * 128:
                                     sl * 1024 + (kt + 1) * 128],
                        ctx.xbf[:, kt * T + tc * S5: kt * T + (tc + 1) * S5],
                        start=(kt == 0), stop=(kt == CT - 1)),
                        waits=tuple(w), inc="t" if kt == CT - 1 else None)
                ms = p.emit("scalar", lambda e, ot=ot, tc=tc, bank=bank: e.copy(
                    qT[:, ot * T + tc * S5: ot * T + (tc + 1) * S5], bank[:]),
                    waits=(("t", mt),), inc="s")
                r_o.release(bi, "s", ms)
            wq_free[sl] = p.val("t")
            if ot + 2 < CT:
                pf_wq(ot + 2)

        for half in range(2):
            for tc in range(NTC):
                bi, bank, war = r_st.take()
                for kt in range(CT):
                    w = list(war) if kt == 0 else []
                    if half == 0 and tc == 0 and kt == 0:
                        w += [("qw", m_wk)] + xw
                    mt = p.emit("tensor", lambda e, half=half, tc=tc, kt=kt,
                                bank=bank: e.matmul(
                        bank[half * 64:(half + 1) * 64, :],
                        wksl[:, kt * 64:(kt + 1) * 64],
                        ctx.xbf[:, kt * T + tc * S5: kt * T + (tc + 1) * S5],
                        start=(kt == 0), stop=(kt == CT - 1)),
                        waits=tuple(w), inc="t" if kt == CT - 1 else None)
                ms = p.emit("scalar", lambda e, half=half, tc=tc, bank=bank:
                            e.copy(k2[half * 64:(half + 1) * 64,
                                      tc * S5:(tc + 1) * S5],
                                   bank[half * 64:(half + 1) * 64, :]),
                            waits=(("t", mt),), inc="s")
                r_st.release(bi, "s", ms)

        for tt in range(NTT):
            p.emit("vector", lambda e, tt=tt: e.memset(
                vext[:, tt * 65 + 64:tt * 65 + 65], 1.0), inc="v")
        m_vext = p.val("v")
        for tt in range(NTT):
            bi, bank, war = r_st.take()
            for kt in range(CT):
                w = list(war) if kt == 0 else []
                if tt == 0 and kt == 0:
                    w += [("qw", m_wv)] + xw
                mt = p.emit("tensor", lambda e, tt=tt, kt=kt, bank=bank:
                            e.matmul(
                    bank[:, 0:64],
                    ctx.xbf[:, kt * T + tt * 128: kt * T + (tt + 1) * 128],
                    wvsl[:, kt * 64:(kt + 1) * 64],
                    start=(kt == 0), stop=(kt == CT - 1)),
                    waits=tuple(w), inc="t" if kt == CT - 1 else None)
            ms = p.emit("scalar", lambda e, tt=tt, bank=bank: e.copy(
                vext[:, tt * 65:tt * 65 + 64], bank[:, 0:64]),
                waits=(("t", mt),), inc="s")
            r_st.release(bi, "s", ms)
        m_qkv = p.val("s")

        pT_war = [0, 0, 0]
        praw_war = 0
        for h in range(NH):
            hb = (h % 2) * 64
            qtile = h // 2
            for ic in range(NTC):
                bi_o, bank_o, war_o = r_o.take()
                bi_z, bank_z, war_z = r_z.take()
                w_first_pv = list(war_o)
                w_first_z = list(war_z)
                jts = list(range(4 * ic + 4))
                exp_ms = {}
                stq = {}

                def emit_pv(pidx, last):
                    pjt, pslot = stq[pidx]
                    w = [exp_ms[pidx]]
                    if pidx == 0:
                        w += w_first_pv
                    p.emit("tensor", lambda e, pjt=pjt, pslot=pslot, hb=hb,
                           bank=bank_o: e.matmul(
                        bank[hb:hb + 64, :],
                        vext[:, pjt * 65:pjt * 65 + 64],
                        pT[:, pslot * S5:(pslot + 1) * S5],
                        start=(pidx == 0), stop=last), waits=tuple(w))
                    w = []
                    if pidx == 0:
                        w += w_first_z
                    mz = p.emit("tensor", lambda e, pjt=pjt, pslot=pslot,
                                bank=bank_z: e.matmul(
                        bank[0:1, :],
                        vext[:, pjt * 65 + 64:pjt * 65 + 65],
                        pT[:, pslot * S5:(pslot + 1) * S5],
                        start=(pidx == 0), stop=last), waits=tuple(w), inc="t")
                    pT_war[pslot] = mz
                    return mz

                for idx, jt in enumerate(jts):
                    bi_s, bank_s, war_s = r_st.take()
                    w = list(war_s)
                    if h == 0 and ic == 0 and idx == 0:
                        w += [("s", m_qkv), ("v", m_vext)]
                    mt_s = p.emit("tensor", lambda e, jt=jt, hb=hb, qtile=qtile,
                                  ic=ic, bank=bank_s: e.matmul(
                        bank[:], k2[hb:hb + 64, jt * 128:(jt + 1) * 128],
                        qT[hb:hb + 64,
                           qtile * T + ic * S5: qtile * T + (ic + 1) * S5],
                        start=True, stop=True), waits=tuple(w), inc="t")
                    slot = idx % 3
                    diag = (jt // 4 == ic)
                    if diag:
                        nonlocal_praw = praw_war
                        ms_e = p.emit("scalar", lambda e, bank=bank_s: e.activation(
                            praw[:], bank[:], AF.Exp, scale=0.125),
                            waits=(("t", mt_s), ("v", nonlocal_praw),
                                   ("t", pT_war[slot]))[0:3], inc="s")
                        mv_e = p.emit("vector", lambda e, jt=jt, slot=slot:
                                      e.tensor_mul(
                            pT[:, slot * S5:(slot + 1) * S5], praw[:],
                            ctx.maskC4[:, (jt % 4) * S5:(jt % 4 + 1) * S5]),
                            waits=(("s", ms_e), ("t", pT_war[slot])), inc="v")
                        praw_war = mv_e
                        exp_ms[idx] = ("v", mv_e)
                    else:
                        ms_e = p.emit("scalar", lambda e, slot=slot, bank=bank_s:
                                      e.activation(
                            pT[:, slot * S5:(slot + 1) * S5], bank[:],
                            AF.Exp, scale=0.125),
                            waits=(("t", mt_s), ("t", pT_war[slot])), inc="s")
                        exp_ms[idx] = ("s", ms_e)
                    r_st.release(bi_s, exp_ms[idx][0], exp_ms[idx][1])
                    stq[idx] = (jt, slot)
                    if idx >= 2:
                        emit_pv(idx - 2, last=False)
                n = len(jts)
                for pidx in range(max(0, n - 2), n):
                    emit_pv(pidx, last=(pidx == n - 1))
                mt_pv = p.val("t")
                mzr = p.emit("vector", lambda e, bank=bank_z: e.reciprocal(
                    zr[:], bank[0:1, :]), waits=(("t", mt_pv),), inc="v")
                mt_rep = p.emit("tensor", lambda e, hb=hb, bank=bank_z: e.matmul(
                    bank[hb:hb + 64, :], ctx.onesF[0:1, 0:64], zr[:],
                    start=True, stop=True), waits=(("v", mzr),), inc="t")
                mv_rep = p.emit("vector", lambda e, hb=hb, bank=bank_z:
                                e.tensor_copy(zrep[hb:hb + 64, :],
                                              bank[hb:hb + 64, :]),
                                waits=(("t", mt_rep),), inc="v")
                mnorm = p.emit("vector", lambda e, hb=hb, qtile=qtile, ic=ic,
                               bank=bank_o: e.tensor_mul(
                    att[hb:hb + 64,
                        qtile * T + ic * S5: qtile * T + (ic + 1) * S5],
                    bank[hb:hb + 64, :], zrep[hb:hb + 64, :]),
                    waits=(("v", mv_rep),), inc="v")
                r_o.release(bi_o, "v", mnorm)
                r_z.release(bi_z, "v", mnorm)
        m_att_v = p.val("v")

    seq = [(tc, ot) for tc in range(NTC) for ot in range(CT)]
    wo_m = {}
    wo_free = [wq_free[0], wq_free[1]]

    def pf_wo(i):
        if i >= len(seq):
            return
        tc, ot = seq[i]
        sl = i % 2
        wo_m[i] = p.emit("gpsimd", lambda e, ot=ot, sl=sl: e.dma_start(
            out=wsl[:, sl * 1024:(sl + 1) * 1024], in_=d_wo[ot]),
            waits=(("t", wo_free[sl]),), inc="qw", inc_by=16)

    pf_wo(0)
    pf_wo(1)
    wo_i = [0]

    def wo_emitter(tc, ot, bank, waits):
        i = wo_i[0]
        assert seq[i] == (tc, ot)
        sl = i % 2
        for kt in range(CT):
            w = list(waits) if kt == 0 else []
            if kt == 0:
                w.append(("qw", wo_m[i]))
                if ot == 0 and tc == 0:
                    w += [("v", m_att_v), ("qx", m_bo)]
            p.emit("tensor", lambda e, ot=ot, tc=tc, kt=kt, sl=sl, bank=bank:
                   e.matmul(
                bank[:], wsl[:, sl * 1024 + kt * 128: sl * 1024 + (kt + 1) * 128],
                att[:, kt * T + tc * S5: kt * T + (tc + 1) * S5],
                start=(kt == 0), stop=False),
                waits=tuple(w))
        mt = p.emit("tensor", lambda e, ot=ot, bank=bank: e.matmul(
            bank[:], bo_sb[0:1, ot * 128:(ot + 1) * 128],
            ctx.onesF[0:1, 0:S5], start=False, stop=True), inc="t")
        wo_free[sl] = p.val("t")
        pf_wo(i + 2)
        wo_i[0] += 1
        return mt

    emit_out_ln(p, ctx, wo_emitter, "l1a")


def prep_l1_ffn(inputs):
    d = {}
    d["l1f_w1T"] = lhsT_blocked(np.ascontiguousarray(np.asarray(inputs["L1_ffn_W1"]).T))
    d["l1f_w2T"] = lhsT_blocked(np.ascontiguousarray(np.asarray(inputs["L1_ffn_W2"]).T))
    d["l1f_b1"] = np.ascontiguousarray(np.asarray(inputs["L1_ffn_b1"]).reshape(1, 4096))
    d["l1f_b2"] = np.ascontiguousarray(np.asarray(inputs["L1_ffn_b2"]).reshape(1, C))
    d["l1f_scale"] = np.ascontiguousarray(np.asarray(inputs["L1_ffn_scale"]).reshape(1, 1))
    import ml_dtypes
    rm = np.zeros((4, 4 * 128), np.float32)
    for j in range(4):
        rm[j, j * 128:(j + 1) * 128] = 1.0
    d["l1f_rowmask"] = rm.astype(ml_dtypes.bfloat16)
    return d


def emit_l1_ffn(p, ctx):
    HT = 32
    S5 = 512
    d_w1 = p.dram("l1f_w1T", [HT, 128, 8 * 128], F32, kind="ExternalInput")
    d_w2 = p.dram("l1f_w2T", [CT, 128, 32 * 128], F32, kind="ExternalInput")
    d_b1 = p.dram("l1f_b1", [1, 4096], F32, kind="ExternalInput")
    d_b2 = p.dram("l1f_b2", [1, C], F32, kind="ExternalInput")
    d_sc = p.dram("l1f_scale", [1, 1], F32, kind="ExternalInput")
    d_rm = p.dram("l1f_rowmask", [4, 4 * 128], BF16, kind="ExternalInput")
    ev4_dram = p.dram(p._n("l1f_ev4d"), [4, 1024], BF16)

    hbuf = p.sb("l1f_h", [128, HT * S5], BF16)
    w1sl = p.sb("l1f_w1s", [128, 2 * 1024], BF16)
    w2sl = p.sb("l1f_w2s", [128, 2 * 4096], BF16)
    e1 = p.sb("l1f_e1", [4, HT * 128], BF16)
    o2 = p.sb("l1f_o2", [128, HT * 16], BF16)
    we = p.sb("l1f_we", [16, 1024], BF16)
    wet = p.sb("l1f_wet", [128, 16], BF16)
    ev4 = p.sb("l1f_ev4", [4, 1024], BF16)
    c_sb = p.sb("l1f_c", [4, T], BF16)
    g_sb = p.sb("l1f_g", [16, S5], BF16)
    b1r = p.sb("l1f_b1r", [1, 4096], F32)
    b2r = p.sb("l1f_b2r", [1, C], F32)
    sc1 = p.sb("l1f_sc1", [1, 1], F32)
    sc4 = p.sb("l1f_sc4", [4, 1], F32)
    sc16 = p.sb("l1f_sc16", [16, 1], F32)
    rmask = p.sb("l1f_rm", [4, 4 * 128], BF16)

    xw = list(ctx.xbf_wait)

    ol = OutLN(p, ctx, "l1f")
    with ExitStack() as pstack:
        PS = lambda nm, shape, dt=F32: pstack.enter_context(
            p.nc.psum_tensor(p._n(nm), shape, dt))
        rf = Rot([PS("l1f", [128, S5]) for _ in range(2)])
        ps_t = PS("l1ft", [4, 128], BF16)
        ps_t2 = PS("l1ft2", [16, 128], BF16)

        for dst, srcd in ((b1r, d_b1), (b2r, d_b2), (sc1, d_sc), (rmask, d_rm)):
            p.emit("sync", lambda e, dst=dst, srcd=srcd: e.dma_start(
                out=dst[:], in_=srcd), inc="qx", inc_by=16)
        m_lds = p.val("qx")

        bi, bank, war = rf.take()
        mt = p.emit("tensor", lambda e, bank=bank: e.matmul(
            bank[0:16, 0:1], ctx.onesF[0:1, 0:16], sc1[:],
            start=True, stop=True),
            waits=tuple(list(war) + [("qx", m_lds)] + xw), inc="t")
        mv = p.emit("vector", lambda e, bank=bank: e.tensor_copy(
            sc16[:], bank[0:16, 0:1]), waits=(("t", mt),), inc="v")
        p.emit("vector", lambda e, bank=bank: e.tensor_copy(
            sc4[:], bank[0:4, 0:1]), inc="v")
        mv_sc = p.val("v")
        rf.release(bi, "v", mv_sc)

        for tc in range(NTC):
            bi, bank, war = rf.take()
            for kt in range(CT):
                w = list(war) if kt == 0 else []
                mt = p.emit("tensor", lambda e, tc=tc, kt=kt, bank=bank:
                            e.matmul(
                    bank[0:4, :], ctx.xbf[:, kt * T + 2: kt * T + 10: 2],
                    ctx.xbf[:, kt * T + tc * S5: kt * T + (tc + 1) * S5],
                    start=(kt == 0), stop=(kt == CT - 1)),
                    waits=tuple(w), inc="t" if kt == CT - 1 else None)
            mv = p.emit("vector", lambda e, tc=tc, bank=bank: e.tensor_scalar_mul(
                c_sb[:, tc * S5:(tc + 1) * S5], bank[0:4, :], sc4[:]),
                waits=(("t", mt), ("v", mv_sc)), inc="v")
            rf.release(bi, "v", mv)

        m_pt_read = 0
        for ht in range(HT):
            j, cti = ht // 8, ht % 8
            w = [("v", m_pt_read)] if m_pt_read else []
            mt = p.emit("tensor", lambda e, cti=cti: e.transpose(
                ps_t[:], ctx.xbf[:, cti * T + 1: cti * T + 9: 2],
                ctx.ident[:]), waits=tuple(w), inc="t")
            m_pt_read = p.emit("vector", lambda e, ht=ht, j=j: e.tensor_mul(
                e1[:, ht * 128:(ht + 1) * 128], ps_t[:],
                rmask[:, j * 128:(j + 1) * 128]), waits=(("t", mt),), inc="v")

        p.emit("vector", lambda e: e.memset(o2[:], 0.0), inc="v")
        for kt in range(HT):
            pm, cti = kt // 8, kt % 8
            for j in range(4):
                p.emit("vector", lambda e, kt=kt, j=j, pm=pm, cti=cti:
                       e.tensor_copy(
                    o2[:, kt * 16 + j * 4 + pm: kt * 16 + j * 4 + pm + 1],
                    ctx.xbf[:, cti * T + 10 + 2 * j: cti * T + 11 + 2 * j]),
                    inc="v")
        m_o2 = p.val("v")

        for cti in range(CT):
            w = [("v", m_pt_read)]
            mt = p.emit("tensor", lambda e, cti=cti: e.transpose(
                ps_t[:], ctx.xbf[:, cti * T + 9: cti * T + 17: 2],
                ctx.ident[:]), waits=tuple(w), inc="t")
            m_pt_read = p.emit("vector", lambda e, cti=cti: e.tensor_copy(
                ev4[:, cti * 128:(cti + 1) * 128], ps_t[:]),
                waits=(("t", mt),), inc="v")
        m_ev4_dma = p.emit("sync", lambda e: e.dma_start(
            out=ev4_dram, in_=ev4[:]), waits=(("v", m_pt_read),),
            inc="qx", inc_by=16)
        p.emit("vector", lambda e: e.memset(wet[:], 0.0), inc="v")
        m_wet0 = p.val("v")
        wet_dmas = 0
        for otE in range(CT):
            j, half = otE // 2, otE % 2
            src_ap = ev4_dram[j, half * 512:(half + 1) * 512].rearrange(
                "(pp f) -> pp f", f=4)
            p.emit("sync", lambda e, otE=otE, j=j, src_ap=src_ap: e.dma_start(
                out=wet[:, j * 4:(j + 1) * 4], in_=src_ap),
                waits=(("qx", m_ev4_dma), ("v", m_wet0),
                       ("t", p.val("t"))), inc="qx", inc_by=16)
            m_wet = p.val("qx")
            mt = p.emit("tensor", lambda e: e.transpose(
                ps_t2[:], wet[:], ctx.ident[:]),
                waits=(("qx", m_wet),), inc="t")
            mv = p.emit("vector", lambda e, otE=otE: e.tensor_copy(
                we[:, otE * 128:(otE + 1) * 128], ps_t2[:]),
                waits=(("t", mt),), inc="v")
            if otE < CT - 1 and (otE + 1) // 2 != j:
                p.emit("vector", lambda e, j=j: e.memset(
                    wet[:, j * 4:(j + 1) * 4], 0.0), inc="v")
            m_wet0 = p.val("v")
        m_we = p.val("v")

        w1_m = {}
        w1_free = [0, 0]
        w2_m = {}
        w2_free = [0, 0]

        def pf_w1(q, ht):
            sl = ht % 2
            w1_m[(q, ht)] = p.emit("gpsimd", lambda e, ht=ht, sl=sl: e.dma_start(
                out=w1sl[:, sl * 1024:(sl + 1) * 1024], in_=d_w1[ht]),
                waits=(("t", w1_free[sl]),), inc="qw", inc_by=16)

        def pf_w2(q, ot):
            sl = ot % 2
            w2_m[(q, ot)] = p.emit("gpsimd", lambda e, ot=ot, sl=sl: e.dma_start(
                out=w2sl[:, sl * 4096:(sl + 1) * 4096], in_=d_w2[ot]),
                waits=(("t", w2_free[sl]),), inc="qw", inc_by=16)

        for q in range(NTC):
            pf_w1(q, 0)
            pf_w1(q, 1)
            for ht in range(HT):
                sl = ht % 2
                bi, bank, war = rf.take()
                for kt in range(CT):
                    w = list(war) if kt == 0 else []
                    if kt == 0:
                        w.append(("qw", w1_m[(q, ht)]))
                        if q == 0 and ht == 0:
                            w.append(("v", m_we))
                    p.emit("tensor", lambda e, q=q, ht=ht, kt=kt, sl=sl,
                           bank=bank: e.matmul(
                        bank[:], w1sl[:, sl * 1024 + kt * 128:
                                      sl * 1024 + (kt + 1) * 128],
                        ctx.xbf[:, kt * T + q * S5: kt * T + (q + 1) * S5],
                        start=(kt == 0), stop=False), waits=tuple(w))
                p.emit("tensor", lambda e, q=q, ht=ht, bank=bank: e.matmul(
                    bank[:], e1[:, ht * 128:(ht + 1) * 128],
                    c_sb[:, q * S5:(q + 1) * S5], start=False, stop=False))
                mt = p.emit("tensor", lambda e, q=q, ht=ht, bank=bank: e.matmul(
                    bank[:], b1r[0:1, ht * 128:(ht + 1) * 128],
                    ctx.onesF[0:1, 0:S5], start=False, stop=True), inc="t")
                w1_free[sl] = p.val("t")
                if ht + 2 < HT:
                    pf_w1(q, ht + 2)
                ms = p.emit("scalar", lambda e, ht=ht, bank=bank: e.activation(
                    hbuf[:, ht * S5:(ht + 1) * S5], bank[:], AF.Gelu),
                    waits=(("t", mt),), inc="s")
                rf.release(bi, "s", ms)
            m_h = p.val("s")

            bi, bank, war = rf.take()
            for kt in range(HT):
                w = list(war) if kt == 0 else []
                if kt == 0:
                    w += [("v", m_o2), ("s", m_h)]
                mt = p.emit("tensor", lambda e, kt=kt, bank=bank: e.matmul(
                    bank[0:16, :], o2[:, kt * 16:(kt + 1) * 16],
                    hbuf[:, kt * S5:(kt + 1) * S5],
                    start=(kt == 0), stop=(kt == HT - 1)),
                    waits=tuple(w), inc="t" if kt == HT - 1 else None)
            mv_g = p.emit("vector", lambda e, bank=bank: e.tensor_scalar_mul(
                g_sb[:], bank[0:16, :], sc16[:]), waits=(("t", mt),), inc="v")
            rf.release(bi, "v", mv_g)

            pf_w2(q, 0)
            pf_w2(q, 1)

            def w2_emitter(tc, ot, bank, waits, q=q, mv_g=mv_g):
                sl = ot % 2
                for kt in range(HT):
                    w = list(waits) if kt == 0 else []
                    if kt == 0:
                        w.append(("qw", w2_m[(q, ot)]))
                        if ot == 0:
                            w.append(("v", mv_g))
                    p.emit("tensor", lambda e, ot=ot, kt=kt, sl=sl, bank=bank:
                           e.matmul(
                        bank[:], w2sl[:, sl * 4096 + kt * 128:
                                      sl * 4096 + (kt + 1) * 128],
                        hbuf[:, kt * S5:(kt + 1) * S5],
                        start=(kt == 0), stop=False), waits=tuple(w))
                p.emit("tensor", lambda e, ot=ot, bank=bank: e.matmul(
                    bank[:], we[:, ot * 128:(ot + 1) * 128], g_sb[:],
                    start=False, stop=False))
                mt = p.emit("tensor", lambda e, ot=ot, bank=bank: e.matmul(
                    bank[:], b2r[0:1, ot * 128:(ot + 1) * 128],
                    ctx.onesF[0:1, 0:S5], start=False, stop=True), inc="t")
                w2_free[sl] = p.val("t")
                if ot + 2 < CT:
                    pf_w2(q, ot + 2)
                return mt

            ol.chunk(q, w2_emitter)
    ol.end()


def prep_l2_attn(inputs):
    import ml_dtypes
    d = {}
    W_in = np.asarray(inputs["L2_attn_W_in"])
    W_pad = np.zeros((1024, 4224), np.float32)
    W_pad[:, :4192] = W_in
    d["l2_winT"] = lhsT_blocked(W_pad)
    d["l2_woutT"] = lhsT_blocked(np.ascontiguousarray(np.asarray(inputs["L2_attn_W_out"])))
    d["l2_lnw"] = np.ascontiguousarray(np.asarray(inputs["L2_attn_ln_w"]).reshape(8, 128).T)
    d["l2_lnb"] = np.ascontiguousarray(np.asarray(inputs["L2_attn_ln_b"]).reshape(8, 128).T)
    d["l2_dtb"] = np.ascontiguousarray(np.asarray(inputs["L2_attn_dt_bias"]).reshape(32, 1))
    d["l2_alog"] = np.ascontiguousarray(np.asarray(inputs["L2_attn_A_log"]).reshape(32, 1))
    d["l2_dexp"] = np.ascontiguousarray(
        np.repeat(np.asarray(inputs["L2_attn_D"]), 64).reshape(16, 128).T)
    d["l2_rmsw"] = np.ascontiguousarray(np.asarray(inputs["L2_attn_rms_w"]).reshape(16, 128).T)
    t = np.arange(T, dtype=np.float64)
    inv = 1.0 / 10000.0 ** (np.arange(0, 16, 2, dtype=np.float64) / 16.0)
    ang = t[:, None] * inv[None, :]
    d["l2_cos"] = np.ascontiguousarray(
        np.cos(ang).astype(np.float32).reshape(16, 128, 8).transpose(1, 0, 2)
        .reshape(128, 16 * 8)).astype(ml_dtypes.bfloat16)
    d["l2_sin"] = np.ascontiguousarray(
        np.sin(ang).astype(np.float32).reshape(16, 128, 8).transpose(1, 0, 2)
        .reshape(128, 16 * 8)).astype(ml_dtypes.bfloat16)
    return d


def emit_l2_attn(p, ctx):
    import os as _os
    PH = int(_os.environ.get("L2PHASE", "9"))
    S5 = 512
    NH2, DH, DS, NCH = 32, 64, 32, 16
    d_win = p.dram("l2_winT", [33, 128, 1024], F32, kind="ExternalInput")
    d_wout = p.dram("l2_woutT", [8, 128, 2048], F32, kind="ExternalInput")
    d_lnw = p.dram("l2_lnw", [128, 8], F32, kind="ExternalInput")
    d_lnb = p.dram("l2_lnb", [128, 8], F32, kind="ExternalInput")
    d_dtb = p.dram("l2_dtb", [32, 1], F32, kind="ExternalInput")
    d_alog = p.dram("l2_alog", [32, 1], F32, kind="ExternalInput")
    d_dexp = p.dram("l2_dexp", [128, 16], F32, kind="ExternalInput")
    d_rmsw = p.dram("l2_rmsw", [128, 16], F32, kind="ExternalInput")
    d_cos = p.dram("l2_cos", [128, 16 * 8], BF16, kind="ExternalInput")
    d_sin = p.dram("l2_sin", [128, 16 * 8], BF16, kind="ExternalInput")
    z_dram = p.dram(p._n("l2_zd"), [16, 128, T], BF16)
    xs_dram = p.dram(p._n("l2_xsd"), [16, 128, T], BF16)
    cs_dram = p.dram(p._n("l2_csd"), [32, T], F32)

    winsl = p.sb("l2_winsl", [128, 2 * 1024], BF16)
    zxst = p.sb("l2_zxst", [128, 2 * S5], BF16)
    hsc = p.sb("l2_hsc", [128, 2 * S5], BF16)
    bc_sb = p.sb("l2_bc", [64, T], BF16)
    dt_sb = p.sb("l2_dt", [96, T], F32)
    cs_sb = p.sb("l2_cs", [96, T], F32)
    a_sc = p.sb("l2_a", [96, 128], F32)
    sp1 = p.sb("l2_sp1", [96, S5], F32)
    brop = p.sb("l2_brop", [32, T], BF16)
    crop = p.sb("l2_crop", [32, T], BF16)
    bctk = p.sb("l2_bctk", [128, 64], BF16)
    btr = p.sb("l2_btr", [128, NCH * 32], BF16)
    ctr = p.sb("l2_ctr", [128, 32], BF16)
    cstok = p.sb("l2_cstok", [128, NCH * 32], F32)
    dttok = p.sb("l2_dttok", [128, NCH * 32], F32)
    csrep = p.sb("l2_csrep", [128, 16 * 128], F32)
    xsc = p.sb("l2_xsc", [128, NCH * 128], BF16)
    zc = p.sb("l2_zc", [128, NCH * 128], BF16)
    xtc = p.sb("l2_xtc", [128, 2048 + 64], BF16)
    ltf = p.sb("l2_ltf", [128, 128], F32)
    ltb = p.sb("l2_ltb", [128, 128], BF16)
    msc = p.sb("l2_msc", [128, 128], BF16)
    cp_sb = p.sb("l2_cp", [32, 128], BF16)
    bpp = p.sb("l2_bpp", [128, 128], BF16)
    zsil = p.sb("l2_zsil", [128, 128], BF16)
    ptb = p.sb("l2_ptb", [128, 128], BF16)
    ecs_sb = p.sb("l2_ecs", [32, 128], BF16)
    ytmp = p.sb("l2_ytmp", [128, 128], F32)
    s_f = p.sb("l2_sf", [32, NH2 * 64], F32)
    s_b = p.sb("l2_sb", [32, NH2 * 64], BF16)
    dac = p.sb("l2_dac", [32, 1], F32)
    dfac = p.sb("l2_dfac", [128, 1], F32)
    ywin = p.sb("l2_ywin", [128, 16 * S5], BF16)
    woutsl = p.sb("l2_wouts", [128, 2048], BF16)
    lnw = p.sb("l2_lnw_s", [128, 8], F32)
    lnb = p.sb("l2_lnb_s", [128, 8], F32)
    dtb = p.sb("l2_dtb_s", [96, 1], F32)
    alog = p.sb("l2_alog_s", [96, 1], F32)
    acol = p.sb("l2_acol", [96, 1], F32)
    dexp = p.sb("l2_dexp_s", [128, 16], F32)
    rmsw = p.sb("l2_rmsw_s", [128, 16], F32)
    cosb = p.sb("l2_cos_s", [128, 16 * 8], BF16)
    sinb = p.sb("l2_sin_s", [128, 16 * 8], BF16)


    xw = list(ctx.xbf_wait)
    R = ctx.ln_rows
    hlnb = ctx.xstage[:].bitcast(BF16)

    ol = OutLN(p, ctx, "l2a")
    with ExitStack() as pstack:
        PS = lambda nm, shape, dt=F32: pstack.enter_context(
            p.nc.psum_tensor(p._n(nm), shape, dt))
        r_a = Rot([PS("l2a", [128, S5])])
        r_sc2 = Rot([PS("l2sc", [128, 128]), PS("l2sc", [128, 128])])
        ps_tr = PS("l2tr", [128, 128], BF16)
        ps_y = PS("l2y", [128, 128])
        m_tr_read = 0
        tr_sem = "v"

        for dst, srcd in ((lnw, d_lnw), (lnb, d_lnb), (dexp, d_dexp),
                          (rmsw, d_rmsw)):
            p.emit("sync", lambda e, dst=dst, srcd=srcd: e.dma_start(
                out=dst[:], in_=srcd), inc="qx", inc_by=16)
        for dst, srcd in ((dtb, d_dtb), (alog, d_alog)):
            p.emit("sync", lambda e, dst=dst, srcd=srcd: e.dma_start(
                out=dst[64:96, :], in_=srcd), inc="qx", inc_by=16)
        for dst, srcd in ((cosb, d_cos), (sinb, d_sin)):
            p.emit("sync", lambda e, dst=dst, srcd=srcd: e.dma_start(
                out=dst[:], in_=srcd), inc="qx", inc_by=16)
        m_consts = p.val("qx")
        p.emit("scalar", lambda e: e.activation(
            acol[64:96, :], alog[64:96, :], AF.Exp),
            waits=(("qx", m_consts),), inc="s")
        p.emit("vector", lambda e: e.tensor_scalar_mul(
            acol[64:96, :], acol[64:96, :], -1.0),
            waits=(("s", p.val("s")),), inc="v")
        m_acol = p.val("v")
        p.emit("vector", lambda e: e.memset(bpp[:, 32:128], 0.0), inc="v")
        p.emit("vector", lambda e: e.memset(xtc[:], 0.0), inc="v")
        m_pads = p.val("v")

        seq = [(tc, ot) for tc in range(NTC) for ot in range(33)]
        win_m = {}
        win_free = [0, 0]

        def pf_win(i):
            if i >= len(seq):
                return
            tc, ot = seq[i]
            sl = i % 2
            win_m[i] = p.emit("gpsimd", lambda e, ot=ot, sl=sl: e.dma_start(
                out=winsl[:, sl * 1024:(sl + 1) * 1024], in_=d_win[ot]),
                waits=(("t", win_free[sl]),), inc="qw", inc_by=16)

        pf_win(0)
        pf_win(1)

        m_hln = {}
        m_sq_prev = 0
        m_hlnsums_prev = 0
        m_row_prev = 0
        zx_war = [0, 0]
        hsc_war = [0, 0]
        spl_m = {}
        i_seq = 0
        for tc in range(NTC):
            for ct in range(CT):
                w = [] if not (tc == 0 and ct == 0) else list(xw)
                if tc > 0 and ct == 0:
                    w.append(("t", m_hlnsums_prev))
                p.emit("scalar", lambda e, ct=ct, tc=tc: e.activation(
                    ctx.ln_sq[:, ct * S5:(ct + 1) * S5],
                    ctx.xbf[:, ct * T + tc * S5: ct * T + (tc + 1) * S5],
                    AF.Square), waits=tuple(w), inc="s")
            m_sq = p.val("s")
            bi_st, b_st, war_st = r_a.take()
            for which in range(2):
                for ct in range(CT):
                    w = []
                    if which == 0 and ct == 0:
                        w = list(war_st) + [("s", m_sq)] + (xw if tc == 0 else [])
                        if m_row_prev:
                            w.append(("v", m_row_prev))
                    srcap = (ctx.xbf[:, ct * T + tc * S5: ct * T + (tc + 1) * S5]
                             if which == 0 else
                             ctx.ln_sq[:, ct * S5:(ct + 1) * S5])
                    rowbase = 0 if which == 0 else 64
                    p.emit("tensor", lambda e, srcap=srcap, rowbase=rowbase,
                           ct=ct, b=b_st: e.matmul(
                        b[rowbase:rowbase + 1, :], ctx.ones_bf[:], srcap,
                        start=(ct == 0), stop=(ct == CT - 1)),
                        waits=tuple(w), inc="t" if ct == CT - 1 else None)
            m_sums = p.val("t")
            m_hlnsums_prev = m_sums
            p.emit("vector", lambda e, b=b_st: e.tensor_scalar_mul(
                R[:, 2 * S5:3 * S5], b[0:1, :], 1.0 / C),
                waits=(("t", m_sums),), inc="v")
            p.emit("vector", lambda e, b=b_st: e.tensor_scalar_mul(
                R[:, 3 * S5:4 * S5], b[64:65, :], 1.0 / C), inc="v")
            m_row_prev = p.val("v")
            r_a.release(bi_st, "v", m_row_prev)
            p.emit("vector", lambda e: e.tensor_mul(
                R[:, 0:S5], R[:, 2 * S5:3 * S5], R[:, 2 * S5:3 * S5]), inc="v")
            p.emit("vector", lambda e: e.tensor_sub(
                R[:, S5:2 * S5], R[:, 3 * S5:4 * S5], R[:, 0:S5]), inc="v")
            p.emit("vector", lambda e: e.tensor_scalar_add(
                R[:, S5:2 * S5], R[:, S5:2 * S5], 1e-6), inc="v")
            m_var = p.val("v")
            p.emit("scalar", lambda e: e.activation(
                R[:, 3 * S5:4 * S5], R[:, S5:2 * S5], AF.Sqrt),
                waits=(("v", m_var),), inc="s")
            m_sqr = p.val("s")
            p.emit("vector", lambda e: e.reciprocal(
                R[:, 4 * S5:5 * S5], R[:, 3 * S5:4 * S5]),
                waits=(("s", m_sqr),), inc="v")
            p.emit("vector", lambda e: e.tensor_mul(
                R[:, 5 * S5:6 * S5], R[:, 2 * S5:3 * S5], R[:, 4 * S5:5 * S5]),
                inc="v")
            p.emit("vector", lambda e: e.tensor_scalar_mul(
                R[:, 5 * S5:6 * S5], R[:, 5 * S5:6 * S5], -1.0), inc="v")
            m_ab = p.val("v")
            for which, dst in ((4, ctx.ln_arep), (5, ctx.ln_brep)):
                bi, bank, war = r_a.take()
                w = list(war)
                if which == 4:
                    w.append(("v", m_ab))
                mt = p.emit("tensor", lambda e, which=which, bank=bank: e.matmul(
                    bank[:], ctx.onesF[0:1, 0:128],
                    R[:, which * S5:(which + 1) * S5],
                    start=True, stop=True), waits=tuple(w), inc="t")
                mv = p.emit("vector", lambda e, dst=dst, bank=bank: e.tensor_copy(
                    dst[:], bank[:]), waits=(("t", mt),), inc="v")
                r_a.release(bi, "v", mv)
            m_rep = p.val("v")
            for ct in range(CT):
                hs = ct % 2
                w = [("v", m_rep)] if ct == 0 else []
                if hsc_war[hs]:
                    w.append(("v", hsc_war[hs]))
                if tc > 0 and ct == 0:
                    w.append(("t", m_hln.get(tc - 1, 0)))
                p.emit("vector", lambda e, ct=ct, tc=tc, hs=hs: e.tensor_mul(
                    hsc[:, hs * S5:(hs + 1) * S5],
                    ctx.xbf[:, ct * T + tc * S5: ct * T + (tc + 1) * S5],
                    ctx.ln_arep[:]), waits=tuple(w), inc="v")
                p.emit("vector", lambda e, hs=hs: e.tensor_add(
                    hsc[:, hs * S5:(hs + 1) * S5], hsc[:, hs * S5:(hs + 1) * S5],
                    ctx.ln_brep[:]), inc="v")
                mv = p.emit("vector", lambda e, ct=ct, hs=hs: e.tensor_scalar(
                    hlnb[:, ct * S5:(ct + 1) * S5],
                    hsc[:, hs * S5:(hs + 1) * S5],
                    lnw[:, ct:ct + 1], lnb[:, ct:ct + 1], ALU.mult, ALU.add),
                    inc="v")
                hsc_war[hs] = mv
            m_hln_v = p.val("v")

            SUB = _os.environ.get("L2SUB", "full")
            for ot in range(0 if SUB == "hln" else 33):
                sl = i_seq % 2
                mwin = win_m[i_seq]
                if ot < 32:
                    bi, bank, war = r_a.take()
                    for kt in range(CT):
                        w = list(war) if kt == 0 else []
                        if kt == 0:
                            w.append(("qw", mwin))
                            w.append(("v", m_hln_v))
                        mt = p.emit("tensor", lambda e, kt=kt, sl=sl, bank=bank:
                                    e.matmul(
                            bank[:], winsl[:, sl * 1024 + kt * 128:
                                           sl * 1024 + (kt + 1) * 128],
                            hlnb[:, kt * S5:(kt + 1) * S5],
                            start=(kt == 0), stop=(kt == CT - 1)),
                            waits=tuple(w), inc="t" if kt == CT - 1 else None)
                    zx = ot % 2
                    w = [("t", mt)]
                    if zx_war[zx]:
                        w.append(("qx", zx_war[zx]))
                    if ot < 16:
                        ms = p.emit("scalar", lambda e, zx=zx, bank=bank: e.copy(
                            zxst[:, zx * S5:(zx + 1) * S5], bank[:]),
                            waits=tuple(w), inc="s")
                        dtgt = z_dram
                        kt_ = ot
                    else:
                        ms = p.emit("scalar", lambda e, zx=zx, bank=bank:
                                    e.activation(
                            zxst[:, zx * S5:(zx + 1) * S5], bank[:], AF.Silu),
                            waits=tuple(w), inc="s")
                        dtgt = xs_dram
                        kt_ = ot - 16
                    r_a.release(bi, "s", ms)
                    zx_war[zx] = p.emit("sync", lambda e, kt_=kt_, tc=tc, zx=zx,
                                        dtgt=dtgt: e.dma_start(
                        out=dtgt[kt_][:, tc * S5:(tc + 1) * S5],
                        in_=zxst[:, zx * S5:(zx + 1) * S5]),
                        waits=(("s", ms),), inc="qx", inc_by=16)
                    spl_m[(tc, ot)] = zx_war[zx]
                elif SUB != "zx":
                    bi, bank, war = r_a.take()
                    for kt in range(CT):
                        w = list(war) if kt == 0 else []
                        if kt == 0:
                            w += [("qw", mwin), ("v", m_hln_v)]
                        mt1 = p.emit("tensor", lambda e, kt=kt, sl=sl, bank=bank:
                                     e.matmul(
                            bank[:], winsl[:, sl * 1024 + kt * 128:
                                           sl * 1024 + (kt + 1) * 128],
                            hlnb[:, kt * S5:(kt + 1) * S5],
                            start=(kt == 0), stop=(kt == CT - 1)),
                            waits=tuple(w), inc="t" if kt == CT - 1 else None)
                    ms1 = p.emit("scalar", lambda e, tc=tc, bank=bank:
                                 e.activation(
                        bc_sb[:, tc * S5:(tc + 1) * S5], bank[0:64, :],
                        AF.Silu), waits=(("t", mt1),), inc="s")
                    w = [("t", mt1), ("qx", m_consts)]
                    p.emit("vector", lambda e, bank=bank: e.tensor_scalar(
                        sp1[64:96, :], bank[64:96, :],
                        dtb[64:96, :], 0.0, ALU.add, ALU.max),
                        waits=tuple(w), inc="v")
                    m_relu = p.val("v")
                    p.emit("scalar", lambda e, tc=tc, bank=bank: e.activation(
                        dt_sb[64:96, tc * S5:(tc + 1) * S5], bank[64:96, :],
                        AF.Abs, bias=dtb[64:96, :]),
                        waits=(("t", mt1), ("qx", m_consts)), inc="s")
                    p.emit("scalar", lambda e, tc=tc: e.activation(
                        dt_sb[64:96, tc * S5:(tc + 1) * S5],
                        dt_sb[64:96, tc * S5:(tc + 1) * S5], AF.Exp,
                        scale=-1.0), inc="s")
                    ms2 = p.emit("scalar", lambda e, tc=tc: e.activation(
                        dt_sb[64:96, tc * S5:(tc + 1) * S5],
                        dt_sb[64:96, tc * S5:(tc + 1) * S5], AF.Ln, bias=1.0),
                        inc="s")
                    r_a.release(bi, "s", ms2)
                    mv = p.emit("vector", lambda e, tc=tc: e.tensor_add(
                        dt_sb[64:96, tc * S5:(tc + 1) * S5],
                        dt_sb[64:96, tc * S5:(tc + 1) * S5],
                        sp1[64:96, :]),
                        waits=(("s", ms2), ("v", m_relu)), inc="v")
                win_free[sl] = p.val("t")
                pf_win(i_seq + 2)
                i_seq += 1
            m_hln[tc] = p.val("t")
        m_dt_v = p.val("v")
        m_bc_s = p.val("s")

        if PH < 2:
            return
        m_brop = {}
        for tt in range(NCH):
            bk = 0
            w = [("s", m_bc_s), ("qx", m_consts)]
            if m_tr_read:
                w.append((tr_sem, m_tr_read))
            mt = p.emit("tensor", lambda e, tt=tt: e.transpose(
                ps_tr[:, 0:64], bc_sb[:, tt * 128:(tt + 1) * 128],
                ctx.ident[0:64, 0:64]), waits=tuple(w), inc="t")
            mv = p.emit("vector", lambda e, bk=bk: e.tensor_copy(
                bctk[:, bk * 64:(bk + 1) * 64], ps_tr[:, 0:64]),
                waits=(("t", mt),), inc="v")
            m_tr_read, tr_sem = mv, "v"
            for half, (dst8, dst_sb) in enumerate(((btr, brop), (ctr, crop))):
                base = bk * 64 + half * 32
                dcol = (tt * 32) if half == 0 else 0
                cs_ = slice(tt * 8, (tt + 1) * 8)
                p.emit("vector", lambda e, base=base, dcol=dcol, cs_=cs_,
                       dst8=dst8: e.tensor_mul(
                    dst8[:, dcol:dcol + 16:2], bctk[:, base:base + 16:2],
                    cosb[:, cs_]), inc="v")
                p.emit("vector", lambda e, base=base, cs_=cs_: e.tensor_mul(
                    ytmp[:, 0:8], bctk[:, base + 1:base + 16:2], sinb[:, cs_]),
                    waits=(("v", p.val("v")),), inc="v")
                p.emit("vector", lambda e, dcol=dcol, dst8=dst8: e.tensor_sub(
                    dst8[:, dcol:dcol + 16:2], dst8[:, dcol:dcol + 16:2],
                    ytmp[:, 0:8]), waits=(("v", p.val("v")),), inc="v")
                p.emit("vector", lambda e, base=base, dcol=dcol, cs_=cs_,
                       dst8=dst8: e.tensor_mul(
                    dst8[:, dcol + 1:dcol + 16:2], bctk[:, base:base + 16:2],
                    sinb[:, cs_]), waits=(("v", p.val("v")),), inc="v")
                p.emit("vector", lambda e, base=base, cs_=cs_: e.tensor_mul(
                    ytmp[:, 0:8], bctk[:, base + 1:base + 16:2], cosb[:, cs_]),
                    waits=(("v", p.val("v")),), inc="v")
                p.emit("vector", lambda e, dcol=dcol, dst8=dst8: e.tensor_add(
                    dst8[:, dcol + 1:dcol + 16:2], dst8[:, dcol + 1:dcol + 16:2],
                    ytmp[:, 0:8]), waits=(("v", p.val("v")),), inc="v")
                p.emit("vector", lambda e, base=base, dcol=dcol, dst8=dst8:
                       e.tensor_copy(
                    dst8[:, dcol + 16:dcol + 32], bctk[:, base + 16:base + 32]),
                    waits=(("v", p.val("v")),), inc="v")
                m_rot = p.val("v")
                w = [("v", m_rot)]
                if m_tr_read:
                    w.append((tr_sem, m_tr_read))
                srcap_col = (tt * 32) if half == 0 else 0
                src_t = dst8
                mt2 = p.emit("tensor", lambda e, src_t=src_t, c0=srcap_col:
                             e.transpose(
                    ps_tr[0:32, 0:128], src_t[:, c0:c0 + 32], ctx.ident[:]),
                    waits=tuple(w), inc="t")
                mv2 = p.emit("vector", lambda e, tt=tt, dst_sb=dst_sb:
                             e.tensor_copy(
                    dst_sb[:, tt * 128:(tt + 1) * 128], ps_tr[0:32, 0:128]),
                    waits=(("t", mt2),), inc="v")
                m_tr_read, tr_sem = mv2, "v"
                m_brop[(tt, half)] = mv2
        m_rope_v = p.val("v")

        if PH < 3:
            return
        for c in range(NCH):
            w = [("v", m_dt_v), ("v", m_acol)] if c == 0 else []
            p.emit("vector", lambda e, c=c: e.tensor_scalar_mul(
                a_sc[64:96, :], dt_sb[64:96, c * 128:(c + 1) * 128],
                acol[64:96, :]), waits=tuple(w), inc="v")
            p.emit("vector", lambda e, c=c: e.tensor_tensor_scan(
                cs_sb[64:96, c * 128:(c + 1) * 128], ctx.onesF[64:96, 0:128],
                a_sc[64:96, :], 0.0, ALU.mult, ALU.add), inc="v")
        m_cs_v = p.val("v")
        m_cs_dma = p.emit("sync", lambda e: e.dma_start(
            out=cs_dram, in_=cs_sb[64:96, :]), waits=(("v", m_cs_v),),
            inc="qx", inc_by=16)
        for tt in range(NCH):
            for srcb, dstb in ((cs_sb, cstok), (dt_sb, dttok)):
                bi, bank, war = r_sc2.take()
                w = [("v", m_cs_v)] + list(war)
                mt = p.emit("tensor", lambda e, tt=tt, srcb=srcb, bank=bank:
                            e.transpose(
                    bank[:, 0:32], srcb[64:96, tt * 128:(tt + 1) * 128],
                    ctx.identF[64:96, 64:96]), waits=tuple(w), inc="t")
                mv = p.emit("vector", lambda e, tt=tt, dstb=dstb, bank=bank:
                            e.tensor_copy(
                    dstb[:, tt * 32:(tt + 1) * 32], bank[:, 0:32]),
                    waits=(("t", mt),), inc="v")
                r_sc2.release(bi, "v", mv)
        m_tok_v = p.val("v")
        m_ncs_v = m_tok_v

        if PH < 4:
            return
        p.emit("vector", lambda e: e.memset(s_f[:], 0.0), inc="v")
        p.emit("vector", lambda e: e.memset(s_b[:], 0.0), inc="v")
        m_sbf_v = p.val("v")

        wout_m = {}
        wout_free = [0, 0]

        def pf_wout(i):
            tc, ot = i // 8, i % 8
            if i >= 32:
                return
            wout_m[i] = p.emit("gpsimd", lambda e, ot=ot: e.dma_start(
                out=woutsl[:], in_=d_wout[ot]),
                waits=(("t", wout_free[0]),), inc="qw", inc_by=16)

        xsc_m = [0, 0]
        zc_m = [0, 0]
        m_spills_done = spl_m[(NTC - 1, 31)]

        def load_chunk(c):
            for dram_t, dst, mlist in ((xs_dram, xsc, xsc_m),
                                       (z_dram, zc, zc_m)):
                srcap = dram_t.rearrange("k p t -> p k t")[
                    :, :, c * 128:(c + 1) * 128]
                w = [("qx", m_spills_done), ("t", p.val("t")),
                     ("v", p.val("v")), ("s", p.val("s"))]
                mlist[0] = p.emit("sync", lambda e, dst=dst,
                                  srcap=srcap: e.dma_start(
                    out=dst[:].rearrange("p (k t) -> p k t", t=128),
                    in_=srcap),
                    waits=tuple(w), inc="qx", inc_by=16)

        m_msc_read = 0
        m_ltf_read = 0
        m_ltb_read = 0
        m_pt_read = 0
        m_csrep_users_s = 0
        m_csrep_users_v = 0
        m_ecs_read = 0
        m_cp_read = 0
        m_dac_read = 0
        m_dfac_read = 0
        m_bpp_read = 0
        m_psy_stt = 0
        m_ytmp_read = 0
        m_zsil_read = 0
        m_yread_t = 0
        m_csrep = 0

        for tc in range(NTC):
            for cc4 in range(4):
                c = tc * 4 + cc4
                sl = 0
                load_chunk(c)
                bi_sc, b_sc, war_sc = r_sc2.take()
                w = list(war_sc) + [("v", m_rope_v)]
                if c == 0:
                    w.append(("v", m_sbf_v))
                    w.append(("v", m_pads))
                mt_sc = p.emit("tensor", lambda e, c=c, b=b_sc: e.matmul(
                    b[:, 0:128], brop[:, c * 128:(c + 1) * 128],
                    crop[:, c * 128:(c + 1) * 128], start=True, stop=True),
                    waits=tuple(w), inc="t")
                w = [("t", mt_sc)]
                if m_msc_read:
                    w.append(("t", m_msc_read))
                mv_msc = p.emit("vector", lambda e, b=b_sc: e.tensor_mul(
                    msc[:], b[:, 0:128], ctx.maskA[:]), waits=tuple(w), inc="v")
                r_sc2.release(bi_sc, "v", mv_msc)

                for half in range(2):
                    w = [("qx", m_cs_dma)]
                    if m_csrep_users_s:
                        w += [("s", m_csrep_users_s), ("v", m_csrep_users_v)]
                    base_ap = cs_dram[half * 16:(half + 1) * 16,
                                      c * 128:(c + 1) * 128]
                    rep_ap = base_ap.unsqueeze(0).partition_broadcast(128)
                    m_csrep = p.emit("sync", lambda e, rep_ap=rep_ap:
                                     e.dma_start(
                        out=csrep[:].rearrange("p (h t) -> p h t", t=128),
                        in_=rep_ap), waits=tuple(w), inc="qx", inc_by=16)
                    for hh in range(16):
                        h = half * 16 + hh
                        hb = (h % 2) * 64
                        kt_x = h // 2
                        xslc = slice(kt_x * 128, (kt_x + 1) * 128)
                        w = [("qx", xsc_m[0])]
                        if m_tr_read:
                            w.append((tr_sem, m_tr_read))
                        mt_x = p.emit("tensor", lambda e, hb=hb, xslc=xslc:
                                      e.transpose(
                            ps_tr[:, 0:64], xsc[hb:hb + 64, xslc],
                            ctx.ident[hb:hb + 64, hb:hb + 64]),
                            waits=tuple(w), inc="t")
                        mv_x = p.emit("vector", lambda e, h=h, c=c:
                                      e.tensor_scalar_mul(
                            xtc[:, h * 64:(h + 1) * 64],
                            ps_tr[:, 0:64],
                            dttok[:, c * 32 + h:c * 32 + h + 1]),
                            waits=(("t", mt_x), ("v", m_tok_v)), inc="v")
                        m_tr_read, tr_sem = mv_x, "v"
                        w = [("qx", m_csrep), ("v", m_ncs_v)]
                        if m_ltf_read:
                            w.append(("s", m_ltf_read))
                        mv_ltf = p.emit("vector", lambda e, hh=hh, c=c, h=h:
                                        e.tensor_scalar(
                            ltf[:], csrep[:, hh * 128:(hh + 1) * 128],
                            cstok[:, c * 32 + h:c * 32 + h + 1], 0.0,
                            ALU.subtract, ALU.min), waits=tuple(w), inc="v")
                        w = [("v", mv_ltf)]
                        if m_ltb_read:
                            w.append(("v", m_ltb_read))
                        m_ltf_read = p.emit("scalar", lambda e: e.activation(
                            ltb[:], ltf[:], AF.Exp), waits=tuple(w), inc="s")
                        w = [("s", m_ltf_read), ("v", mv_msc)]
                        if m_pt_read:
                            w.append(("t", m_pt_read))
                        m_ltb_read = p.emit("vector", lambda e: e.tensor_mul(
                            ptb[:], msc[:], ltb[:]), waits=tuple(w), inc="v")
                        mv_pt = m_ltb_read
                        w = [("qx", m_csrep)]
                        if m_ecs_read:
                            w.append(("v", m_ecs_read))
                        ms_ecs = p.emit("scalar", lambda e, hh=hh: e.activation(
                            ecs_sb[:], csrep[0:32, hh * 128:(hh + 1) * 128],
                            AF.Exp), waits=tuple(w), inc="s")
                        w = [("s", ms_ecs), ("v", m_rope_v)]
                        if m_cp_read:
                            w.append(("t", m_cp_read))
                        m_ecs_read = p.emit("vector", lambda e, c=c: e.tensor_mul(
                            cp_sb[:], crop[:, c * 128:(c + 1) * 128], ecs_sb[:]),
                            waits=tuple(w), inc="v")
                        mv_cp = m_ecs_read
                        w = [("qx", m_csrep)]
                        if m_dac_read:
                            w.append(("v", m_dac_read))
                        ms_dac = p.emit("scalar", lambda e, hh=hh: e.activation(
                            dac[:], csrep[0:32, hh * 128 + 127:hh * 128 + 128],
                            AF.Exp), waits=tuple(w), inc="s")
                        w = [("qx", m_csrep), ("v", m_ncs_v)]
                        if m_dfac_read:
                            w.append(("v", m_dfac_read))
                        ms_dfac = p.emit("scalar", lambda e, hh=hh, c=c, h=h:
                                         e.activation(
                            dfac[:], cstok[:, c * 32 + h:c * 32 + h + 1],
                            AF.Exp, scale=-1.0,
                            bias=csrep[:, hh * 128 + 127:hh * 128 + 128]),
                            waits=tuple(w), inc="s")
                        w = [("s", ms_dfac), ("v", m_rope_v)]
                        if m_bpp_read:
                            w.append(("t", m_bpp_read))
                        mv_bpp = p.emit("vector", lambda e, c=c: e.tensor_scalar_mul(
                            bpp[:, 0:32], btr[:, c * 32:(c + 1) * 32], dfac[:]),
                            waits=tuple(w), inc="v")
                        m_dfac_read = mv_bpp
                        w = [("v", mv_pt), ("v", mv_x)]
                        if m_psy_stt:
                            w.append(("v", m_psy_stt))
                        p.emit("tensor", lambda e, h=h, hb=hb: e.matmul(
                            ps_y[hb:hb + 64, :],
                            xtc[:, h * 64:(h + 1) * 64],
                            ptb[:], start=True, stop=False), waits=tuple(w))
                        m_cp_read = p.emit("tensor", lambda e, h=h, hb=hb:
                                           e.matmul(
                            ps_y[hb:hb + 64, :],
                            s_b[:, h * 64:(h + 1) * 64], cp_sb[:],
                            start=False, stop=True),
                            waits=(("v", mv_cp), ("v", m_sbf_v)), inc="t")
                        mt_y = m_cp_read
                        bi_s, b_si, war_si = r_sc2.take()
                        w = list(war_si) + [("v", mv_bpp), ("v", mv_x)]
                        if c == 0 and h == 0:
                            w.append(("v", m_pads))
                        m_bpp_read = p.emit("tensor", lambda e, h=h,
                                            b=b_si: e.matmul(
                            b[:, 0:128], bpp[:],
                            xtc[:, h * 64: h * 64 + 128],
                            start=True, stop=True), waits=tuple(w), inc="t")
                        mt_si = m_bpp_read
                        m_pt_read = mt_si
                        w = [("t", mt_si), ("s", ms_dac)]
                        mv_su = p.emit("vector", lambda e, h=h, b=b_si:
                                       e.scalar_tensor_tensor(
                            s_f[:, h * 64:(h + 1) * 64],
                            s_f[:, h * 64:(h + 1) * 64], dac[:],
                            b[0:32, 0:64], ALU.mult, ALU.add),
                            waits=tuple(w), inc="v")
                        r_sc2.release(bi_s, "v", mv_su)
                        m_dac_read = mv_su
                        m_sbf_v = p.emit("vector", lambda e, h=h: e.tensor_copy(
                            s_b[:, h * 64:(h + 1) * 64],
                            s_f[:, h * 64:(h + 1) * 64]), inc="v")
                        w = [("qx", zc_m[0])]
                        if m_zsil_read:
                            w.append(("v", m_zsil_read))
                        ms_z = p.emit("scalar", lambda e, hb=hb, xslc=xslc:
                                      e.activation(
                            zsil[hb:hb + 64, :], zc[hb:hb + 64, xslc], AF.Silu),
                            waits=tuple(w), inc="s")
                        w = [("t", mt_y), ("qx", xsc_m[0]), ("qx", m_consts)]
                        if m_ytmp_read:
                            w.append(("v", m_ytmp_read))
                        m_psy_stt = p.emit("vector", lambda e, hb=hb, xslc=xslc,
                                           kt_x=kt_x: e.scalar_tensor_tensor(
                            ytmp[hb:hb + 64, 0:128], xsc[hb:hb + 64, xslc],
                            dexp[hb:hb + 64, kt_x:kt_x + 1],
                            ps_y[hb:hb + 64, :], ALU.mult, ALU.add),
                            waits=tuple(w), inc="v")
                        if c == 0 and h == 0:
                            import os as _os2
                            if _os2.environ.get("L2DBG"):
                                dbg = p.dram("l2dbg", [128, 5 * 128], F32,
                                             kind="ExternalOutput")
                                for col, (buf, mm) in enumerate((
                                        (ltb, ("s", m_ltf_read)),
                                        (ptb, ("v", mv_pt)),
                                        (msc, ("v", mv_msc)),
                                        (xtc, ("v", mv_x)),
                                        (ltf, ("v", mv_ltf)))):
                                    p.emit("gpsimd", lambda e, col=col, buf=buf:
                                           e.dma_start(
                                        out=dbg[:, col * 128:(col + 1) * 128],
                                        in_=buf[:, 0:128]),
                                        waits=(mm, ("v", m_psy_stt)),
                                        inc="qw", inc_by=16)
                        w = [("v", m_psy_stt), ("s", ms_z)]
                        if tc > 0 and c % 4 == 0 and half == 0 and hh == 0:
                            w.append(("t", m_yread_t))
                        m_ytmp_read = p.emit("vector", lambda e, hb=hb, kt_x=kt_x,
                                             cc4=cc4: e.tensor_mul(
                            ywin[hb:hb + 64,
                                 kt_x * S5 + cc4 * 128: kt_x * S5 + (cc4 + 1) * 128],
                            ytmp[hb:hb + 64, 0:128], zsil[hb:hb + 64, :]),
                            waits=tuple(w), inc="v")
                        m_zsil_read = m_ytmp_read
                    m_csrep_users_s = p.val("s")
                    m_csrep_users_v = p.val("v")
            m_ywin_v = p.val("v")
            if PH < 5:
                continue

            for kt in range(NCH):
                w = [("v", m_ywin_v)] if kt == 0 else []
                if tc > 0 and kt == 0:
                    w.append(("t", m_sq_prev))
                p.emit("scalar", lambda e, kt=kt: e.activation(
                    ctx.ln_sq[:, (kt % 8) * S5:(kt % 8 + 1) * S5]
                    if False else
                    ctx.ln_sq[:, (kt % CT) * S5:(kt % CT + 1) * S5],
                    ywin[:, kt * S5:(kt + 1) * S5], AF.Square),
                    waits=tuple(w), inc="s") if False else None
            bi_r, b_r, war_r = r_a.take()
            first = True
            for half8 in range(2):
                for k8 in range(8):
                    kt = half8 * 8 + k8
                    w = [("v", m_ywin_v)] if first else []
                    if tc > 0 and first:
                        w.append(("t", m_sq_prev))
                    if half8 == 1 and k8 == 0:
                        w.append(("t", p.val("t")))
                    first = False
                    p.emit("scalar", lambda e, kt=kt, k8=k8: e.activation(
                        ctx.ln_sq[:, k8 * S5:(k8 + 1) * S5],
                        ywin[:, kt * S5:(kt + 1) * S5], AF.Square),
                        inc="s", waits=tuple(w))
                m_sqy = p.val("s")
                rowb = half8 * 64
                for k8 in range(8):
                    w = []
                    if half8 == 0 and k8 == 0:
                        w = list(war_r) + [("s", m_sqy)]
                    elif k8 == 0:
                        w = [("s", m_sqy)]
                    p.emit("tensor", lambda e, k8=k8, rowb=rowb, b=b_r: e.matmul(
                        b[rowb:rowb + 1, :], ctx.ones_bf[:],
                        ctx.ln_sq[:, k8 * S5:(k8 + 1) * S5],
                        start=(k8 == 0), stop=(k8 == 7)),
                        waits=tuple(w), inc="t" if k8 == 7 else None)
            m_sq_prev = p.val("t")
            p.emit("vector", lambda e, b=b_r: e.tensor_scalar_mul(
                R[:, 0:S5], b[0:1, :], 1.0 / 2048.0),
                waits=(("t", m_sq_prev),), inc="v")
            p.emit("vector", lambda e, b=b_r: e.scalar_tensor_tensor(
                R[:, 0:S5], b[64:65, :], 1.0 / 2048.0, R[:, 0:S5],
                ALU.mult, ALU.add), inc="v")
            mvx = p.val("v")
            r_a.release(bi_r, "v", mvx)
            p.emit("vector", lambda e: e.tensor_scalar_add(
                R[:, 0:S5], R[:, 0:S5], 1e-6), inc="v")
            m_r1 = p.val("v")
            p.emit("scalar", lambda e: e.activation(
                R[:, S5:2 * S5], R[:, 0:S5], AF.Sqrt),
                waits=(("v", m_r1),), inc="s")
            m_r2 = p.val("s")
            p.emit("vector", lambda e: e.reciprocal(
                R[:, 2 * S5:3 * S5], R[:, S5:2 * S5]),
                waits=(("s", m_r2),), inc="v")
            m_r3 = p.val("v")
            bi_r, b_r, war_r = r_a.take()
            mt_rep = p.emit("tensor", lambda e, b=b_r: e.matmul(
                b[:], ctx.onesF[0:1, 0:128], R[:, 2 * S5:3 * S5],
                start=True, stop=True),
                waits=tuple(list(war_r) + [("v", m_r3)]), inc="t")
            mv_rep = p.emit("vector", lambda e, b=b_r: e.tensor_copy(
                ctx.ln_arep[:], b[:]), waits=(("t", mt_rep),), inc="v")
            r_a.release(bi_r, "v", mv_rep)
            hs_war = [0, 0]
            for kt in range(NCH):
                hs = kt % 2
                w = [("v", mv_rep)] if kt == 0 else []
                if hs_war[hs]:
                    w.append(("v", hs_war[hs]))
                p.emit("vector", lambda e, kt=kt, hs=hs: e.tensor_mul(
                    hsc[:, hs * S5:(hs + 1) * S5], ywin[:, kt * S5:(kt + 1) * S5],
                    ctx.ln_arep[:]), waits=tuple(w), inc="v")
                mvk = p.emit("vector", lambda e, kt=kt, hs=hs: e.tensor_scalar_mul(
                    ywin[:, kt * S5:(kt + 1) * S5], hsc[:, hs * S5:(hs + 1) * S5],
                    rmsw[:, kt:kt + 1]), inc="v")
                hs_war[hs] = mvk
            m_yfin_v = p.val("v")

            def wout_emitter(tci, ot, bank, waits, tc=tc, m_yfin_v=m_yfin_v):
                i = tc * 8 + ot
                pf_wout(i)
                mt = None
                for kt in range(NCH):
                    w = list(waits) if kt == 0 else []
                    if kt == 0:
                        w.append(("qw", wout_m[i]))
                        if ot == 0:
                            w.append(("v", m_yfin_v))
                    mt = p.emit("tensor", lambda e, ot=ot, kt=kt,
                                bank=bank: e.matmul(
                        bank[:], woutsl[:, kt * 128:(kt + 1) * 128],
                        ywin[:, kt * S5:(kt + 1) * S5],
                        start=(kt == 0), stop=(kt == NCH - 1)),
                        waits=tuple(w), inc="t" if kt == NCH - 1 else None)
                wout_free[0] = p.val("t")
                return mt

            if tc == 0:
                ol.stage_extra = [("t", m_hln[NTC - 1])]
                if _os.environ.get("L2DBG2"):
                    dbg2 = p.dram("l2dbg2", [128, 16384], F32,
                                  kind="ExternalOutput")
                    for col, (buf, r0, r1, nw, mm) in enumerate((
                            (dt_sb, 64, 96, T, ("v", m_dt_v)),
                            (cs_sb, 64, 96, T, ("v", m_cs_v)),
                            (brop, 0, 32, T, ("v", m_rope_v)),
                            (crop, 0, 32, T, ("v", m_rope_v)),
                            (ywin, 0, 128, 16 * 512, ("v", m_yfin_v)))):
                        p.emit("gpsimd", lambda e, col=col, buf=buf, r0=r0,
                               r1=r1, nw=nw: e.dma_start(
                            out=dbg2[r0:r1, col * 2048:col * 2048 + nw]
                            if nw == T else dbg2[r0:r1, 8192:16384],
                            in_=buf[r0:r1, 0:nw]), waits=(mm,),
                            inc="qw", inc_by=16)
            ol.chunk(tc, wout_emitter, stage_mult=2.0)
            m_yread_t = p.val("t")
    if PH >= 5:
        ol.end()


def prep_l2_ffn(inputs):
    d = {}
    d["l2f_wq"] = lhsT_blocked(np.ascontiguousarray(np.asarray(inputs["L2_ffn_Wq"])))
    d["l2f_wk"] = lhsT_blocked(np.ascontiguousarray(np.asarray(inputs["L2_ffn_Wk"])))
    g = np.zeros((1024, 512), np.float32)
    g[:, 0:40] = np.asarray(inputs["L2_ffn_Ws"])
    g[:, 40:80] = np.asarray(inputs["L2_ffn_Wd"])
    d["l2f_wg"] = rhs_blocked(g)
    wv = np.asarray(inputs["L2_ffn_Wv"]).transpose(1, 0, 2).reshape(1024, 320)
    wvp = np.zeros((1024, 512), np.float32)
    wvp[:, :320] = wv
    d["l2f_wv"] = rhs_blocked(wvp)
    wo = np.asarray(inputs["L2_ffn_Wo"]).reshape(320, 1024)
    wop = np.zeros((384, 1024), np.float32)
    wop[:320] = wo
    d["l2f_wo"] = lhsT_blocked(wop)
    return d


def emit_l2_ffn(p, ctx):
    S5 = 512
    NH, DH, NE = 8, 64, 5
    d_wq = p.dram("l2f_wq", [4, 128, 8 * 128], F32, kind="ExternalInput")
    d_wk = p.dram("l2f_wk", [4, 128, 8 * 128], F32, kind="ExternalInput")
    d_wg = p.dram("l2f_wg", [8, 128, 512], F32, kind="ExternalInput")
    d_wv = p.dram("l2f_wv", [8, 128, 512], F32, kind="ExternalInput")
    d_wo = p.dram("l2f_wo", [8, 128, 3 * 128], F32, kind="ExternalInput")

    qT = p.sb("l2f_qT", [128, 4 * T], BF16)
    kT = p.sb("l2f_kT", [128, 4 * T], BF16)
    att_sc = p.sb("l2f_attsc", [128, S5], BF16)
    atok = p.sb("l2f_atok", [128, 4 * 512], BF16)
    v8 = p.sb("l2f_v8", [128, NTT * (NH * 65)], BF16)
    gl = p.sb("l2f_gl", [128, 2 * 80], F32)
    sig = p.sb("l2f_sig", [128, 40], F32)
    glf = p.sb("l2f_glf", [128, 40], F32)
    msk1 = p.sb("l2f_msk1", [128, 40], F32)
    msk2 = p.sb("l2f_msk2", [128, 40], F32)
    m1 = p.sb("l2f_m1", [128, 16], F32)
    maskv = p.sb("l2f_maskv", [128, NTT * 80], F32)
    xv_sb = p.sb("l2f_xv", [128, 2 * 320], BF16)
    se_sb = p.sb("l2f_se", [128, 320], BF16)
    seT = p.sb("l2f_seT", [128, 3 * T], BF16)
    pT = p.sb("l2f_pT", [128, 2 * S5], BF16)
    zrep = p.sb("l2f_zrep", [128, S5], F32)
    zr = p.sb("l2f_zr", [1, S5], F32)
    wgsl = p.sb("l2f_wgsl", [128, 8 * 512], BF16)
    wsl = wgsl
    wosl = p.sb("l2f_wosl", [128, 2 * 384], BF16)

    xw = list(ctx.xbf_wait)

    with ExitStack() as pstack:
        PS = lambda nm, shape, dt=F32: pstack.enter_context(
            p.nc.psum_tensor(p._n(nm), shape, dt))
        r_st = Rot([PS("l2fs", [128, S5]) for _ in range(2)])
        r_o = Rot([PS("l2fo", [128, S5]) for _ in range(2)])
        r_z = Rot([PS("l2fz", [128, S5])])
        ps_tr = PS("l2ftr", [128, 128], BF16)
        r_a = Rot([PS("l2fa", [128, S5]) for _ in range(2)])

        m_wg = p.emit("gpsimd", lambda e: e.dma_start(
            out=wgsl[:].rearrange("p (k f) -> p k f", k=8),
            in_=d_wg.rearrange("k p f -> p k f")), inc="qw", inc_by=16)
        wq_m = {}
        wq_free = [0, 0]

        def pf_w(i, dsrc):
            sl = i % 2
            ot = i % 4
            wq_m[i] = p.emit("gpsimd", lambda e, ot=ot, sl=sl, dsrc=dsrc:
                             e.dma_start(
                out=wsl[:, sl * 1024:(sl + 1) * 1024], in_=dsrc[ot]),
                waits=(("t", wq_free[sl]),), inc="qw", inc_by=16)

        gl_war = [0, 0]
        for tt in range(NTT):
            bi, bank, war = r_a.take()
            for kt in range(CT):
                w = list(war) if kt == 0 else []
                if tt == 0 and kt == 0:
                    w.append(("qw", m_wg))
                    w += xw
                mt = p.emit("tensor", lambda e, tt=tt, kt=kt, bank=bank:
                            e.matmul(
                    bank[:],
                    ctx.xbf[:, kt * T + tt * 128: kt * T + (tt + 1) * 128],
                    wgsl[:, kt * 512:(kt + 1) * 512],
                    start=(kt == 0), stop=(kt == CT - 1)),
                    waits=tuple(w), inc="t" if kt == CT - 1 else None)
            sl = tt % 2
            w = [("t", mt)]
            if gl_war[sl]:
                w.append(("v", gl_war[sl]))
            mv_gl = p.emit("vector", lambda e, sl=sl, bank=bank: e.tensor_copy(
                gl[:, sl * 80:(sl + 1) * 80], bank[:, 0:80]),
                waits=tuple(w), inc="v")
            r_a.release(bi, "v", mv_gl)
            for half in range(2):
                c0g = sl * 80 + half * 40
                c0 = tt * 80 + half * 40
                p.emit("vector", lambda e, c0g=c0g: e.tensor_copy(
                    glf[:], gl[:, c0g:c0g + 40]),
                    waits=(("v", p.val("v")),), inc="v")
                if half == 0:
                    p.emit("scalar", lambda e, c0g=c0g: e.activation(
                        sig[:], gl[:, c0g:c0g + 40], AF.Sigmoid),
                        waits=(("v", mv_gl), ("v", p.val("v"))), inc="s")
                    m_sig = p.val("s")
                p.emit("vector", lambda e: e.reduce_max(
                    m1[:, 0:8].rearrange("p (a b) -> p a b", b=1),
                    glf[:].rearrange("p (h e) -> p h e", e=5),
                    axis=mybir.AxisListType.X),
                    waits=(("v", p.val("v")),), inc="v")
                for h in range(NH):
                    p.emit("vector", lambda e, h=h: e.tensor_single_scalar(
                        msk1[:, h * 5:(h + 1) * 5], glf[:, h * 5:(h + 1) * 5],
                        m1[:, h:h + 1], ALU.is_ge),
                        waits=(("v", p.val("v")),), inc="v")
                p.emit("vector", lambda e: e.scalar_tensor_tensor(
                    glf[:], msk1[:], -65000.0, glf[:], ALU.mult, ALU.add),
                    waits=(("v", p.val("v")),), inc="v")
                p.emit("vector", lambda e: e.reduce_max(
                    m1[:, 8:16].rearrange("p (a b) -> p a b", b=1),
                    glf[:].rearrange("p (h e) -> p h e", e=5),
                    axis=mybir.AxisListType.X),
                    waits=(("v", p.val("v")),), inc="v")
                for h in range(NH):
                    p.emit("vector", lambda e, h=h: e.tensor_single_scalar(
                        msk2[:, h * 5:(h + 1) * 5], glf[:, h * 5:(h + 1) * 5],
                        m1[:, 8 + h:9 + h], ALU.is_ge),
                        waits=(("v", p.val("v")),), inc="v")
                p.emit("vector", lambda e, c0=c0: e.tensor_add(
                    maskv[:, c0:c0 + 40], msk1[:], msk2[:]),
                    waits=(("v", p.val("v")),), inc="v")
                if half == 0:
                    p.emit("vector", lambda e, c0=c0: e.tensor_mul(
                        maskv[:, c0:c0 + 40], maskv[:, c0:c0 + 40], sig[:]),
                        waits=(("v", p.val("v")), ("s", m_sig)), inc="v")
            gl_war[sl] = p.val("v")
        m_masks = p.val("v")

        m_wv2 = p.emit("gpsimd", lambda e: e.dma_start(
            out=wgsl[:].rearrange("p (k f) -> p k f", k=8),
            in_=d_wv.rearrange("k p f -> p k f")),
            waits=(("t", p.val("t")),), inc="qw", inc_by=16)
        xv_war = [0, 0]
        for tt in range(NTT):
            bi, bank, war = r_a.take()
            for kt in range(CT):
                w = list(war) if kt == 0 else []
                if tt == 0 and kt == 0:
                    w.append(("qw", m_wv2))
                mt = p.emit("tensor", lambda e, tt=tt, kt=kt, bank=bank:
                            e.matmul(
                    bank[:],
                    ctx.xbf[:, kt * T + tt * 128: kt * T + (tt + 1) * 128],
                    wgsl[:, kt * 512:(kt + 1) * 512],
                    start=(kt == 0), stop=(kt == CT - 1)),
                    waits=tuple(w), inc="t" if kt == CT - 1 else None)
            sl = tt % 2
            w = [("t", mt)]
            if xv_war[sl]:
                w.append(("v", xv_war[sl]))
            ms = p.emit("scalar", lambda e, sl=sl, bank=bank: e.copy(
                xv_sb[:, sl * 320:(sl + 1) * 320], bank[:, 0:320]),
                waits=tuple(w), inc="s")
            r_a.release(bi, "s", ms)
            for h in range(NH):
                base = tt * (NH * 65) + h * 65
                for e_ in range(NE):
                    scol = maskv[:, tt * 80 + h * 5 + e_:
                                 tt * 80 + h * 5 + e_ + 1]
                    xcol = xv_sb[:, sl * 320 + e_ * 64: sl * 320 + (e_ + 1) * 64]
                    w = [("s", ms), ("v", p.val("v"))]
                    if e_ == 0:
                        p.emit("vector", lambda e, base=base, scol=scol,
                               xcol=xcol: e.tensor_scalar_mul(
                            v8[:, base:base + 64], xcol, scol),
                            waits=tuple(w), inc="v")
                    else:
                        p.emit("vector", lambda e, base=base, scol=scol,
                               xcol=xcol: e.scalar_tensor_tensor(
                            v8[:, base:base + 64], xcol, scol,
                            v8[:, base:base + 64], ALU.mult, ALU.add),
                            waits=tuple(w), inc="v")
                p.emit("vector", lambda e, base=base: e.memset(
                    v8[:, base + 64:base + 65], 1.0), inc="v")
            xv_war[sl] = p.val("v")
        m_v8 = p.val("v")

        wq_free[0] = p.val("t")
        wq_free[1] = p.val("t")
        pf_w(0, d_wq)
        pf_w(1, d_wq)
        for which, dst in ((0, qT), (1, kT)):
            for ot in range(4):
                i = which * 4 + ot
                sl = i % 2
                for tc in range(NTC):
                    bi, bank, war = r_st.take()
                    for kt in range(CT):
                        w = list(war) if kt == 0 else []
                        if kt == 0:
                            w.append(("qw", wq_m[i]))
                            if i == 0 and tc == 0:
                                w += xw
                        mt = p.emit("tensor", lambda e, tc=tc, kt=kt, sl=sl,
                                    bank=bank: e.matmul(
                            bank[:], wsl[:, sl * 1024 + kt * 128:
                                         sl * 1024 + (kt + 1) * 128],
                            ctx.xbf[:, kt * T + tc * S5: kt * T + (tc + 1) * S5],
                            start=(kt == 0), stop=(kt == CT - 1)),
                            waits=tuple(w), inc="t" if kt == CT - 1 else None)
                    ms = p.emit("scalar", lambda e, ot=ot, tc=tc, dst=dst,
                                bank=bank: e.copy(
                        dst[:, ot * T + tc * S5: ot * T + (tc + 1) * S5],
                        bank[:]), waits=(("t", mt),), inc="s")
                    r_st.release(bi, "s", ms)
                wq_free[sl] = p.val("t")
                if i + 2 < 8:
                    pf_w(i + 2, d_wq if i + 2 < 4 else d_wk)

        m_qk = p.val("s")

        pT_war = [0, 0]
        m_tr = 0
        p.emit("vector", lambda e: e.memset(seT[:], 0.0), inc="v")
        m_seT = p.val("v")
        for ic in range(NTC):
            for h in range(NH):
                hb = (h % 2) * 64
                htile = h // 2
                bi_o, bank_o, war_o = r_o.take()
                bi_z, bank_z, war_z = r_z.take()
                w_first_pv = list(war_o)
                w_first_z = list(war_z)
                exp_ms = {}
                stq = {}

                def emit_pv(pidx, last, h=h, hb=hb, bank_o=bank_o,
                            bank_z=bank_z, w_first_pv=w_first_pv,
                            w_first_z=w_first_z, exp_ms=exp_ms, stq=stq):
                    pjt, pslot = stq[pidx]
                    w = [exp_ms[pidx]]
                    if pidx == 0:
                        w += w_first_pv
                    p.emit("tensor", lambda e, pjt=pjt, pslot=pslot, hb=hb:
                           e.matmul(
                        bank_o[hb:hb + 64, :],
                        v8[:, pjt * 520 + h * 65: pjt * 520 + h * 65 + 64],
                        pT[:, pslot * S5:(pslot + 1) * S5],
                        start=(pidx == 0), stop=last), waits=tuple(w))
                    w = []
                    if pidx == 0:
                        w += w_first_z
                    mz = p.emit("tensor", lambda e, pjt=pjt, pslot=pslot:
                                e.matmul(
                        bank_z[0:1, :],
                        v8[:, pjt * 520 + h * 65 + 64: pjt * 520 + h * 65 + 65],
                        pT[:, pslot * S5:(pslot + 1) * S5],
                        start=(pidx == 0), stop=last), waits=tuple(w), inc="t")
                    pT_war[pslot] = mz
                    return mz

                for jt in range(NTT):
                    bi_s, bank_s, war_s = r_st.take()
                    w = list(war_s)
                    if h == 0 and ic == 0 and jt == 0:
                        w += [("s", m_qk), ("v", m_v8)]
                    mt_s = p.emit("tensor", lambda e, jt=jt, hb=hb, htile=htile,
                                  ic=ic, bank=bank_s: e.matmul(
                        bank[:], kT[hb:hb + 64,
                                    htile * T + jt * 128: htile * T + (jt + 1) * 128],
                        qT[hb:hb + 64,
                           htile * T + ic * S5: htile * T + (ic + 1) * S5],
                        start=True, stop=True), waits=tuple(w), inc="t")
                    if jt >= 2:
                        emit_pv(jt - 2, last=False)
                    slot = jt % 2
                    ms_e = p.emit("scalar", lambda e, slot=slot, bank=bank_s:
                                  e.activation(
                        pT[:, slot * S5:(slot + 1) * S5], bank[:],
                        AF.Exp, scale=0.125),
                        waits=(("t", mt_s), ("t", pT_war[slot])), inc="s")
                    exp_ms[jt] = ("s", ms_e)
                    r_st.release(bi_s, "s", ms_e)
                    stq[jt] = (jt, slot)
                for pidx in (NTT - 2, NTT - 1):
                    emit_pv(pidx, last=(pidx == NTT - 1))
                mt_pv = p.val("t")
                mzr = p.emit("vector", lambda e, bank=bank_z: e.reciprocal(
                    zr[:], bank[0:1, :]), waits=(("t", mt_pv),), inc="v")
                mt_rep = p.emit("tensor", lambda e, hb=hb, bank=bank_z: e.matmul(
                    bank[hb:hb + 64, :], ctx.onesF[0:1, 0:64], zr[:],
                    start=True, stop=True), waits=(("v", mzr),), inc="t")
                mv_rep = p.emit("vector", lambda e, hb=hb, bank=bank_z:
                                e.tensor_copy(zrep[hb:hb + 64, :],
                                              bank[hb:hb + 64, :]),
                                waits=(("t", mt_rep),), inc="v")
                mnorm = p.emit("vector", lambda e, hb=hb, bank=bank_o:
                               e.tensor_mul(
                    att_sc[hb:hb + 64, :],
                    bank[hb:hb + 64, :], zrep[hb:hb + 64, :]),
                    waits=(("v", mv_rep),), inc="v")
                r_o.release(bi_o, "v", mnorm)
                r_z.release(bi_z, "v", mnorm)
                for j4 in range(4):
                    w = [("v", mnorm)]
                    if m_tr:
                        w.append(("v", m_tr))
                    mt = p.emit("tensor", lambda e, hb=hb, j4=j4: e.transpose(
                        ps_tr[:, 0:64],
                        att_sc[hb:hb + 64, j4 * 128:(j4 + 1) * 128],
                        ctx.ident[hb:hb + 64, hb:hb + 64]),
                        waits=tuple(w), inc="t")
                    m_tr = p.emit("vector", lambda e, j4=j4, h=h: e.tensor_copy(
                        atok[:, j4 * 512 + h * 64: j4 * 512 + (h + 1) * 64],
                        ps_tr[:, 0:64]), waits=(("t", mt),), inc="v")
            m_atok = p.val("v")
            for j4 in range(4):
                tt = ic * 4 + j4
                for e_ in range(NE):
                    for h in range(NH):
                        mcol = maskv[:, tt * 80 + 40 + h * 5 + e_:
                                     tt * 80 + 40 + h * 5 + e_ + 1]
                        acol2 = atok[:, j4 * 512 + h * 64: j4 * 512 + (h + 1) * 64]
                        w = [("v", p.val("v"))]
                        if j4 == 0 and e_ == 0 and h == 0:
                            w.append(("v", m_atok))
                            w.append(("v", m_seT))
                        if h == 0:
                            p.emit("vector", lambda e, e_=e_, mcol=mcol,
                                   acol2=acol2: e.tensor_scalar_mul(
                                se_sb[:, e_ * 64:(e_ + 1) * 64], acol2, mcol),
                                waits=tuple(w), inc="v")
                        else:
                            p.emit("vector", lambda e, e_=e_, mcol=mcol,
                                   acol2=acol2: e.scalar_tensor_tensor(
                                se_sb[:, e_ * 64:(e_ + 1) * 64], acol2, mcol,
                                se_sb[:, e_ * 64:(e_ + 1) * 64],
                                ALU.mult, ALU.add), waits=tuple(w), inc="v")
                m_se = p.val("v")
                for ch in range(3):
                    ncol = 128 if ch < 2 else 64
                    w = [("v", m_se)]
                    if m_tr:
                        w.append(("v", m_tr))
                    mt = p.emit("tensor", lambda e, ch=ch, ncol=ncol:
                                e.transpose(
                        ps_tr[0:ncol, 0:128],
                        se_sb[:, ch * 128:ch * 128 + ncol], ctx.ident[:]),
                        waits=tuple(w), inc="t")
                    m_tr = p.emit("vector", lambda e, tt=tt, ch=ch, ncol=ncol:
                                  e.tensor_copy(
                        seT[0:ncol, ch * T + tt * 128: ch * T + (tt + 1) * 128],
                        ps_tr[0:ncol, 0:128]), waits=(("t", mt),), inc="v")
        m_seT = p.val("v")

    wo_m = {}
    wo_free = [0, 0]
    seq2 = [(tc, ot) for tc in range(NTC) for ot in range(CT)]

    def pf_wo2(i):
        if i >= len(seq2):
            return
        tc, ot = seq2[i]
        sl = i % 2
        wo_m[i] = p.emit("gpsimd", lambda e, ot=ot, sl=sl: e.dma_start(
            out=wosl[:, sl * 384:(sl + 1) * 384], in_=d_wo[ot]),
            waits=(("t", wo_free[sl]),), inc="qw", inc_by=16)

    pf_wo2(0)
    pf_wo2(1)
    wo_i = [0]

    def wo_emitter(tc, ot, bank, waits):
        i = wo_i[0]
        assert seq2[i] == (tc, ot)
        sl = i % 2
        mt = None
        for kt in range(3):
            w = list(waits) if kt == 0 else []
            if kt == 0:
                w.append(("qw", wo_m[i]))
                if i == 0:
                    w.append(("v", m_seT))
            mt = p.emit("tensor", lambda e, tc=tc, ot=ot, kt=kt, sl=sl,
                        bank=bank: e.matmul(
                bank[:], wosl[:, sl * 384 + kt * 128: sl * 384 + (kt + 1) * 128],
                seT[:, kt * T + tc * S5: kt * T + (tc + 1) * S5],
                start=(kt == 0), stop=(kt == 2)),
                waits=tuple(w), inc="t" if kt == 2 else None)
        wo_free[sl] = p.val("t")
        pf_wo2(i + 2)
        wo_i[0] += 1
        return mt

    emit_out_ln(p, ctx, wo_emitter, "l2f")






def prep_l3_attn(inputs):
    d = {}
    d["l3_winT"] = lhsT_blocked(np.ascontiguousarray(
        np.asarray(inputs["L3_attn_W_in"], np.float32).T))
    wx = np.asarray(inputs["L3_attn_W_x"], np.float32)
    wxp = np.zeros((2048, 128), np.float32)
    wxp[:, 0:96] = wx.T
    d["l3_wxT"] = lhsT_blocked(wxp)
    wdtT = np.asarray(inputs["L3_attn_W_dt"], np.float32).T
    d["l3_wdtT"] = np.ascontiguousarray(
        wdtT.reshape(64, 16, 128).transpose(1, 0, 2))
    d["l3_woutT"] = lhsT_blocked(np.ascontiguousarray(
        np.asarray(inputs["L3_attn_W_out"], np.float32).T))
    d["l3_binc"] = np.ascontiguousarray(
        np.asarray(inputs["L3_attn_b_in"], np.float32).reshape(32, 128).T)
    bx = np.zeros((128, 1), np.float32)
    bx[0:96, 0] = np.asarray(inputs["L3_attn_b_x"], np.float32)
    d["l3_bxc"] = bx
    d["l3_bdtc"] = np.ascontiguousarray(
        np.asarray(inputs["L3_attn_b_dt"], np.float32).reshape(16, 128).T)
    d["l3_bout"] = np.ascontiguousarray(
        np.asarray(inputs["L3_attn_b_out"], np.float32).reshape(1, C))
    d["l3_alog"] = np.ascontiguousarray(
        np.asarray(inputs["L3_attn_A_log"], np.float32)
        .reshape(16, 128, 16).transpose(1, 0, 2).reshape(128, 256))
    d["l3_dcol"] = np.ascontiguousarray(
        np.asarray(inputs["L3_attn_D"], np.float32).reshape(16, 128).T)
    return d


def emit_l3_attn(p, ctx, debug=None):
    S5 = 512
    NDT = 16
    d_win = p.dram("l3_winT", [32, 128, 8 * 128], F32, kind="ExternalInput")
    d_wx = p.dram("l3_wxT", [1, 128, 16 * 128], F32, kind="ExternalInput")
    d_wdt = p.dram("l3_wdtT", [16, 64, 128], F32, kind="ExternalInput")
    d_wout = p.dram("l3_woutT", [8, 128, 16 * 128], F32, kind="ExternalInput")
    d_binc = p.dram("l3_binc", [128, 32], F32, kind="ExternalInput")
    d_bxc = p.dram("l3_bxc", [128, 1], F32, kind="ExternalInput")
    d_bdtc = p.dram("l3_bdtc", [128, 16], F32, kind="ExternalInput")
    d_bout = p.dram("l3_bout", [1, C], F32, kind="ExternalInput")
    d_alog = p.dram("l3_alog", [128, 256], F32, kind="ExternalInput")
    d_dcol = p.dram("l3_dcol", [128, 16], F32, kind="ExternalInput")
    xin_dram = p.dram(p._n("l3_xind"), [16, 128, T], BF16)
    z_dram = p.dram(p._n("l3_zd"), [16, 128, T], BF16)
    y_dram = p.dram(p._n("l3_yd"), [16, 128, T], BF16)
    bc_dram = p.dram(p._n("l3_bcd"), [32, T], BF16)

    xt = p.sb("l3_xt", [128, T], BF16)
    zt = p.sb("l3_zt", [128, T], BF16)
    delta = p.sb("l3_delta", [128, 2 * T], F32)
    draw = p.sb("l3_draw", [64, T], BF16)
    bcsb = p.sb("l3_bc", [96, T], BF16)
    dx = p.sb("l3_dx", [128, T], BF16)
    a_b = p.sb("l3_a", [128, T], F32)
    b_b = p.sb("l3_b", [128, T], BF16)
    h_b = p.sb("l3_h", [128, T], BF16)
    yac = p.sb("l3_yac", [128, T], BF16)
    brep = p.sb("l3_brep", [128, 2 * T], BF16)
    crep = p.sb("l3_crep", [128, 2 * T], BF16)
    rowb = p.sb("l3_rowb", [1, T], BF16)
    binc = p.sb("l3_binc_s", [128, 32], F32)
    bxc = p.sb("l3_bxc_s", [128, 1], F32)
    bdtc = p.sb("l3_bdtc_s", [128, 16], F32)
    boutr = p.sb("l3_boutr", [1, C], F32)
    acol = p.sb("l3_acol", [128, 256], F32)
    dcol = p.sb("l3_dcol_s", [128, 16], F32)
    winsl = p.sb("l3_winsl", [128, 2 * 1024], BF16)
    wxsl = p.sb("l3_wxsl", [128, 16 * 128], BF16)
    wdtsl = p.sb("l3_wdtsl", [64, 2 * 128], BF16)
    woutsl = p.sb("l3_wouts", [128, 2 * 2048], BF16)
    zxst = p.sb("l3_zxst", [128, 2 * S5], BF16)

    xw = list(ctx.xbf_wait)

    with ExitStack() as pstack:
        PS = lambda nm, shape: pstack.enter_context(
            p.nc.psum_tensor(p._n(nm), shape, F32))
        r_a = Rot([PS("l3a", [128, S5]) for _ in range(4)])
        ps_rep = PS("l3rep", [128, T])

        for dst, srcd in ((binc, d_binc), (bxc, d_bxc), (bdtc, d_bdtc),
                          (boutr, d_bout), (acol, d_alog), (dcol, d_dcol)):
            p.emit("sync", lambda e, dst=dst, srcd=srcd: e.dma_start(
                out=dst[:], in_=srcd), inc="qx", inc_by=16)
        m_consts = p.val("qx")
        ms_ae = p.emit("scalar", lambda e: e.activation(
            acol[:], acol[:], AF.Exp), waits=(("qx", m_consts),), inc="s")
        m_acol = p.emit("vector", lambda e: e.tensor_scalar_mul(
            acol[:], acol[:], -1.0), waits=(("s", ms_ae),), inc="v")

        win_m = {}
        win_free = [0, 0]

        def pf_win(i):
            if i >= 32:
                return
            sl = i % 2
            win_m[i] = p.emit("gpsimd", lambda e, i=i, sl=sl: e.dma_start(
                out=winsl[:, sl * 1024:(sl + 1) * 1024], in_=d_win[i]),
                waits=(("t", win_free[sl]),), inc="qw", inc_by=16)

        pf_win(0)
        pf_win(1)
        zx_war = [0, 0]
        for ot in range(32):
            sl = ot % 2
            for tc in range(NTC):
                bi, bank, war = r_a.take()
                for kt in range(CT):
                    w = list(war) if kt == 0 else []
                    if kt == 0:
                        w.append(("qw", win_m[ot]))
                        if ot == 0 and tc == 0:
                            w += xw
                    p.emit("tensor", lambda e, tc=tc, kt=kt, sl=sl,
                           bank=bank: e.matmul(
                        bank[:], winsl[:, sl * 1024 + kt * 128:
                                       sl * 1024 + (kt + 1) * 128],
                        ctx.xbf[:, kt * T + tc * S5: kt * T + (tc + 1) * S5],
                        start=(kt == 0), stop=(kt == CT - 1)),
                        waits=tuple(w), inc="t" if kt == CT - 1 else None)
                mt = p.val("t")
                zx = tc % 2
                w = [("t", mt)]
                if zx_war[zx]:
                    w.append(("qx", zx_war[zx]))
                if ot == 0 and tc == 0:
                    w.append(("qx", m_consts))
                mv = p.emit("vector", lambda e, ot=ot, zx=zx, bank=bank:
                            e.tensor_scalar(
                    zxst[:, zx * S5:(zx + 1) * S5], bank[:],
                    binc[:, ot:ot + 1], 0.0, ALU.add, ALU.add),
                    waits=tuple(w), inc="v")
                r_a.release(bi, "v", mv)
                dtgt = xin_dram if ot < 16 else z_dram
                kt_ = ot % 16
                zx_war[zx] = p.emit("sync", lambda e, kt_=kt_, tc=tc, zx=zx,
                                    dtgt=dtgt, mv=mv: e.dma_start(
                    out=dtgt[kt_][:, tc * S5:(tc + 1) * S5],
                    in_=zxst[:, zx * S5:(zx + 1) * S5]),
                    waits=(("v", mv),), inc="qx", inc_by=16)
            win_free[sl] = p.val("t")
            pf_win(ot + 2)
        m_xin_spill = p.val("qx")
        ctx.m_xbf_users_t = max(ctx.m_xbf_users_t, p.val("t"))

        m_wx = p.emit("gpsimd", lambda e: e.dma_start(
            out=wxsl[:], in_=d_wx[0]), inc="qw", inc_by=16)
        m_stg_reads = 0
        stgbufs = (brep, crep)
        for tc in range(NTC):
            lds = []
            for half in range(2):
                w = [("qx", m_xin_spill)]
                if m_stg_reads:
                    w.append(("t", m_stg_reads))
                lds.append(p.emit("sync", lambda e, tc=tc, half=half:
                                  e.dma_start(
                    out=stgbufs[half][:].rearrange("p (k t) -> p k t", t=S5),
                    in_=xin_dram.rearrange("k p t -> p k t")[
                        :, half * 8:(half + 1) * 8, tc * S5:(tc + 1) * S5]),
                    waits=tuple(w), inc="qx", inc_by=16))
            bi, bank, war = r_a.take()
            for kt in range(16):
                w = list(war) if kt == 0 else []
                if kt == 0:
                    w += [("qx", lds[0]), ("qw", m_wx)]
                if kt == 8:
                    w.append(("qx", lds[1]))
                p.emit("tensor", lambda e, tc=tc, kt=kt, bank=bank: e.matmul(
                    bank[:], wxsl[:, kt * 128:(kt + 1) * 128],
                    stgbufs[kt // 8][:, (kt % 8) * S5:(kt % 8 + 1) * S5],
                    start=(kt == 0), stop=(kt == 15)),
                    waits=tuple(w), inc="t" if kt == 15 else None)
            mt = p.val("t")
            m_stg_reads = mt
            mv1 = p.emit("vector", lambda e, tc=tc, bank=bank: e.tensor_scalar(
                draw[:, tc * S5:(tc + 1) * S5], bank[0:64, :],
                bxc[0:64, 0:1], 0.0, ALU.add, ALU.add),
                waits=(("t", mt),), inc="v")
            mv2 = p.emit("vector", lambda e, tc=tc, bank=bank: e.tensor_scalar(
                bcsb[64:96, tc * S5:(tc + 1) * S5], bank[64:96, :],
                bxc[64:96, 0:1], 0.0, ALU.add, ALU.add), inc="v")
            r_a.release(bi, "v", mv2)
        m_draw = p.val("v")
        m_bc_dma = p.emit("sync", lambda e: e.dma_start(
            out=bc_dram, in_=bcsb[64:96, :]), waits=(("v", m_draw),),
            inc="qx", inc_by=16)
        m_dbc_t = p.val("t")

        wdt_m = {}
        wdt_free = [0, 0]

        def pf_wdt(dt_i):
            if dt_i >= NDT:
                return
            sl = dt_i % 2
            wdt_m[dt_i] = p.emit("gpsimd", lambda e, dt_i=dt_i, sl=sl:
                                 e.dma_start(
                out=wdtsl[:, sl * 128:(sl + 1) * 128], in_=d_wdt[dt_i]),
                waits=(("t", wdt_free[sl]),), inc="qw", inc_by=16)

        pf_wdt(0)
        pf_wdt(1)

        rep_war = {0: ("t", m_dbc_t), 1: ("t", m_dbc_t)}
        repB_m = {}
        repC_m = {}
        rowb_free = [0]
        ps_rep_free = [0]

        def rep_build(dt_i, n):
            sl = n % 2
            w = [("qx", m_bc_dma)]
            if rowb_free[0]:
                w.append(("t", rowb_free[0]))
            mr1 = p.emit("sync", lambda e, n=n: e.dma_start(
                out=rowb[0:1, 0:T], in_=bc_dram[n:n + 1, :]),
                waits=tuple(w), inc="qx", inc_by=16)
            for c in range(4):
                w = []
                if c == 0:
                    w.append(("qx", mr1))
                    if ps_rep_free[0]:
                        w.append(("s", ps_rep_free[0]))
                p.emit("tensor", lambda e, c=c: e.matmul(
                    ps_rep[:, c * S5:(c + 1) * S5], ctx.maskA[0:1, 0:128],
                    rowb[0:1, c * S5:(c + 1) * S5], start=True, stop=True),
                    waits=tuple(w), inc="t" if c == 3 else None)
            mtB = p.val("t")
            mr2 = p.emit("sync", lambda e, n=n: e.dma_start(
                out=rowb[0:1, 0:T], in_=bc_dram[16 + n:17 + n, :]),
                waits=(("t", mtB),), inc="qx", inc_by=16)
            wB = [("t", mtB)]
            if rep_war[sl]:
                wB.append(rep_war[sl])
            msB = p.emit("scalar", lambda e, sl=sl: e.copy(
                brep[:, sl * T:(sl + 1) * T], ps_rep[:]),
                waits=tuple(wB), inc="s")
            repB_m[n] = msB
            for c in range(4):
                w = [("s", msB)] if c == 0 else []
                if c == 0:
                    w.append(("qx", mr2))
                p.emit("tensor", lambda e, c=c: e.matmul(
                    ps_rep[:, c * S5:(c + 1) * S5], ctx.maskA[0:1, 0:128],
                    rowb[0:1, c * S5:(c + 1) * S5],
                    start=True, stop=True), waits=tuple(w),
                    inc="t" if c == 3 else None)
            mtC = p.val("t")
            rowb_free[0] = mtC
            wC = [("t", mtC)]
            if rep_war[sl]:
                wC.append(rep_war[sl])
            msC = p.emit("scalar", lambda e, sl=sl: e.copy(
                crep[:, sl * T:(sl + 1) * T], ps_rep[:]),
                waits=tuple(wC), inc="s")
            repC_m[n] = msC
            ps_rep_free[0] = msC

        xt_war = 0
        zt_war = 0
        y_sp = {}
        m_scan_h_prev = 0
        for dt_i in range(NDT):
            sl = dt_i % 2
            dso = (dt_i % 2) * T
            bcol = bdtc[:, dt_i:dt_i + 1]
            for tc in range(NTC):
                bi, bank, war = r_a.take()
                w = list(war)
                if tc == 0:
                    w += [("qw", wdt_m[dt_i]), ("v", m_draw)]
                mt = p.emit("tensor", lambda e, tc=tc, sl=sl, bank=bank:
                            e.matmul(
                    bank[:], wdtsl[:, sl * 128:(sl + 1) * 128],
                    draw[:, tc * S5:(tc + 1) * S5], start=True, stop=True),
                    waits=tuple(w), inc="t")
                w = [("t", mt)]
                if tc == 0:
                    w.append(("qx", m_consts))
                    if m_scan_h_prev:
                        w.append(("v", m_scan_h_prev))
                ms1 = p.emit("scalar", lambda e, tc=tc, bank=bank, bcol=bcol:
                             e.activation(
                    a_b[:, tc * S5:(tc + 1) * S5], bank[:], AF.Abs,
                    bias=bcol), waits=tuple(w), inc="s")
                p.emit("scalar", lambda e, tc=tc: e.activation(
                    a_b[:, tc * S5:(tc + 1) * S5],
                    a_b[:, tc * S5:(tc + 1) * S5], AF.Exp, scale=-1.0),
                    inc="s")
                ms3 = p.emit("scalar", lambda e, tc=tc: e.activation(
                    a_b[:, tc * S5:(tc + 1) * S5],
                    a_b[:, tc * S5:(tc + 1) * S5], AF.Ln, bias=1.0), inc="s")
                w = [("t", mt)]
                if tc == 0:
                    w.append(("s", p.val("s")))
                mv4 = p.emit("vector", lambda e, tc=tc, dso=dso, bank=bank,
                             bcol=bcol: e.tensor_scalar(
                    delta[:, dso + tc * S5:dso + (tc + 1) * S5], bank[:],
                    bcol, 0.0, ALU.add, ALU.max), waits=tuple(w), inc="v")
                mv5 = p.emit("vector", lambda e, tc=tc, dso=dso: e.tensor_add(
                    delta[:, dso + tc * S5:dso + (tc + 1) * S5],
                    delta[:, dso + tc * S5:dso + (tc + 1) * S5],
                    a_b[:, tc * S5:(tc + 1) * S5]),
                    waits=(("s", ms3),), inc="v")
                r_a.release(bi, "v", mv5)
            m_delta = p.val("v")
            wdt_free[sl] = p.val("t")
            pf_wdt(dt_i + 2)
            w = [("qx", m_xin_spill)]
            if xt_war:
                w.append(("v", xt_war))
            m_xt = p.emit("sync", lambda e, dt_i=dt_i: e.dma_start(
                out=xt[:], in_=xin_dram[dt_i]), waits=tuple(w),
                inc="qx", inc_by=16)
            mv_dx = p.emit("vector", lambda e, dso=dso: e.tensor_mul(
                dx[:], delta[:, dso:dso + T], xt[:]),
                waits=(("qx", m_xt),), inc="v")
            rep_build(dt_i, 0)
            rep_build(dt_i, 1)
            for n in range(16):
                nsl = n % 2
                w = [("v", mv_dx), ("v", m_acol), ("v", p.val("v"))]
                ms_a = p.emit("scalar", lambda e, dt_i=dt_i, n=n, dso=dso:
                              e.activation(
                    a_b[:], delta[:, dso:dso + T], AF.Exp,
                    scale=acol[:, dt_i * 16 + n:dt_i * 16 + n + 1]),
                    waits=tuple(w), inc="s")
                mv_b = p.emit("vector", lambda e, nsl=nsl: e.tensor_mul(
                    b_b[:], dx[:], brep[:, nsl * T:(nsl + 1) * T]),
                    waits=(("s", repB_m[n]),), inc="v")
                mv_h = p.emit("vector", lambda e: e.tensor_tensor_scan(
                    h_b[:], a_b[:], b_b[:], 0.0, ALU.mult, ALU.add),
                    waits=(("s", ms_a),), inc="v")
                p.emit("vector", lambda e, nsl=nsl: e.tensor_mul(
                    b_b[:], h_b[:], crep[:, nsl * T:(nsl + 1) * T]),
                    waits=(("s", repC_m[n]),), inc="v")
                if n == 0:
                    w = [("qx", y_sp[dt_i - 1])] if dt_i > 0 else []
                    mv_y = p.emit("vector", lambda e: e.tensor_copy(
                        yac[:], b_b[:]), waits=tuple(w), inc="v")
                else:
                    mv_y = p.emit("vector", lambda e: e.tensor_add(
                        yac[:], yac[:], b_b[:]), inc="v")
                rep_war[nsl] = ("v", mv_y)
                if n == 15:
                    m_scan_h_prev = mv_h
                if n + 2 < 16:
                    rep_build(dt_i, n + 2)
            w = [("qx", m_xin_spill)]
            if zt_war:
                w.append(("v", zt_war))
            m_zt = p.emit("sync", lambda e, dt_i=dt_i: e.dma_start(
                out=zt[:], in_=z_dram[dt_i]), waits=tuple(w),
                inc="qx", inc_by=16)
            mv_y2 = p.emit("vector", lambda e, dt_i=dt_i:
                           e.scalar_tensor_tensor(
                yac[:], xt[:], dcol[:, dt_i:dt_i + 1], yac[:],
                ALU.mult, ALU.add), inc="v")
            xt_war = mv_y2
            ms_z = p.emit("scalar", lambda e: e.activation(
                zt[:], zt[:], AF.Silu), waits=(("qx", m_zt),), inc="s")
            mv_y3 = p.emit("vector", lambda e: e.tensor_mul(
                yac[:], yac[:], zt[:]), waits=(("s", ms_z),), inc="v")
            zt_war = mv_y3
            y_sp[dt_i] = p.emit("sync", lambda e, dt_i=dt_i: e.dma_start(
                out=y_dram[dt_i], in_=yac[:]), waits=(("v", mv_y3),),
                inc="qx", inc_by=16)
        m_y_done = p.val("qx")
        m_scan_v = p.val("v")

    wout_m = {}
    wout_free = [0, 0]
    seq = [(tc, ot) for tc in range(NTC) for ot in range(CT)]

    def pf_wout(i):
        if i >= len(seq):
            return
        _, ot = seq[i]
        sl = i % 2
        wout_m[i] = p.emit("gpsimd", lambda e, ot=ot, sl=sl: e.dma_start(
            out=woutsl[:, sl * 2048:(sl + 1) * 2048], in_=d_wout[ot]),
            waits=(("t", wout_free[sl]),), inc="qw", inc_by=16)

    pf_wout(0)
    pf_wout(1)
    wo_i = [0]
    m_ystage = [0, 0]
    m_stg_war = [0]

    def wo_emitter(tc, ot, bank, waits):
        i = wo_i[0]
        assert seq[i] == (tc, ot)
        sl = i % 2
        if ot == 0:
            for half in range(2):
                w = [("qx", m_y_done)]
                w.append(("v", m_scan_v) if tc == 0 else
                         ("t", m_stg_war[0]))
                m_ystage[half] = p.emit("sync", lambda e, tc=tc, half=half:
                                        e.dma_start(
                    out=stgbufs[half][:].rearrange("p (k t) -> p k t", t=S5),
                    in_=y_dram.rearrange("k p t -> p k t")[
                        :, half * 8:(half + 1) * 8, tc * S5:(tc + 1) * S5]),
                    waits=tuple(w), inc="qx", inc_by=16)
        for kt in range(16):
            w = list(waits) if kt == 0 else []
            if kt == 0:
                w += [("qw", wout_m[i]), ("qx", m_ystage[0])]
            if kt == 8:
                w.append(("qx", m_ystage[1]))
            p.emit("tensor", lambda e, kt=kt, sl=sl, bank=bank: e.matmul(
                bank[:], woutsl[:, sl * 2048 + kt * 128:
                                sl * 2048 + (kt + 1) * 128],
                stgbufs[kt // 8][:, (kt % 8) * S5:(kt % 8 + 1) * S5],
                start=(kt == 0), stop=False), waits=tuple(w))
        mt = p.emit("tensor", lambda e, ot=ot, bank=bank: e.matmul(
            bank[:], boutr[0:1, ot * 128:(ot + 1) * 128],
            ctx.onesF[0:1, 0:S5], start=False, stop=True), inc="t")
        wout_free[sl] = p.val("t")
        if ot == CT - 1:
            m_stg_war[0] = p.val("t")
        pf_wout(i + 2)
        wo_i[0] += 1
        return mt

    emit_out_ln(p, ctx, wo_emitter, "l3a")

    if debug:
        dbg = p.dram("l3_dbg", [16, 128, T], BF16, kind="ExternalOutput")
        srcs = {"xin": xin_dram, "z": z_dram, "y": y_dram}
        for k in range(16):
            p.emit("sync", lambda e, k=k, s=srcs[debug]: e.dma_start(
                out=dbg[k], in_=s[k]), waits=(("qx", p.val("qx")),),
                inc="qx", inc_by=16)


def prep_l3_ffn(inputs):
    d = {}
    for nm, key in (("l3f_wrT", "L3_ffn_W_r"), ("l3f_wkT", "L3_ffn_W_k"),
                    ("l3f_wvT", "L3_ffn_W_v")):
        d[nm] = lhsT_blocked(np.ascontiguousarray(
            np.asarray(inputs[key], np.float32).T))
    d["l3f_mu"] = np.ascontiguousarray(np.stack(
        [np.asarray(inputs["L3_ffn_mu_r"], np.float32).reshape(8, 128).T,
         np.asarray(inputs["L3_ffn_mu_k"], np.float32).reshape(8, 128).T],
        axis=2).reshape(128, 16))
    return d


def emit_l3_ffn(p, ctx):
    S5 = 512
    d_wr = p.dram("l3f_wrT", [8, 128, 1024], F32, kind="ExternalInput")
    d_wk = p.dram("l3f_wkT", [8, 128, 1024], F32, kind="ExternalInput")
    d_wv = p.dram("l3f_wvT", [8, 128, 1024], F32, kind="ExternalInput")
    d_mu = p.dram("l3f_mu", [128, 16], F32, kind="ExternalInput")

    xsh = p.sb("l3f_xsh", [128, T], BF16)
    tmp = p.sb("l3f_tmp", [128, T], BF16)
    xrb = p.sb("l3f_xr", [128, CT * T], BF16)
    xkb = p.sb("l3f_xk", [128, CT * T], BF16)
    rb = p.sb("l3f_rb", [128, CT * S5], BF16)
    kab = p.sb("l3f_ka", [128, CT * S5], BF16)
    obuf = p.sb("l3f_ob", [128, CT * S5], BF16)
    mus = p.sb("l3f_mus", [128, 16], F32)
    wsl = p.sb("l3f_wsl", [128, 2 * 1024], BF16)

    m_mu = p.emit("sync", lambda e: e.dma_start(out=mus[:], in_=d_mu),
                  inc="qx", inc_by=16)
    xw = list(ctx.xbf_wait)

    for ct in range(CT):
        w = list(xw) if ct == 0 else []
        if ct == 0:
            w.append(("qx", m_mu))
        p.emit("vector", lambda e, ct=ct: e.memset(xsh[:, 0:1], 0.0),
               waits=tuple(w), inc="v")
        p.emit("vector", lambda e, ct=ct: e.tensor_copy(
            xsh[:, 1:T], ctx.xbf[:, ct * T:ct * T + T - 1]), inc="v")
        p.emit("vector", lambda e, ct=ct: e.tensor_sub(
            tmp[:], ctx.xbf[:, ct * T:(ct + 1) * T], xsh[:]), inc="v")
        p.emit("vector", lambda e, ct=ct: e.scalar_tensor_tensor(
            xrb[:, ct * T:(ct + 1) * T], tmp[:], mus[:, 2 * ct:2 * ct + 1],
            xsh[:], ALU.mult, ALU.add), inc="v")
        p.emit("vector", lambda e, ct=ct: e.scalar_tensor_tensor(
            xkb[:, ct * T:(ct + 1) * T], tmp[:], mus[:, 2 * ct + 1:2 * ct + 2],
            xsh[:], ALU.mult, ALU.add), inc="v")
    m_lerp = p.val("v")

    seq = []
    for tc in range(NTC):
        seq += [(tc, "r", ot) for ot in range(CT)]
        seq += [(tc, "k", ot) for ot in range(CT)]
        seq += [(tc, "v", ot) for ot in range(CT)]
    dmap = {"r": d_wr, "k": d_wk, "v": d_wv}
    w_m = {}
    w_free = [0, 0]

    def pf_w(i):
        if i >= len(seq):
            return
        _, wh, ot = seq[i]
        sl = i % 2
        w_m[i] = p.emit("gpsimd", lambda e, wh=wh, ot=ot, sl=sl: e.dma_start(
            out=wsl[:, sl * 1024:(sl + 1) * 1024], in_=dmap[wh][ot]),
            waits=(("t", w_free[sl]),), inc="qw", inc_by=16)

    pf_w(0)
    pf_w(1)
    w_i = [0]

    def mm_group(i, rhs_buf, bank, waits):
        sl = i % 2
        for kt in range(CT):
            w = list(waits) if kt == 0 else []
            if kt == 0:
                w.append(("qw", w_m[i]))
            p.emit("tensor", lambda e, kt=kt, sl=sl, bank=bank,
                   rhs_buf=rhs_buf: e.matmul(
                bank[:], wsl[:, sl * 1024 + kt * 128:
                             sl * 1024 + (kt + 1) * 128],
                rhs_buf[kt], start=(kt == 0), stop=(kt == CT - 1)),
                waits=tuple(w), inc="t" if kt == CT - 1 else None)
        mt = p.val("t")
        w_free[sl] = mt
        pf_w(i + 2)
        w_i[0] += 1
        return mt

    ol = OutLN(p, ctx, "l3f")
    with ExitStack() as pstack:
        PS = lambda nm, shape: pstack.enter_context(
            p.nc.psum_tensor(p._n(nm), shape, F32))
        rf = Rot([PS("l3f", [128, S5]) for _ in range(4)])
        rb_war = 0
        ka_war = 0
        ob_war = 0
        for tc in range(NTC):
            for ot in range(CT):
                bi, bank, war = rf.take()
                w = list(war)
                if ot == 0:
                    w.append(("v", m_lerp))
                    if rb_war:
                        w.append(("v", rb_war))
                rhs = [xrb[:, kt * T + tc * S5: kt * T + (tc + 1) * S5]
                       for kt in range(CT)]
                mt = mm_group(w_i[0], rhs, bank, tuple(w))
                ms = p.emit("scalar", lambda e, ot=ot, bank=bank: e.activation(
                    rb[:, ot * S5:(ot + 1) * S5], bank[:], AF.Sigmoid),
                    waits=(("t", mt),), inc="s")
                rf.release(bi, "s", ms)
            m_r = p.val("s")
            for ot in range(CT):
                bi, bank, war = rf.take()
                w = list(war)
                if ot == 0 and ka_war:
                    w.append(("t", ka_war))
                rhs = [xkb[:, kt * T + tc * S5: kt * T + (tc + 1) * S5]
                       for kt in range(CT)]
                mt = mm_group(w_i[0], rhs, bank, tuple(w))
                p.emit("scalar", lambda e, ot=ot, bank=bank: e.activation(
                    kab[:, ot * S5:(ot + 1) * S5], bank[:], AF.Relu),
                    waits=(("t", mt),), inc="s")
                ms = p.emit("scalar", lambda e, ot=ot: e.activation(
                    kab[:, ot * S5:(ot + 1) * S5],
                    kab[:, ot * S5:(ot + 1) * S5], AF.Square), inc="s")
                rf.release(bi, "s", ms)
            m_ka = p.val("s")
            for ot in range(CT):
                bi, bank, war = rf.take()
                w = list(war)
                if ot == 0:
                    w.append(("s", m_ka))
                    if ob_war:
                        w.append(("t", ob_war))
                rhs = [kab[:, kt * S5:(kt + 1) * S5] for kt in range(CT)]
                mt = mm_group(w_i[0], rhs, bank, tuple(w))
                mv = p.emit("vector", lambda e, ot=ot, bank=bank: e.tensor_mul(
                    obuf[:, ot * S5:(ot + 1) * S5], bank[:],
                    rb[:, ot * S5:(ot + 1) * S5]),
                    waits=(("t", mt), ("s", m_r)), inc="v")
                rf.release(bi, "v", mv)
            rb_war = p.val("v")
            m_ob = p.val("v")

            def o_emitter(tc_, ot, bank, waits, m_ob=m_ob):
                w = list(waits)
                w.append(("v", m_ob))
                mt = p.emit("tensor", lambda e, ot=ot, bank=bank: e.matmul(
                    bank[:], ctx.ident[:],
                    obuf[:, ot * S5:(ot + 1) * S5], start=True, stop=True),
                    waits=tuple(w), inc="t")
                return mt

            ol.chunk(tc, o_emitter)
            ka_war = p.val("t")
            ob_war = p.val("t")
    ol.end()


MODS = [
    (prep_l0_attn, emit_l0_attn),
    (prep_l0_ffn, emit_l0_ffn),
    (prep_l1_attn, emit_l1_attn),
    (prep_l1_ffn, emit_l1_ffn),
    (prep_l2_attn, emit_l2_attn),
    (prep_l2_ffn, emit_l2_ffn),
    (prep_l3_attn, emit_l3_attn),
    (prep_l3_ffn, emit_l3_ffn),
]


def kernel(**inputs):
    feed = dict(const_arrays())
    p = Prog(qx_lanes=2)
    x_in = p.dram("x_in", [C, T], F32, kind="ExternalInput")
    x_out = p.dram("x_out", [C, T], F32, kind="ExternalOutput")
    ctx = Ctx(p)
    emit_load_x(p, ctx, x_in)
    for prep, emit in MODS:
        feed.update(prep(inputs))
        with p.module_scope():
            emit(p, ctx)
    m_out = emit_store_out(p, ctx, x_out)
    p.wait("sync", (("qx", m_out),))
    nc = p.finalize()
    xs = [np.ascontiguousarray(np.asarray(inputs["L0_attn_X"][i],
                                          np.float32).T)
          for i in range(NCORES)]
    feeds = [dict(feed, x_in=xs[i]) for i in range(NCORES)]
    r = run_bass_kernel_spmd(nc, feeds, core_ids=list(range(NCORES)))
    return np.stack([np.asarray(r.results[i]["x_out"]).T
                     for i in range(NCORES)]).astype(np.float32)
